# version 07
# baseline (speedup 1.0000x reference)
import os

import numpy as np
import ml_dtypes

import concourse.bass as bass
import concourse.mybir as mybir
from concourse.bass_utils import run_bass_kernel_spmd

F32 = mybir.dt.float32
BF16 = mybir.dt.bfloat16
Alu = mybir.AluOpType
Act = mybir.ActivationFunctionType
Ax = mybir.AxisListType

NCORES = 8
MASK_NEG = -30000.0
KNOCK_NEG = -1e30


class CFG:
    SEQS = 2
    S = 2048
    D = 1024
    H = 16
    NE = 16
    INTER = 512
    LN_EPS = 1e-5
    SHIFT_CLAMP = -60.0

    HD = property(lambda s: s.D // s.H)
    NPAIR = property(lambda s: s.H // 2)
    KC = property(lambda s: s.D // 128)
    T128 = property(lambda s: s.S // 128)
    T512 = property(lambda s: s.S // 512)
    SHARED = property(lambda s: 2 * s.INTER)
    GW = property(lambda s: 2 * s.INTER)
    ICG = property(lambda s: s.GW // 128)
    NGRP = property(lambda s: s.NE // 2 + 1)
    MODC = property(lambda s: 6 * s.D // 128)



ENGINES = ("tensor", "vector", "scalar", "sync", "gpsimd")
DMA_ENGINES = frozenset(("sync", "gpsimd"))


class Tok:
    __slots__ = ("eng", "idx", "gid")

    def __init__(self, eng, idx, gid):
        self.eng = eng
        self.idx = idx
        self.gid = gid


class Mem:

    __slots__ = ("name", "writers", "readers", "coarse")

    def __init__(self, name, coarse=False):
        self.name = name
        self.writers = []
        self.readers = []
        self.coarse = coarse

    def on_read(self, lo, hi, tok, deps):
        if self.coarse:
            lo, hi = 0, 1 << 30
        for (l, h, t) in self.writers:
            if l < hi and lo < h:
                deps.append(t)
        rd = self.readers
        keep = [e for e in rd if not (e[2].eng == tok.eng and lo <= e[0] and e[1] <= hi)]
        keep.append([lo, hi, tok])
        self.readers = keep

    def on_write(self, lo, hi, tok, deps):
        if self.coarse:
            lo, hi = 0, 1 << 30
        for (l, h, t) in self.writers:
            if l < hi and lo < h:
                deps.append(t)
        for (l, h, t) in self.readers:
            if l < hi and lo < h:
                deps.append(t)
        self.writers = [e for e in self.writers if not (lo <= e[0] and e[1] <= hi)]
        self.readers = [e for e in self.readers if not (lo <= e[0] and e[1] <= hi)]
        self.writers.append([lo, hi, tok])


class Prog:

    def __init__(self, paranoid=False):
        if paranoid is True:
            paranoid = set(ENGINES)
        elif not paranoid:
            paranoid = set()
        self.paranoid = paranoid
        self.streams = {e: [] for e in ENGINES}
        self.keys = []
        self.gid = 0

    def do(self, eng, emit, r=(), w=(), key=None, drain=False):
        if eng in DMA_ENGINES and emit is not None:
            assert key is not None, "DMA op requires a semaphore key"
            if key not in self.keys:
                self.keys.append(key)
        self.gid += 1
        tok = Tok(eng, len(self.streams[eng]), self.gid)
        deps = []
        for (m, lo, hi) in r:
            m.on_read(lo, hi, tok, deps)
        for (m, lo, hi) in w:
            m.on_write(lo, hi, tok, deps)
        deps = [d for d in deps if d is not tok]
        self.streams[eng].append([emit, deps, False, key, tok.gid, drain])
        return tok

    def finalize(self):
        import bisect
        for e in ENGINES:
            for op in self.streams[e]:
                if e in DMA_ENGINES and op[0] is not None:
                    op[2] = True
                for d in op[1]:
                    de = d.eng
                    if de in DMA_ENGINES:
                        continue
                    if de != e or de in self.paranoid:
                        self.streams[de][d.idx][2] = True
        key_gids = {k: [] for k in self.keys}
        for e in DMA_ENGINES:
            for op in self.streams[e]:
                if op[0] is not None:
                    key_gids[op[3]].append(op[4])
        for k in key_gids:
            key_gids[k].sort()
        ordinals = {e: [] for e in ENGINES}
        cnt_e = {e: 0 for e in ENGINES}
        for e in ENGINES:
            for op in self.streams[e]:
                if e not in DMA_ENGINES and op[2]:
                    cnt_e[e] += 1
                ordinals[e].append(cnt_e[e])
        plans = {}
        for e in ENGINES:
            plan = []
            last_wait = {}
            for emit, deps, marked, key, gid, drain in self.streams[e]:
                need = {}
                for d in deps:
                    de = d.eng
                    if de in DMA_ENGINES:
                        dk = self.streams[de][d.idx][3]
                        v = bisect.bisect_left(key_gids[dk], gid)
                        if v > 0 and v > need.get(dk, 0):
                            need[dk] = v
                    else:
                        if de == e and de not in self.paranoid:
                            continue
                        v = ordinals[de][d.idx]
                        if v > 0 and v > need.get(de, 0):
                            need[de] = v
                waits = []
                for sn, v in need.items():
                    if last_wait.get(sn, 0) < v:
                        waits.append((sn, v * (16 if sn in key_gids else 1)))
                        last_wait[sn] = v
                inc_sem = key if e in DMA_ENGINES else e
                inc = (16 if e in DMA_ENGINES else 1) if marked else 0
                plan.append((waits, emit, inc, inc_sem, drain))
            plans[e] = plan
        return plans, [e for e in ENGINES if e not in DMA_ENGINES] + self.keys



def build_program(cfg, paranoid=False, stop_after=None, dbg=False):
    nc = bass.Bass("TRN2", target_bir_lowering=False,
                   detect_race_conditions=bool(int(os.environ.get("KBASS_RACED", "0"))),
                   dynamic_dma_scratch_size=16384)
    SEQS, S, D, NE, INTER = cfg.SEQS, cfg.S, cfg.D, cfg.NE, cfg.INTER
    KC, NPAIR, T128, T512 = cfg.KC, cfg.NPAIR, cfg.T128, cfg.T512
    ICG, NGRP, MODC = cfg.ICG, cfg.NGRP, cfg.MODC
    NK = 2 * KC
    OW = min(512, D)
    NOT = D // OW

    dx = nc.dram_tensor("x", [SEQS, S, D], F32, kind="ExternalInput").ap()
    dwin = nc.dram_tensor("w_in_re", [D, NPAIR, 384], F32, kind="ExternalInput").ap()
    dbin = nc.dram_tensor("b_in", [3 * D], F32, kind="ExternalInput").ap()
    dwout = nc.dram_tensor("w_out", [D, D], F32, kind="ExternalInput").ap()
    dbout = nc.dram_tensor("b_out", [D], F32, kind="ExternalInput").ap()
    dgate = nc.dram_tensor("gate_w", [NE, D], F32, kind="ExternalInput").ap()
    dw1 = nc.dram_tensor("w1", [NE, D, INTER], F32, kind="ExternalInput").ap()
    dw2 = nc.dram_tensor("w2", [NE, INTER, D], F32, kind="ExternalInput").ap()
    dw3 = nc.dram_tensor("w3", [NE, D, INTER], F32, kind="ExternalInput").ap()
    dws1 = nc.dram_tensor("ws1", [D, cfg.SHARED], F32, kind="ExternalInput").ap()
    dws2 = nc.dram_tensor("ws2", [cfg.SHARED, D], F32, kind="ExternalInput").ap()
    dws3 = nc.dram_tensor("ws3", [D, cfg.SHARED], F32, kind="ExternalInput").ap()
    dc = nc.dram_tensor("cvec", [SEQS, D], F32, kind="ExternalInput").ap()
    dwmod = nc.dram_tensor("wmod", [D, 6 * D], F32, kind="ExternalInput").ap()
    did = nc.dram_tensor("id128d", [128, 128], BF16, kind="ExternalInput").ap()
    dmask = nc.dram_tensor("mask_td", [128, 128], F32, kind="ExternalInput").ap()
    dsel = nc.dram_tensor("sel16d", [NE, NE * 128], BF16, kind="ExternalInput").ap()
    dout = nc.dram_tensor("out", [SEQS, S, D], F32, kind="ExternalOutput").ap()
    dxb = nc.dram_tensor("xb_scr", [SEQS, S, D], F32,
                         kind="ExternalOutput" if dbg else "Internal").ap()
    if dbg:
        dbg_mods = nc.dram_tensor("dbg_mods", [128, 4 * NK], F32, kind="ExternalOutput").ap()
        dbg_h = nc.dram_tensor("dbg_h", [128, KC * S], F32, kind="ExternalOutput").ap()
        dbg_attn = nc.dram_tensor("dbg_attn", [128, KC * S], F32, kind="ExternalOutput").ap()
        dbg_cfm = nc.dram_tensor("dbg_cfm", [NE, S], F32, kind="ExternalOutput").ap()
        dbg_y = nc.dram_tensor("dbg_y", [128, KC * S], F32, kind="ExternalOutput").ap()
        dbg_f32 = nc.dram_tensor("dbg_f32", [128, KC * S], F32, kind="Internal").ap()
        dbg_qk = nc.dram_tensor("dbg_qk", [128, 2 * S], F32, kind="ExternalOutput").ap()
        dbg_v = nc.dram_tensor("dbg_v", [128, 4224], F32, kind="ExternalOutput").ap()
        dbg_at = nc.dram_tensor("dbg_at", [128, 2048], F32, kind="ExternalOutput").ap()
        dbg_nc = nc.dram_tensor("dbg_nc", [128, 4], F32, kind="ExternalOutput").ap()
        dbg_bir = nc.dram_tensor("dbg_bir", [1, 2 * D + 512 + D], F32, kind="ExternalOutput").ap()
        m_dbg = Mem("dbg")

        def dump_bf(dst, src_ap, mem, width):
            for col in range(0, width, 4096):
                wdt = min(4096, width - col)
                dma("gpsimd", "dbgk", dst[:, col:col + wdt], src_ap[:, col:col + wdt],
                    [(mem, col, col + wdt)], [(m_dbg, 0, 1)])

    p = Prog(paranoid=paranoid)
    ctx = []

    def sb(name, shape, dtype):
        g = nc.sbuf_tensor(name, shape, dtype)
        t = g.__enter__()
        ctx.append(g)
        return t, Mem(name)

    AW = max(3 * KC * cfg.GW, 2 * KC * 384, 2 * 6 * D, KC * D)
    arena_a, m_a = sb("arena_a", [128, AW], BF16)
    GWD = cfg.GW
    assert GWD == D
    WAB = KC * GWD
    A_WA, A_WB, A_WC = 0, WAB, 2 * WAB
    arena_b, m_b = sb("arena_b", [128, KC * S], BF16)
    arena_c, m_c = sb("arena_c", [128, KC * S], BF16)
    D_PT, D_SQ = 0, 1024
    D_VA, D_VB = 2048, 2048 + T128 * 65
    D_AT = 2048 + 2 * T128 * 65
    DW = max(2 * ICG * 512, D_AT + T128 * 128, KC * 1024)
    arena_d, m_d = sb("arena_d", [128, DW], BF16)
    arena_e, m_e = sb("arena_e", [128, 2 * S], BF16)
    arena_g, m_g = sb("arena_g", [128, 4 * D], F32)
    hn_ring, m_hn = sb("hn_ring", [128, 2 * D], BF16)
    moe_tmp, m_mt = sb("moe_tmp", [128, 2048], BF16)
    cb_ring, m_cb = sb("cb_ring", [128, 2048], BF16)
    c_fm, m_cfm = sb("c_fm", [NE, S], BF16)
    bcast, m_bc = sb("bcast", [128, 2 * D], BF16)
    rows, m_rows = sb("rows", [1, 6 * D], BF16)
    R_GA, R_GAB, R_GB, R_SAP, R_SBP = 0, 2 * D, 3 * D, 4 * D, 5 * D
    bvrow, m_bv = sb("bvrow", [1, D], BF16)
    borow, m_bo = sb("borow", [1, D], BF16)
    mods_fm, m_mods = sb("mods_fm", [128, 4 * NK], F32)
    ga_fm, m_gafm = sb("ga_fm", [128, 2 * NK], BF16)
    id_bf, m_id = sb("id_bf", [128, 128], BF16)
    mask_t, m_mask = sb("mask_t", [128, 128], F32)
    sel16, m_sel = sb("sel16", [NE, NE * 128], BF16)
    gw_bf, m_gw = sb("gw_bf", [NE, D], BF16)
    gateT, m_gT = sb("gateT", [128, KC * NE], BF16)
    c2row, m_c2 = sb("c2row", [2, D], F32)
    c2bf, m_c2b = sb("c2bf", [2, D], BF16)
    csilu, m_cs = sb("csilu", [128, NK], BF16)
    ones_r, m_onr = sb("ones_r", [1, 512], BF16)
    ones_c, m_onc = sb("ones_c", [128, 1], BF16)
    rout, m_rt = sb("rout", [128, 128], F32)
    sc_bf, m_scb = sb("sc_bf", [128, NE], BF16)
    smalls, m_sm = sb("smalls", [128, 16], F32)
    brow, m_br = sb("brow", [1, 16], F32)
    brow_bf, m_brb = sb("brow_bf", [1, 4], BF16)
    b_in_row, m_bir = sb("b_in_row", [1, 2 * D], BF16)
    sap_bf, m_sap = sb("sap_bf", [128, 2 * NK], BF16)

    PS = []
    m_ps = []
    for i in range(8):
        g = nc.psum_tensor(f"ps{i}", [128, 512], F32)
        PS.append(g.__enter__())
        ctx.append(g)
        m_ps.append(Mem(f"ps{i}", coarse=True))
    m_xbs = Mem("xb_scr")
    m_out = Mem("out")

    WHOLE = (0, 1 << 30)

    def mm(out_ap, lhsT_ap, rhs_ap, r, w, start=True, stop=True, skip=False):
        lsh, rsh, osh = lhsT_ap.shape, rhs_ap.shape, out_ap.shape
        assert lsh[0] == rsh[0], f"mm K mismatch {lsh} vs {rsh}"
        assert lsh[1] == osh[0], f"mm M mismatch lhsT {lsh} out {osh}"
        assert rsh[1] == osh[1], f"mm N mismatch rhs {rsh} out {osh}"
        return p.do("tensor",
                    lambda e, o=out_ap, l=lhsT_ap, rr=rhs_ap, st=start, sp=stop, sk=skip:
                        e.matmul(o, l, rr, start=st, stop=sp, skip_group_check=sk),
                    r=r, w=w)

    def vec(fn, r, w, drain=False):
        return p.do("vector", fn, r=r, w=w, drain=drain)

    def sca(fn, r, w):
        return p.do("scalar", fn, r=r, w=w)

    def dma(q, key, out_ap, in_ap, r, w):
        return p.do(q, lambda e, o=out_ap, i=in_ap: e.dma_start(o, i), r=r, w=w, key=key)

    dma("sync", "k_id", id_bf[:, :], did, [], [(m_id, *WHOLE)])
    dma("sync", "k_mask", mask_t[:, :], dmask, [], [(m_mask, *WHOLE)])
    dma("sync", "k_sel", sel16[:, :], dsel, [], [(m_sel, *WHOLE)])
    if SEQS < 2:
        vec(lambda e: e.memset(c2row[:, :], 0.0), [], [(m_c2, *WHOLE)])
    dma("sync", "k_c2", c2row[0:SEQS, :], dc[:, :], [],
        [(m_c2, 0, D if SEQS == 2 else 1)])
    dma("gpsimd", "k_bo", borow[0:1, :], dbout.rearrange("(o d) -> o d", o=1), [], [(m_bo, *WHOLE)])
    dma("gpsimd", "k_bir", b_in_row[0:1, :], dbin.rearrange("(o d) -> o d", o=1)[0:1, 0:2 * D],
        [], [(m_bir, *WHOLE)])
    dma("gpsimd", "k_bv", bvrow[0:1, :], dbin.rearrange("(o d) -> o d", o=1)[0:1, 2 * D:3 * D],
        [], [(m_bv, *WHOLE)])
    dma("gpsimd", "k_gw", gw_bf[:, :], dgate, [], [(m_gw, *WHOLE)])

    vec(lambda e: e.memset(ones_r[:, :], 1.0), [], [(m_onr, *WHOLE)])
    vec(lambda e: e.memset(smalls[:, 6:7], cfg.LN_EPS), [], [(m_sm, 6, 7)])
    vec(lambda e: e.memset(ones_c[:, :], 1.0), [], [(m_onc, *WHOLE)])



    for n in range(NOT):
        mm(PS[6][:, 0:OW], ones_r[0:1, 0:128], bvrow[0:1, n * OW:(n + 1) * OW],
           [(m_onr, *WHOLE), (m_bv, n * OW, (n + 1) * OW)], [(m_ps[6], 0, OW)])
        vec(lambda e, n=n: e.tensor_copy(bcast[:, n * OW:(n + 1) * OW], PS[6][:, 0:OW]),
            [(m_ps[6], 0, OW)], [(m_bc, n * OW, (n + 1) * OW)])

    sca(lambda e: e.activation(c2row[0:2, :], c2row[0:2, :], Act.Silu),
        [(m_c2, *WHOLE)], [(m_c2, *WHOLE)])
    vec(lambda e: e.tensor_copy(c2bf[0:2, :], c2row[0:2, :]),
        [(m_c2, *WHOLE)], [(m_c2b, *WHOLE)])
    for ch in range(KC):
        mm(PS[6][0:128, 0:2], c2bf[0:2, ch * 128:(ch + 1) * 128], id_bf[0:2, 0:2],
           [(m_c2b, ch * 128, (ch + 1) * 128), (m_id, 0, 2)], [(m_ps[6], 0, 2)])
        vec(lambda e, ch=ch: e.tensor_copy(csilu[:, ch * 2:ch * 2 + 2], PS[6][0:128, 0:2]),
            [(m_ps[6], 0, 2)], [(m_cs, ch * 2, ch * 2 + 2)])

    for ch in range(KC):
        mm(PS[6][0:128, 0:NE], gw_bf[0:NE, ch * 128:(ch + 1) * 128], id_bf[0:NE, 0:NE],
           [(m_gw, ch * 128, (ch + 1) * 128), (m_id, 0, NE)], [(m_ps[6], 0, NE)])
        vec(lambda e, ch=ch: e.tensor_copy(gateT[:, ch * NE:(ch + 1) * NE], PS[6][0:128, 0:NE]),
            [(m_ps[6], 0, NE)], [(m_gT, ch * NE, (ch + 1) * NE)])

    WM = 6 * D
    for kc in range(KC):
        sl = kc % 2
        dma("gpsimd", f"wm{sl}", arena_a[:, sl * WM:(sl + 1) * WM], dwmod[kc * 128:(kc + 1) * 128, :],
            [], [(m_a, sl * WM, (sl + 1) * WM)])
        for mc in range(MODC):
            mm(PS[7][:, mc * 2:mc * 2 + 2],
               arena_a[:, sl * WM + mc * 128: sl * WM + (mc + 1) * 128],
               csilu[:, kc * 2:kc * 2 + 2],
               [(m_a, sl * WM + mc * 128, sl * WM + (mc + 1) * 128), (m_cs, kc * 2, kc * 2 + 2)],
               [(m_ps[7], mc * 2, mc * 2 + 2)],
               start=(kc == 0 and mc == 0), stop=(kc == KC - 1), skip=True)
    CA0, SA0, GA0, CB0, SB0, GB0 = (NK, 0, 2 * NK, 4 * NK, 3 * NK, 5 * NK)
    vec(lambda e: e.tensor_scalar_add(mods_fm[:, 0:NK], PS[7][:, CA0:CA0 + NK], 1.0),
        [(m_ps[7], CA0, CA0 + NK)], [(m_mods, 0, NK)])
    vec(lambda e: e.tensor_copy(mods_fm[:, NK:2 * NK], PS[7][:, SA0:SA0 + NK]),
        [(m_ps[7], SA0, SA0 + NK)], [(m_mods, NK, 2 * NK)])
    vec(lambda e: e.tensor_scalar_add(mods_fm[:, 2 * NK:3 * NK], PS[7][:, CB0:CB0 + NK], 1.0),
        [(m_ps[7], CB0, CB0 + NK)], [(m_mods, 2 * NK, 3 * NK)])
    vec(lambda e: e.tensor_copy(mods_fm[:, 3 * NK:4 * NK], PS[7][:, SB0:SB0 + NK]),
        [(m_ps[7], SB0, SB0 + NK)], [(m_mods, 3 * NK, 4 * NK)])
    vec(lambda e: e.reciprocal(rout[:, NK:2 * NK], mods_fm[:, 0:NK]),
        [(m_mods, 0, NK)], [(m_rt, NK, 2 * NK)])
    vec(lambda e: e.tensor_tensor(rout[:, 0:NK], mods_fm[:, NK:2 * NK],
                                  rout[:, NK:2 * NK], Alu.mult),
        [(m_mods, NK, 2 * NK), (m_rt, NK, 2 * NK)], [(m_rt, 0, NK)])
    vec(lambda e: e.tensor_copy(sap_bf[:, 0:NK], rout[:, 0:NK]),
        [(m_rt, 0, NK)], [(m_sap, 0, NK)])
    vec(lambda e: e.reciprocal(rout[:, NK:2 * NK], mods_fm[:, 2 * NK:3 * NK]),
        [(m_mods, 2 * NK, 3 * NK)], [(m_rt, NK, 2 * NK)])
    vec(lambda e: e.tensor_tensor(rout[:, 0:NK], mods_fm[:, 3 * NK:4 * NK],
                                  rout[:, NK:2 * NK], Alu.mult),
        [(m_mods, 3 * NK, 4 * NK), (m_rt, NK, 2 * NK)], [(m_rt, 0, NK)])
    vec(lambda e: e.tensor_copy(sap_bf[:, NK:2 * NK], rout[:, 0:NK]),
        [(m_rt, 0, NK)], [(m_sap, NK, 2 * NK)])
    for i, off in enumerate((GA0, GB0)):
        vec(lambda e, off=off: e.tensor_scalar(rout[:, 0:NK], PS[7][:, off:off + NK],
                                               0.0, None, Alu.is_ge),
            [(m_ps[7], off, off + NK)], [(m_rt, 0, NK)])
        vec(lambda e: e.tensor_scalar(rout[:, 0:NK], rout[:, 0:NK], 2.0, -1.0,
                                      Alu.mult, Alu.add),
            [(m_rt, 0, NK)], [(m_rt, 0, NK)])
        vec(lambda e, off=off: e.tensor_scalar_mul(rout[:, NK:2 * NK], PS[7][:, off:off + NK], -1.0),
            [(m_ps[7], off, off + NK)], [(m_rt, NK, 2 * NK)])
        vec(lambda e, off=off: e.tensor_tensor(rout[:, NK:2 * NK], PS[7][:, off:off + NK],
                                               rout[:, NK:2 * NK], Alu.max),
            [(m_ps[7], off, off + NK), (m_rt, NK, 2 * NK)], [(m_rt, NK, 2 * NK)])
        vec(lambda e: e.tensor_scalar_max(rout[:, NK:2 * NK], rout[:, NK:2 * NK], 0.25),
            [(m_rt, NK, 2 * NK)], [(m_rt, NK, 2 * NK)])
        vec(lambda e, i=i: e.tensor_tensor(ga_fm[:, i * NK:(i + 1) * NK], rout[:, 0:NK],
                                           rout[:, NK:2 * NK], Alu.mult),
            [(m_rt, 0, 2 * NK)], [(m_gafm, i * NK, (i + 1) * NK)])
    for s in range(SEQS):
        for ch in range(KC):
            mm(PS[6][0:1, 0:128], ga_fm[:, ch * 2 + s:ch * 2 + s + 1], id_bf[:, :],
               [(m_gafm, ch * 2 + s, ch * 2 + s + 1), (m_id, *WHOLE)], [(m_ps[6], 0, 128)])
            vec(lambda e, s=s, ch=ch: e.tensor_copy(
                    rows[0:1, R_GA + s * D + ch * 128:R_GA + s * D + (ch + 1) * 128],
                    PS[6][0:1, 0:128]),
                [(m_ps[6], 0, 128)], [(m_rows, R_GA + s * D + ch * 128, R_GA + s * D + (ch + 1) * 128)])

    if dbg:
        dump_bf(dbg_mods, mods_fm, m_mods, 4 * NK)

    BNW = min(512, D)
    NBN = D // BNW

    def ln_tile(x_ap, mx, xlo, hn_ap, mhn_lo):
        for i in range(NBN):
            vec(lambda e, i=i: e.bn_stats(rout[:, i * 6:(i + 1) * 6],
                                          x_ap[:, i * BNW:(i + 1) * BNW]),
                [(mx, xlo + i * BNW, xlo + (i + 1) * BNW)], [(m_rt, i * 6, (i + 1) * 6)],
                drain=True)
        mcol = 6 * NBN
        vec(lambda e: e.bn_aggr(rout[:, mcol:mcol + 2], rout[:, 0:6 * NBN]),
            [(m_rt, 0, 6 * NBN)], [(m_rt, mcol, mcol + 2)], drain=True)
        sca(lambda e: e.activation(smalls[:, 2:3], rout[:, mcol + 1:mcol + 2], Act.Sqrt,
                                   bias=smalls[:, 6:7]),
            [(m_rt, mcol + 1, mcol + 2), (m_sm, 6, 7)], [(m_sm, 2, 3)])
        vec(lambda e: e.reciprocal(smalls[:, 3:4], smalls[:, 2:3]),
            [(m_sm, 2, 3)], [(m_sm, 3, 4)])
        vec(lambda e: e.tensor_scalar_add(smalls[:, 4:5], rout[:, mcol + 1:mcol + 2], cfg.LN_EPS),
            [(m_rt, mcol + 1, mcol + 2)], [(m_sm, 4, 5)])
        vec(lambda e: e.tensor_tensor(smalls[:, 5:6], smalls[:, 3:4], smalls[:, 3:4], Alu.mult),
            [(m_sm, 3, 4)], [(m_sm, 5, 6)])
        vec(lambda e: e.tensor_tensor(smalls[:, 5:6], smalls[:, 5:6], smalls[:, 4:5], Alu.mult),
            [(m_sm, 4, 6)], [(m_sm, 5, 6)])
        vec(lambda e: e.tensor_scalar(smalls[:, 5:6], smalls[:, 5:6], -0.5, 1.5,
                                      Alu.mult, Alu.add),
            [(m_sm, 5, 6)], [(m_sm, 5, 6)])
        vec(lambda e: e.tensor_tensor(smalls[:, 3:4], smalls[:, 3:4], smalls[:, 5:6], Alu.mult),
            [(m_sm, 3, 6)], [(m_sm, 3, 4)])
        vec(lambda e: e.tensor_scalar(hn_ap, x_ap, rout[:, mcol:mcol + 1], smalls[:, 3:4],
                                      Alu.subtract, Alu.mult),
            [(mx, xlo, xlo + D), (m_rt, mcol, mcol + 1), (m_sm, 3, 4)],
            [(m_hn, mhn_lo, mhn_lo + D)])

    stopped = [False]
    do_gb = True
    for s in range(SEQS):
        if stop_after == "setup":
            stopped[0] = True
            break
        for half, rbase in ((0, R_SAP), (1, R_SBP)):
            for ch in range(KC):
                col = half * NK + ch * 2 + s
                mm(PS[6][0:1, 0:128], sap_bf[:, col:col + 1], id_bf[:, :],
                   [(m_sap, col, col + 1), (m_id, *WHOLE)], [(m_ps[6], 0, 128)])
                vec(lambda e, rbase=rbase, ch=ch: e.tensor_copy(
                        rows[0:1, rbase + ch * 128:rbase + (ch + 1) * 128], PS[6][0:1, 0:128]),
                    [(m_ps[6], 0, 128)], [(m_rows, rbase + ch * 128, rbase + (ch + 1) * 128)])
        for t in range(T128):
            sl = t % 2
            xg = arena_g[:, sl * D:(sl + 1) * D]
            dma("sync", f"x{sl}", xg, dx[s, t * 128:(t + 1) * 128, :], [],
                [(m_g, sl * D, (sl + 1) * D)])
            hn = hn_ring[:, sl * D:(sl + 1) * D]
            ln_tile(xg, m_g, sl * D, hn, sl * D)
            for ch in range(KC):
                pb = 2 + (ch % 2)
                mm(PS[pb][:, 0:128], hn[:, ch * 128:(ch + 1) * 128], id_bf[:, :],
                   [(m_hn, sl * D + ch * 128, sl * D + (ch + 1) * 128), (m_id, *WHOLE)],
                   [(m_ps[pb], 0, 128)], start=True, stop=False)
                mm(PS[pb][:, 0:128], rows[0:1, R_SAP + ch * 128:R_SAP + (ch + 1) * 128],
                   ones_r[0:1, 0:128],
                   [(m_rows, R_SAP + ch * 128, R_SAP + (ch + 1) * 128), (m_onr, *WHOLE)],
                   [(m_ps[pb], 0, 128)], start=False, stop=True)
                sca(lambda e, ch=ch, t=t, pb=pb, s=s: e.activation(
                        arena_b[:, ch * S + t * 128:ch * S + (t + 1) * 128],
                        PS[pb][:, 0:128], Act.Copy,
                        scale=mods_fm[:, ch * 2 + s:ch * 2 + s + 1]),
                    [(m_ps[pb], 0, 128), (m_mods, 0, 2 * NK)],
                    [(m_b, ch * S + t * 128, ch * S + (t + 1) * 128)])

        if dbg and s == 0:
            dump_bf(dbg_h, arena_b, m_b, KC * S)
        if stop_after == "ln1":
            stopped[0] = True
            break
        for tb in range(T128):
            vec(lambda e, tb=tb: e.memset(arena_d[:, D_VA + tb * 65 + 64:D_VA + tb * 65 + 65], 1.0),
                [], [(m_d, D_VA + tb * 65 + 64, D_VA + tb * 65 + 65)])
            vec(lambda e, tb=tb: e.memset(arena_d[:, D_VB + tb * 65 + 64:D_VB + tb * 65 + 65], 1.0),
                [], [(m_d, D_VB + tb * 65 + 64, D_VB + tb * 65 + 65)])
        WIN = KC * 384
        for pr in range(NPAIR):
            wsl = pr % 2
            win_lo = wsl * WIN
            dma("gpsimd", f"wi{wsl}",
                arena_a[:, win_lo:win_lo + WIN].rearrange("p (a c) -> p a c", a=KC),
                dwin[:, pr, :].rearrange("(a p) c -> p a c", p=128),
                [], [(m_a, win_lo, win_lo + WIN)])
            win = arena_a[:, win_lo:win_lo + WIN]

            for part in range(2):
                off = part * 128
                dst0 = part * S
                pb = part
                for nt in range(T512):
                    for kc in range(KC):
                        mm(PS[pb][:, 0:512],
                           win[:, kc * 384 + off:kc * 384 + off + 128],
                           arena_b[:, kc * S + nt * 512:kc * S + (nt + 1) * 512],
                           [(m_a, win_lo + kc * 384 + off, win_lo + kc * 384 + off + 128),
                            (m_b, kc * S + nt * 512, kc * S + (nt + 1) * 512)],
                           [(m_ps[pb], 0, 512)],
                           start=(kc == 0), stop=False)
                    bcol = part * D + pr * 128
                    mm(PS[pb][:, 0:512], b_in_row[0:1, bcol:bcol + 128], ones_r[0:1, 0:512],
                       [(m_bir, bcol, bcol + 128), (m_onr, *WHOLE)],
                       [(m_ps[pb], 0, 512)], start=False, stop=True)
                    sca(lambda e, nt=nt, pb=pb, dst0=dst0, scf=(0.125 if part == 0 else 1.0):
                            e.activation(arena_e[:, dst0 + nt * 512:dst0 + (nt + 1) * 512],
                                         PS[pb][:, 0:512], Act.Copy, scale=scf),
                        [(m_ps[pb], 0, 512)],
                        [(m_e, dst0 + nt * 512, dst0 + (nt + 1) * 512)])
            for tb in range(T128):
                pb = 2 + (tb % 2)
                for kc in range(KC):
                    mm(PS[pb][:, 0:128],
                       arena_b[:, kc * S + tb * 128:kc * S + (tb + 1) * 128],
                       win[:, kc * 384 + 256:kc * 384 + 384],
                       [(m_b, kc * S + tb * 128, kc * S + (tb + 1) * 128),
                        (m_a, win_lo + kc * 384 + 256, win_lo + kc * 384 + 384)],
                       [(m_ps[pb], 0, 128)],
                       start=(kc == 0), stop=(kc == KC - 1))
                vec(lambda e, tb=tb, pb=pb, pr=pr: e.tensor_tensor(
                        arena_d[:, D_VA + tb * 65:D_VA + tb * 65 + 64],
                        PS[pb][:, 0:64], bcast[:, pr * 128:pr * 128 + 64], Alu.add),
                    [(m_ps[pb], 0, 64), (m_bc, pr * 128, pr * 128 + 64)],
                    [(m_d, D_VA + tb * 65, D_VA + tb * 65 + 64)])
                vec(lambda e, tb=tb, pb=pb, pr=pr: e.tensor_tensor(
                        arena_d[:, D_VB + tb * 65:D_VB + tb * 65 + 64],
                        PS[pb][:, 64:128], bcast[:, pr * 128 + 64:pr * 128 + 128], Alu.add),
                    [(m_ps[pb], 64, 128), (m_bc, pr * 128 + 64, pr * 128 + 128)],
                    [(m_d, D_VB + tb * 65, D_VB + tb * 65 + 64)])

            for side in range(2):
                src0 = side * S
                for nt in range(T512):
                    qsl = nt % 2
                    sca(lambda e, nt=nt, qsl=qsl, src0=src0: e.activation(
                            arena_d[:, D_SQ + qsl * 512:D_SQ + (qsl + 1) * 512],
                            arena_e[:, src0 + nt * 512:src0 + (nt + 1) * 512], Act.Square),
                        [(m_e, src0 + nt * 512, src0 + (nt + 1) * 512)],
                        [(m_d, D_SQ + qsl * 512, D_SQ + (qsl + 1) * 512)])
                    for hh in range(2):
                        pb = 4 + hh
                        mm(PS[pb][0:1, 0:512],
                           ones_c[hh * 64:(hh + 1) * 64, 0:1],
                           arena_d[hh * 64:(hh + 1) * 64, D_SQ + qsl * 512:D_SQ + (qsl + 1) * 512],
                           [(m_onc, *WHOLE), (m_d, D_SQ + qsl * 512, D_SQ + (qsl + 1) * 512)],
                           [(m_ps[pb], 0, 512)],
                           start=(nt == 0), stop=(nt == T512 - 1), skip=True)
                for hh in range(2):
                    pb = 4 + hh
                    vec(lambda e, side=side, hh=hh, pb=pb: e.reduce_max(
                            brow[0:1, side * 2 + hh:side * 2 + hh + 1], PS[pb][0:1, 0:512], Ax.X),
                        [(m_ps[pb], 0, 512)], [(m_br, side * 2 + hh, side * 2 + hh + 1)],
                        drain=True)
            for hh in range(2):
                vec(lambda e, hh=hh: e.tensor_tensor(
                        brow[0:1, 4 + hh:5 + hh], brow[0:1, hh:hh + 1],
                        brow[0:1, 2 + hh:3 + hh], Alu.add),
                    [(m_br, 0, 4)], [(m_br, 4 + hh, 5 + hh)])
                vec(lambda e, hh=hh: e.tensor_scalar(
                        brow[0:1, 4 + hh:5 + hh], brow[0:1, 4 + hh:5 + hh],
                        -1.0 / 16.0, cfg.SHIFT_CLAMP, Alu.mult, Alu.max),
                    [(m_br, 4 + hh, 5 + hh)], [(m_br, 4 + hh, 5 + hh)])
            vec(lambda e: e.tensor_copy(brow_bf[0:1, 0:2], brow[0:1, 4:6]),
                [(m_br, 4, 6)], [(m_brb, 0, 2)])

            for hh in range(2):
                qrow = arena_e[hh * 64:(hh + 1) * 64, 0:S]
                krow = arena_e[hh * 64:(hh + 1) * 64, S:2 * S]
                voff = D_VA if hh == 0 else D_VB
                mm(PS[6][:, 0:1], ones_r[0:1, 0:128], brow_bf[0:1, hh:hh + 1],
                   [(m_onr, *WHOLE), (m_brb, hh, hh + 1)], [(m_ps[6], 0, 1)])
                vec(lambda e: e.tensor_copy(smalls[:, 0:1], PS[6][:, 0:1]),
                    [(m_ps[6], 0, 1)], [(m_sm, 0, 1)])

                for qt in range(T512):
                    nk = 4 * qt + 4

                    def emit_st(kblk):
                        sl = kblk % 2
                        off = max(0, kblk * 128 - qt * 512)
                        mm(PS[sl][:, off:512],
                           krow[:, kblk * 128:(kblk + 1) * 128],
                           qrow[:, qt * 512 + off:(qt + 1) * 512],
                           [(m_e, S + kblk * 128, S + (kblk + 1) * 128),
                            (m_e, qt * 512 + off, (qt + 1) * 512)],
                           [(m_ps[sl], 0, 512)])
                        if kblk >= 4 * qt:
                            vec(lambda e, sl=sl, off=off: e.tensor_tensor(
                                    PS[sl][:, off:off + 128], PS[sl][:, off:off + 128],
                                    mask_t[:, :], Alu.add),
                                [(m_mask, *WHOLE), (m_ps[sl], off, off + 128)],
                                [(m_ps[sl], off, off + 128)])
                        sca(lambda e, sl=sl, off=off: e.activation(
                                arena_d[:, D_PT + sl * 512 + off:D_PT + (sl + 1) * 512],
                                PS[sl][:, off:512], Act.Exp, bias=smalls[:, 0:1]),
                            [(m_ps[sl], off, 512), (m_sm, 0, 1)],
                            [(m_d, D_PT + sl * 512 + off, D_PT + (sl + 1) * 512)])

                    def emit_pv(kblk):
                        sl = kblk % 2
                        for qb in range(4):
                            if 4 * qt + qb < kblk:
                                continue
                            pb = 2 + qb
                            mm(PS[pb][0:128, 0:65],
                               arena_d[:, D_PT + sl * 512 + qb * 128:D_PT + sl * 512 + (qb + 1) * 128],
                               arena_d[:, voff + kblk * 65:voff + (kblk + 1) * 65],
                               [(m_d, D_PT + sl * 512 + qb * 128, D_PT + sl * 512 + (qb + 1) * 128),
                                (m_d, voff + kblk * 65, voff + (kblk + 1) * 65)],
                               [(m_ps[pb], 0, 65)],
                               start=(kblk == 0), stop=(kblk == 4 * qt + qb), skip=True)

                    emit_st(0)
                    for kblk in range(1, nk):
                        emit_st(kblk)
                        emit_pv(kblk - 1)
                    emit_pv(nk - 1)
                    for qb in range(4):
                        pb = 2 + qb
                        vec(lambda e, pb=pb: e.reciprocal(smalls[:, 1:2], PS[pb][:, 64:65]),
                            [(m_ps[pb], 64, 65)], [(m_sm, 1, 2)])
                        sca(lambda e, pb=pb, qt=qt, qb=qb, hh=hh: e.activation(
                                arena_d[:, D_AT + (qt * 4 + qb) * 128 + hh * 64:
                                        D_AT + (qt * 4 + qb) * 128 + hh * 64 + 64],
                                PS[pb][:, 0:64], Act.Copy, scale=smalls[:, 1:2]),
                            [(m_ps[pb], 0, 64), (m_sm, 1, 2)],
                            [(m_d, D_AT + (qt * 4 + qb) * 128 + hh * 64,
                              D_AT + (qt * 4 + qb) * 128 + hh * 64 + 64)])
            if dbg and s == 0 and pr == 0:
                dma("gpsimd", "dbgk", dbg_bir[0:1, 0:2 * D], b_in_row[0:1, :],
                    [(m_bir, 0, 2 * D)], [(m_dbg, 0, 1)])
                dma("gpsimd", "dbgk", dbg_bir[0:1, 2 * D:2 * D + 512], ones_r[0:1, :],
                    [(m_onr, 0, 512)], [(m_dbg, 0, 1)])
                dma("gpsimd", "dbgk", dbg_bir[0:1, 2 * D + 512:2 * D + 512 + D], bvrow[0:1, :],
                    [(m_bv, 0, D)], [(m_dbg, 0, 1)])
                dump_bf(dbg_qk, arena_e, m_e, 2 * S)
                dump_bf(dbg_v, arena_d, m_d, 4224)
                for col in range(0, T128 * 128, 2048):
                    dma("gpsimd", "dbgk", dbg_at[:, col:col + 2048],
                        arena_d[:, D_AT + col:D_AT + col + 2048],
                        [(m_d, D_AT + col, D_AT + col + 2048)], [(m_dbg, 0, 1)])
                dma("gpsimd", "dbgk", dbg_nc[:, 0:4], smalls[:, 0:4],
                    [(m_sm, 0, 4)], [(m_dbg, 0, 1)])
            for qb in range(T128):
                pb = 6 + (qb % 2)
                mm(PS[pb][:, 0:128], arena_d[:, D_AT + qb * 128:D_AT + (qb + 1) * 128],
                   id_bf[:, :],
                   [(m_d, D_AT + qb * 128, D_AT + (qb + 1) * 128), (m_id, *WHOLE)],
                   [(m_ps[pb], 0, 128)])
                vec(lambda e, qb=qb, pb=pb, pr=pr: e.tensor_copy(
                        arena_c[:, pr * S + qb * 128:pr * S + (qb + 1) * 128], PS[pb][:, 0:128]),
                    [(m_ps[pb], 0, 128)],
                    [(m_c, pr * S + qb * 128, pr * S + (qb + 1) * 128)])

        if dbg and s == 0:
            dump_bf(dbg_attn, arena_c, m_c, KC * S)
        if stop_after == "attn":
            stopped[0] = True
            break
        dma("gpsimd", "wo", arena_a[:, 0:KC * D].rearrange("p (a c) -> p a c", a=KC),
            dwout.rearrange("(a p) c -> p a c", p=128),
            [], [(m_a, 0, KC * D)])
        vec(lambda e, s=s: e.tensor_tensor(rows[0:1, R_GAB:R_GAB + D],
                                           rows[0:1, R_GA + s * D:R_GA + (s + 1) * D],
                                           borow[0:1, :], Alu.mult),
            [(m_rows, R_GA + s * D, R_GA + (s + 1) * D), (m_bo, *WHOLE)],
            [(m_rows, R_GAB, R_GAB + D)])
        for n in range(NOT):
            mm(PS[6][:, 0:OW], ones_r[0:1, 0:128],
               rows[0:1, R_GA + s * D + n * OW:R_GA + s * D + (n + 1) * OW],
               [(m_onr, *WHOLE), (m_rows, R_GA + s * D + n * OW, R_GA + s * D + (n + 1) * OW)],
               [(m_ps[6], 0, OW)])
            vec(lambda e, n=n: e.tensor_copy(bcast[:, D + n * OW:D + (n + 1) * OW],
                                             PS[6][:, 0:OW]),
                [(m_ps[6], 0, OW)], [(m_bc, D + n * OW, D + (n + 1) * OW)])
        for kc in range(KC):
            vec(lambda e, kc=kc: e.tensor_tensor(
                    arena_d[:, kc * D:(kc + 1) * D],
                    arena_a[:, kc * D:(kc + 1) * D],
                    bcast[:, D:2 * D], Alu.mult),
                [(m_a, kc * D, (kc + 1) * D), (m_bc, D, 2 * D)],
                [(m_d, kc * D, (kc + 1) * D)])

        for t in range(T128):
            sl = t % 2
            xg = arena_g[:, sl * D:(sl + 1) * D]
            dma("sync", f"x{sl}", xg, dx[s, t * 128:(t + 1) * 128, :], [],
                [(m_g, sl * D, (sl + 1) * D)])
            for n in range(NOT):
                pb = n % 2
                for kc in range(KC):
                    mm(PS[pb][:, 0:OW],
                       arena_c[:, kc * S + t * 128:kc * S + (t + 1) * 128],
                       arena_d[:, kc * D + n * OW:kc * D + (n + 1) * OW],
                       [(m_c, kc * S + t * 128, kc * S + (t + 1) * 128),
                        (m_d, kc * D + n * OW, kc * D + (n + 1) * OW)],
                       [(m_ps[pb], 0, OW)],
                       start=(kc == 0), stop=False)
                mm(PS[pb][:, 0:OW], ones_r[:, 0:128],
                   rows[0:1, R_GAB + n * OW:R_GAB + (n + 1) * OW],
                   [(m_onr, *WHOLE), (m_rows, R_GAB + n * OW, R_GAB + (n + 1) * OW)],
                   [(m_ps[pb], 0, OW)], start=False, stop=True)
            xb = arena_g[:, (2 + sl) * D:(3 + sl) * D]
            for n in range(NOT):
                vec(lambda e, n=n, xb=xb, xg=xg: e.tensor_tensor(
                        xb[:, n * OW:(n + 1) * OW], PS[n % 2][:, 0:OW],
                        xg[:, n * OW:(n + 1) * OW], Alu.add),
                    [(m_ps[n % 2], 0, OW), (m_g, sl * D + n * OW, sl * D + (n + 1) * OW)],
                    [(m_g, (2 + sl) * D + n * OW, (2 + sl) * D + (n + 1) * OW)])
            dma("sync", f"sp{sl}", dxb[s, t * 128:(t + 1) * 128, :], xb,
                [(m_g, (2 + sl) * D, (3 + sl) * D)],
                [(m_xbs, s * S + t * 128, s * S + (t + 1) * 128)])
            hn = hn_ring[:, sl * D:(sl + 1) * D]
            ln_tile(xb, m_g, (2 + sl) * D, hn, sl * D)
            for ch in range(KC):
                pb = 2 + (ch % 2)
                mm(PS[pb][:, 0:128], hn[:, ch * 128:(ch + 1) * 128], id_bf[:, :],
                   [(m_hn, sl * D + ch * 128, sl * D + (ch + 1) * 128), (m_id, *WHOLE)],
                   [(m_ps[pb], 0, 128)], start=True, stop=False)
                mm(PS[pb][:, 0:128], rows[0:1, R_SBP + ch * 128:R_SBP + (ch + 1) * 128],
                   ones_r[0:1, 0:128],
                   [(m_rows, R_SBP + ch * 128, R_SBP + (ch + 1) * 128), (m_onr, *WHOLE)],
                   [(m_ps[pb], 0, 128)], start=False, stop=True)
                sca(lambda e, ch=ch, t=t, pb=pb, s=s: e.activation(
                        arena_b[:, ch * S + t * 128:ch * S + (t + 1) * 128],
                        PS[pb][:, 0:128], Act.Copy,
                        scale=mods_fm[:, 2 * NK + ch * 2 + s:2 * NK + ch * 2 + s + 1]),
                    [(m_ps[pb], 0, 128), (m_mods, 2 * NK, 4 * NK)],
                    [(m_b, ch * S + t * 128, ch * S + (t + 1) * 128)])

        if stop_after == "proj":
            stopped[0] = True
            break
        for t in range(T128):
            for kc in range(KC):
                mm(PS[4][:, 0:NE],
                   arena_b[:, kc * S + t * 128:kc * S + (t + 1) * 128],
                   gateT[:, kc * NE:(kc + 1) * NE],
                   [(m_b, kc * S + t * 128, kc * S + (t + 1) * 128), (m_gT, kc * NE, (kc + 1) * NE)],
                   [(m_ps[4], 0, NE)],
                   start=(kc == 0), stop=(kc == KC - 1))
            routing_tile(p, cfg, PS[4], m_ps[4], rout, m_rt, sc_bf, m_scb, vec, sca)
            mm(PS[5][0:NE, 0:128], sc_bf[:, 0:NE], id_bf[:, :],
               [(m_scb, 0, NE), (m_id, *WHOLE)], [(m_ps[5], 0, 128)])
            vec(lambda e, t=t: e.tensor_copy(c_fm[0:NE, t * 128:(t + 1) * 128],
                                             PS[5][0:NE, 0:128]),
                [(m_ps[5], 0, 128)], [(m_cfm, t * 128, (t + 1) * 128)])

        if dbg and s == 0:
            dump_bf(dbg_cfm, c_fm, m_cfm, S)
        if stop_after == "gate":
            stopped[0] = True
            break
        do_gb = stop_after != "moe_nogb"
        for ch in (range(KC) if do_gb else ()):
            mm(PS[6][0:1, 0:128], ga_fm[:, NK + ch * 2 + s:NK + ch * 2 + s + 1], id_bf[:, :],
               [(m_gafm, NK + ch * 2 + s, NK + ch * 2 + s + 1), (m_id, *WHOLE)],
               [(m_ps[6], 0, 128)])
            vec(lambda e, ch=ch: e.tensor_copy(rows[0:1, R_GB + ch * 128:R_GB + (ch + 1) * 128],
                                               PS[6][0:1, 0:128]),
                [(m_ps[6], 0, 128)], [(m_rows, R_GB + ch * 128, R_GB + (ch + 1) * 128)])
        for n in (range(NOT) if do_gb else ()):
            mm(PS[6][:, 0:OW], ones_r[0:1, 0:128], rows[0:1, R_GB + n * OW:R_GB + (n + 1) * OW],
               [(m_onr, *WHOLE), (m_rows, R_GB + n * OW, R_GB + (n + 1) * OW)],
               [(m_ps[6], 0, OW)])
            vec(lambda e, n=n: e.tensor_copy(bcast[:, D + n * OW:D + (n + 1) * OW],
                                             PS[6][:, 0:OW]),
                [(m_ps[6], 0, OW)], [(m_bc, D + n * OW, D + (n + 1) * OW)])

        moe_groups = NGRP
        if stop_after in ("moe_dma", "moe_g0t0", "moe_g0", "moe_nogb"):
            moe_groups = 1
        for g in range(moe_groups):
            shared = (g == NGRP - 1)
            if not shared:
                e0 = 2 * g
                for ei in range(2):
                    wiv = [(m_a, A_WA + kc * GWD + ei * INTER, A_WA + kc * GWD + (ei + 1) * INTER)
                           for kc in range(KC)]
                    dma("gpsimd", "wa",
                        arena_a[:, A_WA:A_WA + WAB].rearrange(
                            "p (a e c) -> p a e c", a=KC, e=2)[:, :, ei, :],
                        dw1[e0 + ei].rearrange("(a p) c -> p a c", p=128),
                        [], wiv)
                    wiv = [(m_a, A_WB + kc * GWD + ei * INTER, A_WB + kc * GWD + (ei + 1) * INTER)
                           for kc in range(KC)]
                    dma("gpsimd", "wb",
                        arena_a[:, A_WB:A_WB + WAB].rearrange(
                            "p (a e c) -> p a e c", a=KC, e=2)[:, :, ei, :],
                        dw3[e0 + ei].rearrange("(a p) c -> p a c", p=128),
                        [], wiv)
                    dma("gpsimd", "wc",
                        arena_a[:, A_WC:A_WC + WAB].rearrange(
                            "p (a c) -> p a c", a=ICG)[:, ei * (ICG // 2):(ei + 1) * (ICG // 2), :],
                        dw2[e0 + ei].rearrange("(a p) c -> p a c", p=128),
                        [], [(m_a, A_WC + ei * (ICG // 2) * D, A_WC + (ei + 1) * (ICG // 2) * D)])
            else:
                dma("gpsimd", "wa",
                    arena_a[:, A_WA:A_WA + WAB].rearrange("p (a c) -> p a c", a=KC),
                    dws1.rearrange("(a p) c -> p a c", p=128),
                    [], [(m_a, A_WA, A_WA + WAB)])
                dma("gpsimd", "wb",
                    arena_a[:, A_WB:A_WB + WAB].rearrange("p (a c) -> p a c", a=KC),
                    dws3.rearrange("(a p) c -> p a c", p=128),
                    [], [(m_a, A_WB, A_WB + WAB)])
                dma("gpsimd", "wc",
                    arena_a[:, A_WC:A_WC + WAB].rearrange("p (a c) -> p a c", a=ICG),
                    dws2.rearrange("(a p) c -> p a c", p=128),
                    [], [(m_a, A_WC, A_WC + WAB)])
            for ic in (range(ICG) if do_gb else ()):
                vec(lambda e, ic=ic: e.tensor_tensor(
                        arena_a[:, A_WC + ic * D:A_WC + (ic + 1) * D],
                        arena_a[:, A_WC + ic * D:A_WC + (ic + 1) * D],
                        bcast[:, D:2 * D], Alu.mult),
                    [(m_a, A_WC + ic * D, A_WC + (ic + 1) * D), (m_bc, D, 2 * D)],
                    [(m_a, A_WC + ic * D, A_WC + (ic + 1) * D)])

            if stop_after in ("moe_dma", "moe_nogb"):
                break
            for t in range(T512):
                if stop_after == "moe_g0t0" and t > 0:
                    break
                asl = t % 2
                A0 = asl * ICG * 512
                if not shared:
                    for ei in range(2):
                        mm(PS[6][:, 0:512],
                           sel16[:, (2 * g + ei) * 128:(2 * g + ei + 1) * 128],
                           c_fm[0:NE, t * 512:(t + 1) * 512],
                           [(m_sel, (2 * g + ei) * 128, (2 * g + ei + 1) * 128),
                            (m_cfm, t * 512, (t + 1) * 512)],
                           [(m_ps[6], 0, 512)])
                        vec(lambda e, asl=asl, ei=ei: e.tensor_copy(
                                cb_ring[:, asl * 1024 + ei * 512:asl * 1024 + (ei + 1) * 512],
                                PS[6][:, 0:512]),
                            [(m_ps[6], 0, 512)],
                            [(m_cb, asl * 1024 + ei * 512, asl * 1024 + (ei + 1) * 512)])
                for ic in range(ICG):
                    p1b, p3b = ic % 2, 2 + ic % 2
                    for kc in range(KC):
                        mm(PS[p1b][:, 0:512],
                           arena_a[:, A_WA + kc * GWD + ic * 128:A_WA + kc * GWD + (ic + 1) * 128],
                           arena_b[:, kc * S + t * 512:kc * S + (t + 1) * 512],
                           [(m_a, A_WA + kc * GWD + ic * 128, A_WA + kc * GWD + (ic + 1) * 128),
                            (m_b, kc * S + t * 512, kc * S + (t + 1) * 512)],
                           [(m_ps[p1b], 0, 512)],
                           start=(kc == 0), stop=(kc == KC - 1))
                    for kc in range(KC):
                        mm(PS[p3b][:, 0:512],
                           arena_a[:, A_WB + kc * GWD + ic * 128:A_WB + kc * GWD + (ic + 1) * 128],
                           arena_b[:, kc * S + t * 512:kc * S + (t + 1) * 512],
                           [(m_a, A_WB + kc * GWD + ic * 128, A_WB + kc * GWD + (ic + 1) * 128),
                            (m_b, kc * S + t * 512, kc * S + (t + 1) * 512)],
                           [(m_ps[p3b], 0, 512)],
                           start=(kc == 0), stop=(kc == KC - 1))
                    tsl = ic % 2
                    t1 = moe_tmp[:, tsl * 512:(tsl + 1) * 512]
                    t3 = moe_tmp[:, 1024 + tsl * 512:1024 + (tsl + 1) * 512]
                    if shared:
                        vec(lambda e, t3=t3, p3b=p3b: e.tensor_copy(t3, PS[p3b][:, 0:512]),
                            [(m_ps[p3b], 0, 512)], [(m_mt, 1024 + tsl * 512, 1024 + (tsl + 1) * 512)])
                    else:
                        ei = ic // (ICG // 2)
                        vec(lambda e, t3=t3, p3b=p3b, asl=asl, ei=ei: e.tensor_tensor(
                                t3, PS[p3b][:, 0:512],
                                cb_ring[:, asl * 1024 + ei * 512:asl * 1024 + (ei + 1) * 512],
                                Alu.mult),
                            [(m_ps[p3b], 0, 512),
                             (m_cb, asl * 1024 + ei * 512, asl * 1024 + (ei + 1) * 512)],
                            [(m_mt, 1024 + tsl * 512, 1024 + (tsl + 1) * 512)])
                    sca(lambda e, t1=t1, p1b=p1b: e.activation(t1, PS[p1b][:, 0:512], Act.Silu),
                        [(m_ps[p1b], 0, 512)], [(m_mt, tsl * 512, (tsl + 1) * 512)])
                    vec(lambda e, t1=t1, t3=t3, ic=ic, A0=A0: e.tensor_tensor(
                            arena_d[:, A0 + ic * 512:A0 + (ic + 1) * 512], t1, t3, Alu.mult),
                        [(m_mt, tsl * 512, (tsl + 1) * 512),
                         (m_mt, 1024 + tsl * 512, 1024 + (tsl + 1) * 512)],
                        [(m_d, A0 + ic * 512, A0 + (ic + 1) * 512)])
                for mtile in range(4):
                    for n in range(NOT):
                        pb = 4 + ((mtile * NOT + n) % 2)
                        for ic in range(ICG):
                            mm(PS[pb][:, 0:OW],
                               arena_d[:, A0 + ic * 512 + mtile * 128:A0 + ic * 512 + (mtile + 1) * 128],
                               arena_a[:, A_WC + ic * D + n * OW:A_WC + ic * D + (n + 1) * OW],
                               [(m_d, A0 + ic * 512 + mtile * 128, A0 + ic * 512 + (mtile + 1) * 128),
                                (m_a, A_WC + ic * D + n * OW, A_WC + ic * D + (n + 1) * OW)],
                               [(m_ps[pb], 0, OW)],
                               start=(ic == 0), stop=(ic == ICG - 1))
                        ycol = (t * 4 + mtile) * D + n * OW
                        if g == 0:
                            vec(lambda e, pb=pb, ycol=ycol: e.tensor_copy(
                                    arena_c[:, ycol:ycol + OW], PS[pb][:, 0:OW]),
                                [(m_ps[pb], 0, OW)], [(m_c, ycol, ycol + OW)])
                        else:
                            vec(lambda e, pb=pb, ycol=ycol: e.tensor_tensor(
                                    arena_c[:, ycol:ycol + OW], PS[pb][:, 0:OW],
                                    arena_c[:, ycol:ycol + OW], Alu.add),
                                [(m_ps[pb], 0, OW), (m_c, ycol, ycol + OW)],
                                [(m_c, ycol, ycol + OW)])

        if dbg and s == 0:
            dump_bf(dbg_y, arena_c, m_c, KC * S)
        if stop_after in ("moe", "moe_dma", "moe_g0t0", "moe_g0", "moe_nogb"):
            stopped[0] = True
            break
        for t in range(T128):
            sl = t % 2
            xg = arena_g[:, sl * D:(sl + 1) * D]
            dma("sync", f"x{sl}", xg, dxb[s, t * 128:(t + 1) * 128, :],
                [(m_xbs, s * S + t * 128, s * S + (t + 1) * 128)],
                [(m_g, sl * D, (sl + 1) * D)])
            vec(lambda e, t=t, sl=sl, xg=xg: e.tensor_tensor(
                    arena_g[:, (2 + sl) * D:(3 + sl) * D], xg,
                    arena_c[:, t * D:(t + 1) * D], Alu.add),
                [(m_g, sl * D, (sl + 1) * D), (m_c, t * D, (t + 1) * D)],
                [(m_g, (2 + sl) * D, (3 + sl) * D)])
            dma("sync", "st", dout[s, t * 128:(t + 1) * 128, :],
                arena_g[:, (2 + sl) * D:(3 + sl) * D],
                [(m_g, (2 + sl) * D, (3 + sl) * D)],
                [(m_out, s * S + t * 128, s * S + (t + 1) * 128)])

    if not stopped[0]:
        p.do("sync", None, r=[(m_out, 0, SEQS * S)])

    plans, sem_names = p.finalize()

    sems = {}
    for sn in sem_names:
        g = nc.semaphore(f"sem_{sn}")
        sems[sn] = g.__enter__()
        ctx.append(g)

    def mk(e):
        def run(eng):
            for waits, emit, inc, inc_sem, drain in plans[e]:
                for (sn, v) in waits:
                    eng.wait_ge(sems[sn], v)
                if emit is not None:
                    inst = emit(eng)
                    if inc:
                        inst.then_inc(sems[inc_sem], inc)
                    if drain:
                        eng.drain()
        return run

    blk = nc.Block()
    block = blk.__enter__()
    block.tensor(mk("tensor"))
    block.vector(mk("vector"))
    block.scalar(mk("scalar"))
    block.sync(mk("sync"))
    block.gpsimd(mk("gpsimd"))
    blk.__exit__(None, None, None)

    for g in reversed(ctx):
        g.__exit__(None, None, None)
    return nc


def routing_tile(p, cfg, psl, m_psl, rout, m_rt, sc_bf, m_scb, vec, sca):
    NE = cfg.NE
    G0, W0, E0, S0, M0, T0 = 0, 4, 20, 36, 52, 68
    vec(lambda e: e.reduce_max(rout[:, T0:T0 + 1], psl[:, 0:NE], Ax.X, negate=True),
        [(m_psl, 0, NE)], [(m_rt, T0, T0 + 1)], drain=True)
    sca(lambda e: e.activation(rout[:, E0:E0 + NE], psl[:, 0:NE], Act.Exp,
                               bias=rout[:, T0:T0 + 1]),
        [(m_psl, 0, NE), (m_rt, T0, T0 + 1)], [(m_rt, E0, E0 + NE)])
    vec(lambda e: e.reduce_sum(rout[:, T0 + 1:T0 + 2], rout[:, E0:E0 + NE], Ax.X),
        [(m_rt, E0, E0 + NE)], [(m_rt, T0 + 1, T0 + 2)], drain=True)
    vec(lambda e: e.reciprocal(rout[:, T0 + 2:T0 + 3], rout[:, T0 + 1:T0 + 2]),
        [(m_rt, T0 + 1, T0 + 2)], [(m_rt, T0 + 2, T0 + 3)])
    vec(lambda e: e.tensor_scalar_mul(rout[:, S0:S0 + NE], rout[:, E0:E0 + NE],
                                      rout[:, T0 + 2:T0 + 3]),
        [(m_rt, E0, E0 + NE), (m_rt, T0 + 2, T0 + 3)], [(m_rt, S0, S0 + NE)])
    for r in range(4):
        vec(lambda e, r=r: e.reduce_max(rout[:, G0 + r:G0 + r + 1],
                                        rout[:, S0 + 4 * r:S0 + 4 * r + 4], Ax.X),
            [(m_rt, S0 + 4 * r, S0 + 4 * r + 4)], [(m_rt, G0 + r, G0 + r + 1)], drain=True)
    vec(lambda e: e.reduce_max(rout[:, T0 + 3:T0 + 4], rout[:, G0:G0 + 4], Ax.X),
        [(m_rt, G0, G0 + 4)], [(m_rt, T0 + 3, T0 + 4)], drain=True)
    vec(lambda e: e.tensor_scalar(rout[:, W0:W0 + 4], rout[:, G0:G0 + 4],
                                  rout[:, T0 + 3:T0 + 4], None, Alu.is_ge),
        [(m_rt, G0, G0 + 4), (m_rt, T0 + 3, T0 + 4)], [(m_rt, W0, W0 + 4)])
    vec(lambda e: e.scalar_tensor_tensor(rout[:, W0:W0 + 4], rout[:, W0:W0 + 4], KNOCK_NEG,
                                         rout[:, G0:G0 + 4], Alu.mult, Alu.add),
        [(m_rt, W0, W0 + 4), (m_rt, G0, G0 + 4)], [(m_rt, W0, W0 + 4)])
    vec(lambda e: e.reduce_max(rout[:, T0 + 4:T0 + 5], rout[:, W0:W0 + 4], Ax.X),
        [(m_rt, W0, W0 + 4)], [(m_rt, T0 + 4, T0 + 5)], drain=True)
    vec(lambda e: e.tensor_scalar(rout[:, G0:G0 + 4], rout[:, G0:G0 + 4],
                                  rout[:, T0 + 4:T0 + 5], None, Alu.is_ge),
        [(m_rt, G0, G0 + 4), (m_rt, T0 + 4, T0 + 5)], [(m_rt, G0, G0 + 4)])
    for r in range(4):
        vec(lambda e, r=r: e.tensor_scalar_mul(rout[:, M0 + 4 * r:M0 + 4 * r + 4],
                                               rout[:, S0 + 4 * r:S0 + 4 * r + 4],
                                               rout[:, G0 + r:G0 + r + 1]),
            [(m_rt, S0 + 4 * r, S0 + 4 * r + 4), (m_rt, G0 + r, G0 + r + 1)],
            [(m_rt, M0 + 4 * r, M0 + 4 * r + 4)])
    vec(lambda e: e.tensor_copy(rout[:, W0:W0 + NE], rout[:, M0:M0 + NE]),
        [(m_rt, M0, M0 + NE)], [(m_rt, W0, W0 + NE)])
    for k in range(3):
        vec(lambda e: e.reduce_max(rout[:, T0 + 5:T0 + 6], rout[:, W0:W0 + NE], Ax.X),
            [(m_rt, W0, W0 + NE)], [(m_rt, T0 + 5, T0 + 6)], drain=True)
        vec(lambda e: e.tensor_scalar(rout[:, E0:E0 + NE], rout[:, W0:W0 + NE],
                                      rout[:, T0 + 5:T0 + 6], None, Alu.is_ge),
            [(m_rt, W0, W0 + NE), (m_rt, T0 + 5, T0 + 6)], [(m_rt, E0, E0 + NE)])
        vec(lambda e: e.scalar_tensor_tensor(rout[:, W0:W0 + NE], rout[:, E0:E0 + NE], KNOCK_NEG,
                                             rout[:, W0:W0 + NE], Alu.mult, Alu.add),
            [(m_rt, E0, E0 + NE), (m_rt, W0, W0 + NE)], [(m_rt, W0, W0 + NE)])
    vec(lambda e: e.reduce_max(rout[:, T0 + 5:T0 + 6], rout[:, W0:W0 + NE], Ax.X),
        [(m_rt, W0, W0 + NE)], [(m_rt, T0 + 5, T0 + 6)], drain=True)
    vec(lambda e: e.tensor_scalar(rout[:, E0:E0 + NE], rout[:, M0:M0 + NE],
                                  rout[:, T0 + 5:T0 + 6], None, Alu.is_ge),
        [(m_rt, M0, M0 + NE), (m_rt, T0 + 5, T0 + 6)], [(m_rt, E0, E0 + NE)])
    vec(lambda e: e.tensor_tensor(sc_bf[:, 0:NE], rout[:, S0:S0 + NE],
                                  rout[:, E0:E0 + NE], Alu.mult),
        [(m_rt, S0, S0 + NE), (m_rt, E0, E0 + NE)], [(m_scb, 0, NE)])



def mk_consts(cfg):
    NE = cfg.NE
    id128 = np.eye(128, dtype=ml_dtypes.bfloat16)
    i = np.arange(128)
    mask = np.where(i[None, :] >= i[:, None], 0.0, MASK_NEG).astype(np.float32)
    sel = np.zeros((NE, NE * 128), dtype=ml_dtypes.bfloat16)
    for e in range(NE):
        sel[e, e * 128:(e + 1) * 128] = 1.0
    return id128, mask, sel


def make_inputs(cfg, **a):
    D = cfg.D
    x = np.ascontiguousarray(np.asarray(a["x"], dtype=np.float32))
    w_in = np.asarray(a["w_in"], dtype=np.float32)
    wq = w_in[:, 0:D].reshape(D, cfg.NPAIR, 128)
    wk = w_in[:, D:2 * D].reshape(D, cfg.NPAIR, 128)
    wv = w_in[:, 2 * D:3 * D].reshape(D, cfg.NPAIR, 128)
    w_in_re = np.ascontiguousarray(np.concatenate([wq, wk, wv], axis=2))
    id128, mask, sel = mk_consts(cfg)
    shared = dict(
        w_in_re=w_in_re,
        b_in=np.ascontiguousarray(np.asarray(a["b_in"], dtype=np.float32)),
        w_out=np.ascontiguousarray(np.asarray(a["w_out"], dtype=np.float32)),
        b_out=np.ascontiguousarray(np.asarray(a["b_out"], dtype=np.float32)),
        gate_w=np.ascontiguousarray(np.asarray(a["gate_w"], dtype=np.float32)),
        w1=np.ascontiguousarray(np.asarray(a["w1"], dtype=np.float32)),
        w2=np.ascontiguousarray(np.asarray(a["w2"], dtype=np.float32)),
        w3=np.ascontiguousarray(np.asarray(a["w3"], dtype=np.float32)),
        ws1=np.ascontiguousarray(np.asarray(a["ws1"], dtype=np.float32)),
        ws2=np.ascontiguousarray(np.asarray(a["ws2"], dtype=np.float32)),
        ws3=np.ascontiguousarray(np.asarray(a["ws3"], dtype=np.float32)),
        wmod=np.ascontiguousarray(np.asarray(a["wmod"], dtype=np.float32)),
        id128d=id128, mask_td=mask, sel16d=sel,
    )
    cvec = np.asarray(a["cvec"], dtype=np.float32)
    return x, cvec, shared


def kernel(L0_attn_x, L0_attn_w_in, L0_attn_b_in, L0_attn_w_out, L0_attn_b_out,
           L0_ffn_gate_w, L0_ffn_w1, L0_ffn_w2, L0_ffn_w3, L0_ffn_ws1, L0_ffn_ws2,
           L0_ffn_ws3, L0_c, L0_wmod):
    cfg = CFG()
    x, cvec, shared = make_inputs(
        cfg, x=L0_attn_x, w_in=L0_attn_w_in, b_in=L0_attn_b_in, w_out=L0_attn_w_out,
        b_out=L0_attn_b_out, gate_w=L0_ffn_gate_w, w1=L0_ffn_w1, w2=L0_ffn_w2,
        w3=L0_ffn_w3, ws1=L0_ffn_ws1, ws2=L0_ffn_ws2, ws3=L0_ffn_ws3, cvec=L0_c,
        wmod=L0_wmod)
    nc = build_program(cfg, paranoid=True)
    in_maps = []
    for c in range(NCORES):
        m = dict(shared)
        m["x"] = np.ascontiguousarray(x[c * cfg.SEQS:(c + 1) * cfg.SEQS])
        m["cvec"] = np.ascontiguousarray(cvec[c * cfg.SEQS:(c + 1) * cfg.SEQS])
        in_maps.append(m)
    res = run_bass_kernel_spmd(nc, in_maps, core_ids=list(range(NCORES)))
    out = np.concatenate([res.results[c]["out"] for c in range(NCORES)], axis=0)
    return np.ascontiguousarray(out.astype(np.float32))


# version 08
# speedup vs baseline: 2.1027x; 2.1027x over previous
import os

import numpy as np
import ml_dtypes

import concourse.bass as bass
import concourse.mybir as mybir
from concourse.bass_utils import run_bass_kernel_spmd

F32 = mybir.dt.float32
BF16 = mybir.dt.bfloat16
Alu = mybir.AluOpType
Act = mybir.ActivationFunctionType
Ax = mybir.AxisListType

NCORES = 8
MASK_NEG = -30000.0
KNOCK_NEG = -1e30


class CFG:
    SEQS = 2
    S = 2048
    D = 1024
    H = 16
    NE = 16
    INTER = 512
    LN_EPS = 1e-5
    SHIFT_CLAMP = -60.0

    HD = property(lambda s: s.D // s.H)
    NPAIR = property(lambda s: s.H // 2)
    KC = property(lambda s: s.D // 128)
    T128 = property(lambda s: s.S // 128)
    T512 = property(lambda s: s.S // 512)
    SHARED = property(lambda s: 2 * s.INTER)
    GW = property(lambda s: 2 * s.INTER)
    ICG = property(lambda s: s.GW // 128)
    NGRP = property(lambda s: s.NE // 2 + 1)
    MODC = property(lambda s: 6 * s.D // 128)



ENGINES = ("tensor", "vector", "scalar", "sync", "gpsimd")
DMA_ENGINES = frozenset(("sync", "gpsimd"))


class Tok:
    __slots__ = ("eng", "idx", "gid")

    def __init__(self, eng, idx, gid):
        self.eng = eng
        self.idx = idx
        self.gid = gid


class Mem:

    __slots__ = ("name", "writers", "readers", "coarse")

    def __init__(self, name, coarse=False):
        self.name = name
        self.writers = []
        self.readers = []
        self.coarse = coarse

    def on_read(self, lo, hi, tok, deps):
        if self.coarse:
            lo, hi = 0, 1 << 30
        for (l, h, t) in self.writers:
            if l < hi and lo < h:
                deps.append(t)
        rd = self.readers
        keep = [e for e in rd if not (e[2].eng == tok.eng and lo <= e[0] and e[1] <= hi)]
        keep.append([lo, hi, tok])
        self.readers = keep

    def on_write(self, lo, hi, tok, deps):
        if self.coarse:
            lo, hi = 0, 1 << 30
        for (l, h, t) in self.writers:
            if l < hi and lo < h:
                deps.append(t)
        for (l, h, t) in self.readers:
            if l < hi and lo < h:
                deps.append(t)
        self.writers = [e for e in self.writers if not (lo <= e[0] and e[1] <= hi)]
        self.readers = [e for e in self.readers if not (lo <= e[0] and e[1] <= hi)]
        self.writers.append([lo, hi, tok])


class Prog:

    def __init__(self, paranoid=False):
        if paranoid is True:
            paranoid = set(ENGINES)
        elif not paranoid:
            paranoid = set()
        self.paranoid = paranoid
        self.streams = {e: [] for e in ENGINES}
        self.keys = []
        self.gid = 0

    def do(self, eng, emit, r=(), w=(), key=None, drain=False):
        if eng in DMA_ENGINES and emit is not None:
            assert key is not None, "DMA op requires a semaphore key"
            if key not in self.keys:
                self.keys.append(key)
        self.gid += 1
        tok = Tok(eng, len(self.streams[eng]), self.gid)
        deps = []
        for (m, lo, hi) in r:
            m.on_read(lo, hi, tok, deps)
        for (m, lo, hi) in w:
            m.on_write(lo, hi, tok, deps)
        deps = [d for d in deps if d is not tok]
        self.streams[eng].append([emit, deps, False, key, tok.gid, drain])
        return tok

    def finalize(self):
        import bisect
        for e in ENGINES:
            for op in self.streams[e]:
                if e in DMA_ENGINES and op[0] is not None:
                    op[2] = True
                for d in op[1]:
                    de = d.eng
                    if de in DMA_ENGINES:
                        continue
                    if de != e or de in self.paranoid:
                        self.streams[de][d.idx][2] = True
        key_gids = {k: [] for k in self.keys}
        for e in DMA_ENGINES:
            for op in self.streams[e]:
                if op[0] is not None:
                    key_gids[op[3]].append(op[4])
        for k in key_gids:
            key_gids[k].sort()
        ordinals = {e: [] for e in ENGINES}
        cnt_e = {e: 0 for e in ENGINES}
        for e in ENGINES:
            for op in self.streams[e]:
                if e not in DMA_ENGINES and op[2]:
                    cnt_e[e] += 1
                ordinals[e].append(cnt_e[e])
        plans = {}
        for e in ENGINES:
            plan = []
            last_wait = {}
            for emit, deps, marked, key, gid, drain in self.streams[e]:
                need = {}
                for d in deps:
                    de = d.eng
                    if de in DMA_ENGINES:
                        dk = self.streams[de][d.idx][3]
                        v = bisect.bisect_left(key_gids[dk], gid)
                        if v > 0 and v > need.get(dk, 0):
                            need[dk] = v
                    else:
                        if de == e and de not in self.paranoid:
                            continue
                        v = ordinals[de][d.idx]
                        if v > 0 and v > need.get(de, 0):
                            need[de] = v
                waits = []
                for sn, v in need.items():
                    if last_wait.get(sn, 0) < v:
                        waits.append((sn, v * (16 if sn in key_gids else 1)))
                        last_wait[sn] = v
                inc_sem = key if e in DMA_ENGINES else e
                inc = (16 if e in DMA_ENGINES else 1) if marked else 0
                plan.append((waits, emit, inc, inc_sem, drain))
            plans[e] = plan
        return plans, [e for e in ENGINES if e not in DMA_ENGINES] + self.keys



def build_program(cfg, paranoid=False, stop_after=None, dbg=False):
    nc = bass.Bass("TRN2", target_bir_lowering=False,
                   detect_race_conditions=bool(int(os.environ.get("KBASS_RACED", "0"))),
                   dynamic_dma_scratch_size=16384)
    SEQS, S, D, NE, INTER = cfg.SEQS, cfg.S, cfg.D, cfg.NE, cfg.INTER
    KC, NPAIR, T128, T512 = cfg.KC, cfg.NPAIR, cfg.T128, cfg.T512
    ICG, NGRP, MODC = cfg.ICG, cfg.NGRP, cfg.MODC
    NK = 2 * KC
    OW = min(512, D)
    NOT = D // OW

    dx = nc.dram_tensor("x", [SEQS, S, D], F32, kind="ExternalInput").ap()
    dwin = nc.dram_tensor("w_in_re", [D, NPAIR, 384], F32, kind="ExternalInput").ap()
    dbin = nc.dram_tensor("b_in", [3 * D], F32, kind="ExternalInput").ap()
    dwout = nc.dram_tensor("w_out", [D, D], F32, kind="ExternalInput").ap()
    dbout = nc.dram_tensor("b_out", [D], F32, kind="ExternalInput").ap()
    dgate = nc.dram_tensor("gate_w", [NE, D], F32, kind="ExternalInput").ap()
    dw1 = nc.dram_tensor("w1", [NE, D, INTER], F32, kind="ExternalInput").ap()
    dw2 = nc.dram_tensor("w2", [NE, INTER, D], F32, kind="ExternalInput").ap()
    dw3 = nc.dram_tensor("w3", [NE, D, INTER], F32, kind="ExternalInput").ap()
    dws1 = nc.dram_tensor("ws1", [D, cfg.SHARED], F32, kind="ExternalInput").ap()
    dws2 = nc.dram_tensor("ws2", [cfg.SHARED, D], F32, kind="ExternalInput").ap()
    dws3 = nc.dram_tensor("ws3", [D, cfg.SHARED], F32, kind="ExternalInput").ap()
    dc = nc.dram_tensor("cvec", [SEQS, D], F32, kind="ExternalInput").ap()
    dwmod = nc.dram_tensor("wmod", [D, 6 * D], F32, kind="ExternalInput").ap()
    did = nc.dram_tensor("id128d", [128, 128], BF16, kind="ExternalInput").ap()
    dmask = nc.dram_tensor("mask_td", [128, 128], F32, kind="ExternalInput").ap()
    dsel = nc.dram_tensor("sel16d", [NE, NE * 128], BF16, kind="ExternalInput").ap()
    dout = nc.dram_tensor("out", [SEQS, S, D], F32, kind="ExternalOutput").ap()
    dxb = nc.dram_tensor("xb_scr", [SEQS, S, D], F32,
                         kind="ExternalOutput" if dbg else "Internal").ap()
    if dbg:
        dbg_mods = nc.dram_tensor("dbg_mods", [128, 4 * NK], F32, kind="ExternalOutput").ap()
        dbg_h = nc.dram_tensor("dbg_h", [128, KC * S], F32, kind="ExternalOutput").ap()
        dbg_attn = nc.dram_tensor("dbg_attn", [128, KC * S], F32, kind="ExternalOutput").ap()
        dbg_cfm = nc.dram_tensor("dbg_cfm", [NE, S], F32, kind="ExternalOutput").ap()
        dbg_y = nc.dram_tensor("dbg_y", [128, KC * S], F32, kind="ExternalOutput").ap()
        dbg_f32 = nc.dram_tensor("dbg_f32", [128, KC * S], F32, kind="Internal").ap()
        dbg_qk = nc.dram_tensor("dbg_qk", [128, 2 * S], F32, kind="ExternalOutput").ap()
        dbg_v = nc.dram_tensor("dbg_v", [128, 4224], F32, kind="ExternalOutput").ap()
        dbg_at = nc.dram_tensor("dbg_at", [128, 2048], F32, kind="ExternalOutput").ap()
        dbg_nc = nc.dram_tensor("dbg_nc", [128, 4], F32, kind="ExternalOutput").ap()
        dbg_bir = nc.dram_tensor("dbg_bir", [1, 2 * D + 512 + D], F32, kind="ExternalOutput").ap()
        m_dbg = Mem("dbg")

        def dump_bf(dst, src_ap, mem, width):
            for col in range(0, width, 4096):
                wdt = min(4096, width - col)
                dma("gpsimd", "dbgk", dst[:, col:col + wdt], src_ap[:, col:col + wdt],
                    [(mem, col, col + wdt)], [(m_dbg, 0, 1)])

    p = Prog(paranoid=paranoid)
    ctx = []

    def sb(name, shape, dtype):
        g = nc.sbuf_tensor(name, shape, dtype)
        t = g.__enter__()
        ctx.append(g)
        return t, Mem(name)

    AW = max(3 * KC * cfg.GW, 2 * KC * 384, 2 * 6 * D, KC * D)
    arena_a, m_a = sb("arena_a", [128, AW], BF16)
    GWD = cfg.GW
    assert GWD == D
    WAB = KC * GWD
    A_WA, A_WB, A_WC = 0, WAB, 2 * WAB
    arena_b, m_b = sb("arena_b", [128, KC * S], BF16)
    arena_c, m_c = sb("arena_c", [128, KC * S], BF16)
    D_PT, D_SQ = 0, 1024
    D_VA, D_VB = 2048, 2048 + T128 * 65
    D_AT = 2048 + 2 * T128 * 65
    DW = max(2 * ICG * 512, D_AT + T128 * 128, KC * 1024)
    arena_d, m_d = sb("arena_d", [128, DW], BF16)
    arena_e, m_e = sb("arena_e", [128, 2 * S], BF16)
    arena_g, m_g = sb("arena_g", [128, 4 * D], F32)
    hn_ring, m_hn = sb("hn_ring", [128, 2 * D], BF16)
    moe_tmp, m_mt = sb("moe_tmp", [128, 2048], BF16)
    cb_ring, m_cb = sb("cb_ring", [128, 2048], BF16)
    c_fm, m_cfm = sb("c_fm", [NE, S], BF16)
    bcast, m_bc = sb("bcast", [128, 2 * D], BF16)
    rows, m_rows = sb("rows", [1, 6 * D], BF16)
    R_GA, R_GAB, R_GB, R_SAP, R_SBP = 0, 2 * D, 3 * D, 4 * D, 5 * D
    bvrow, m_bv = sb("bvrow", [1, D], BF16)
    borow, m_bo = sb("borow", [1, D], BF16)
    mods_fm, m_mods = sb("mods_fm", [128, 4 * NK], F32)
    ga_fm, m_gafm = sb("ga_fm", [128, 2 * NK], BF16)
    id_bf, m_id = sb("id_bf", [128, 128], BF16)
    mask_t, m_mask = sb("mask_t", [128, 128], F32)
    sel16, m_sel = sb("sel16", [NE, NE * 128], BF16)
    gw_bf, m_gw = sb("gw_bf", [NE, D], BF16)
    gateT, m_gT = sb("gateT", [128, KC * NE], BF16)
    c2row, m_c2 = sb("c2row", [2, D], F32)
    c2bf, m_c2b = sb("c2bf", [2, D], BF16)
    csilu, m_cs = sb("csilu", [128, NK], BF16)
    ones_r, m_onr = sb("ones_r", [1, 512], BF16)
    ones_c, m_onc = sb("ones_c", [128, 1], BF16)
    rout, m_rt = sb("rout", [128, 128], F32)
    sc_bf, m_scb = sb("sc_bf", [128, NE], BF16)
    smalls, m_sm = sb("smalls", [128, 16], F32)
    brow, m_br = sb("brow", [1, 16], F32)
    brow_bf, m_brb = sb("brow_bf", [1, 4], BF16)
    b_in_row, m_bir = sb("b_in_row", [1, 2 * D], BF16)
    sap_bf, m_sap = sb("sap_bf", [128, 2 * NK], BF16)

    PS = []
    m_ps = []
    for i in range(8):
        g = nc.psum_tensor(f"ps{i}", [128, 512], F32)
        PS.append(g.__enter__())
        ctx.append(g)
        m_ps.append(Mem(f"ps{i}", coarse=True))
    m_xbs = Mem("xb_scr")
    m_out = Mem("out")

    WHOLE = (0, 1 << 30)

    def mm(out_ap, lhsT_ap, rhs_ap, r, w, start=True, stop=True, skip=False):
        lsh, rsh, osh = lhsT_ap.shape, rhs_ap.shape, out_ap.shape
        assert lsh[0] == rsh[0], f"mm K mismatch {lsh} vs {rsh}"
        assert lsh[1] == osh[0], f"mm M mismatch lhsT {lsh} out {osh}"
        assert rsh[1] == osh[1], f"mm N mismatch rhs {rsh} out {osh}"
        return p.do("tensor",
                    lambda e, o=out_ap, l=lhsT_ap, rr=rhs_ap, st=start, sp=stop, sk=skip:
                        e.matmul(o, l, rr, start=st, stop=sp, skip_group_check=sk),
                    r=r, w=w)

    def vec(fn, r, w, drain=False):
        return p.do("vector", fn, r=r, w=w, drain=drain)

    def sca(fn, r, w):
        return p.do("scalar", fn, r=r, w=w)

    def dma(q, key, out_ap, in_ap, r, w):
        return p.do(q, lambda e, o=out_ap, i=in_ap: e.dma_start(o, i), r=r, w=w, key=key)

    dma("sync", "k_id", id_bf[:, :], did, [], [(m_id, *WHOLE)])
    dma("sync", "k_mask", mask_t[:, :], dmask, [], [(m_mask, *WHOLE)])
    dma("sync", "k_sel", sel16[:, :], dsel, [], [(m_sel, *WHOLE)])
    if SEQS < 2:
        vec(lambda e: e.memset(c2row[:, :], 0.0), [], [(m_c2, *WHOLE)])
    dma("sync", "k_c2", c2row[0:SEQS, :], dc[:, :], [],
        [(m_c2, 0, D if SEQS == 2 else 1)])
    dma("gpsimd", "k_bo", borow[0:1, :], dbout.rearrange("(o d) -> o d", o=1), [], [(m_bo, *WHOLE)])
    dma("gpsimd", "k_bir", b_in_row[0:1, :], dbin.rearrange("(o d) -> o d", o=1)[0:1, 0:2 * D],
        [], [(m_bir, *WHOLE)])
    dma("gpsimd", "k_bv", bvrow[0:1, :], dbin.rearrange("(o d) -> o d", o=1)[0:1, 2 * D:3 * D],
        [], [(m_bv, *WHOLE)])
    dma("gpsimd", "k_gw", gw_bf[:, :], dgate, [], [(m_gw, *WHOLE)])

    vec(lambda e: e.memset(ones_r[:, :], 1.0), [], [(m_onr, *WHOLE)])
    vec(lambda e: e.memset(smalls[:, 6:7], cfg.LN_EPS), [], [(m_sm, 6, 7)])
    vec(lambda e: e.memset(ones_c[:, :], 1.0), [], [(m_onc, *WHOLE)])



    for n in range(NOT):
        mm(PS[6][:, 0:OW], ones_r[0:1, 0:128], bvrow[0:1, n * OW:(n + 1) * OW],
           [(m_onr, *WHOLE), (m_bv, n * OW, (n + 1) * OW)], [(m_ps[6], 0, OW)])
        vec(lambda e, n=n: e.tensor_copy(bcast[:, n * OW:(n + 1) * OW], PS[6][:, 0:OW]),
            [(m_ps[6], 0, OW)], [(m_bc, n * OW, (n + 1) * OW)])

    sca(lambda e: e.activation(c2row[0:2, :], c2row[0:2, :], Act.Silu),
        [(m_c2, *WHOLE)], [(m_c2, *WHOLE)])
    vec(lambda e: e.tensor_copy(c2bf[0:2, :], c2row[0:2, :]),
        [(m_c2, *WHOLE)], [(m_c2b, *WHOLE)])
    for ch in range(KC):
        mm(PS[6][0:128, 0:2], c2bf[0:2, ch * 128:(ch + 1) * 128], id_bf[0:2, 0:2],
           [(m_c2b, ch * 128, (ch + 1) * 128), (m_id, 0, 2)], [(m_ps[6], 0, 2)])
        vec(lambda e, ch=ch: e.tensor_copy(csilu[:, ch * 2:ch * 2 + 2], PS[6][0:128, 0:2]),
            [(m_ps[6], 0, 2)], [(m_cs, ch * 2, ch * 2 + 2)])

    for ch in range(KC):
        mm(PS[6][0:128, 0:NE], gw_bf[0:NE, ch * 128:(ch + 1) * 128], id_bf[0:NE, 0:NE],
           [(m_gw, ch * 128, (ch + 1) * 128), (m_id, 0, NE)], [(m_ps[6], 0, NE)])
        vec(lambda e, ch=ch: e.tensor_copy(gateT[:, ch * NE:(ch + 1) * NE], PS[6][0:128, 0:NE]),
            [(m_ps[6], 0, NE)], [(m_gT, ch * NE, (ch + 1) * NE)])

    WM = 6 * D
    for kc in range(KC):
        sl = kc % 2
        dma("gpsimd", f"wm{sl}", arena_a[:, sl * WM:(sl + 1) * WM], dwmod[kc * 128:(kc + 1) * 128, :],
            [], [(m_a, sl * WM, (sl + 1) * WM)])
        for mc in range(MODC):
            mm(PS[7][:, mc * 2:mc * 2 + 2],
               arena_a[:, sl * WM + mc * 128: sl * WM + (mc + 1) * 128],
               csilu[:, kc * 2:kc * 2 + 2],
               [(m_a, sl * WM + mc * 128, sl * WM + (mc + 1) * 128), (m_cs, kc * 2, kc * 2 + 2)],
               [(m_ps[7], mc * 2, mc * 2 + 2)],
               start=(kc == 0 and mc == 0), stop=(kc == KC - 1), skip=True)
    CA0, SA0, GA0, CB0, SB0, GB0 = (NK, 0, 2 * NK, 4 * NK, 3 * NK, 5 * NK)
    vec(lambda e: e.tensor_scalar_add(mods_fm[:, 0:NK], PS[7][:, CA0:CA0 + NK], 1.0),
        [(m_ps[7], CA0, CA0 + NK)], [(m_mods, 0, NK)])
    vec(lambda e: e.tensor_copy(mods_fm[:, NK:2 * NK], PS[7][:, SA0:SA0 + NK]),
        [(m_ps[7], SA0, SA0 + NK)], [(m_mods, NK, 2 * NK)])
    vec(lambda e: e.tensor_scalar_add(mods_fm[:, 2 * NK:3 * NK], PS[7][:, CB0:CB0 + NK], 1.0),
        [(m_ps[7], CB0, CB0 + NK)], [(m_mods, 2 * NK, 3 * NK)])
    vec(lambda e: e.tensor_copy(mods_fm[:, 3 * NK:4 * NK], PS[7][:, SB0:SB0 + NK]),
        [(m_ps[7], SB0, SB0 + NK)], [(m_mods, 3 * NK, 4 * NK)])
    vec(lambda e: e.reciprocal(rout[:, NK:2 * NK], mods_fm[:, 0:NK]),
        [(m_mods, 0, NK)], [(m_rt, NK, 2 * NK)])
    vec(lambda e: e.tensor_tensor(rout[:, 0:NK], mods_fm[:, NK:2 * NK],
                                  rout[:, NK:2 * NK], Alu.mult),
        [(m_mods, NK, 2 * NK), (m_rt, NK, 2 * NK)], [(m_rt, 0, NK)])
    vec(lambda e: e.tensor_copy(sap_bf[:, 0:NK], rout[:, 0:NK]),
        [(m_rt, 0, NK)], [(m_sap, 0, NK)])
    vec(lambda e: e.reciprocal(rout[:, NK:2 * NK], mods_fm[:, 2 * NK:3 * NK]),
        [(m_mods, 2 * NK, 3 * NK)], [(m_rt, NK, 2 * NK)])
    vec(lambda e: e.tensor_tensor(rout[:, 0:NK], mods_fm[:, 3 * NK:4 * NK],
                                  rout[:, NK:2 * NK], Alu.mult),
        [(m_mods, 3 * NK, 4 * NK), (m_rt, NK, 2 * NK)], [(m_rt, 0, NK)])
    vec(lambda e: e.tensor_copy(sap_bf[:, NK:2 * NK], rout[:, 0:NK]),
        [(m_rt, 0, NK)], [(m_sap, NK, 2 * NK)])
    for i, off in enumerate((GA0, GB0)):
        vec(lambda e, off=off: e.tensor_scalar(rout[:, 0:NK], PS[7][:, off:off + NK],
                                               0.0, None, Alu.is_ge),
            [(m_ps[7], off, off + NK)], [(m_rt, 0, NK)])
        vec(lambda e: e.tensor_scalar(rout[:, 0:NK], rout[:, 0:NK], 2.0, -1.0,
                                      Alu.mult, Alu.add),
            [(m_rt, 0, NK)], [(m_rt, 0, NK)])
        vec(lambda e, off=off: e.tensor_scalar_mul(rout[:, NK:2 * NK], PS[7][:, off:off + NK], -1.0),
            [(m_ps[7], off, off + NK)], [(m_rt, NK, 2 * NK)])
        vec(lambda e, off=off: e.tensor_tensor(rout[:, NK:2 * NK], PS[7][:, off:off + NK],
                                               rout[:, NK:2 * NK], Alu.max),
            [(m_ps[7], off, off + NK), (m_rt, NK, 2 * NK)], [(m_rt, NK, 2 * NK)])
        vec(lambda e: e.tensor_scalar_max(rout[:, NK:2 * NK], rout[:, NK:2 * NK], 0.25),
            [(m_rt, NK, 2 * NK)], [(m_rt, NK, 2 * NK)])
        vec(lambda e, i=i: e.tensor_tensor(ga_fm[:, i * NK:(i + 1) * NK], rout[:, 0:NK],
                                           rout[:, NK:2 * NK], Alu.mult),
            [(m_rt, 0, 2 * NK)], [(m_gafm, i * NK, (i + 1) * NK)])
    for s in range(SEQS):
        for ch in range(KC):
            mm(PS[6][0:1, 0:128], ga_fm[:, ch * 2 + s:ch * 2 + s + 1], id_bf[:, :],
               [(m_gafm, ch * 2 + s, ch * 2 + s + 1), (m_id, *WHOLE)], [(m_ps[6], 0, 128)])
            vec(lambda e, s=s, ch=ch: e.tensor_copy(
                    rows[0:1, R_GA + s * D + ch * 128:R_GA + s * D + (ch + 1) * 128],
                    PS[6][0:1, 0:128]),
                [(m_ps[6], 0, 128)], [(m_rows, R_GA + s * D + ch * 128, R_GA + s * D + (ch + 1) * 128)])

    if dbg:
        dump_bf(dbg_mods, mods_fm, m_mods, 4 * NK)

    BNW = min(512, D)
    NBN = D // BNW

    def ln_tile(x_ap, mx, xlo, hn_ap, mhn_lo):
        for i in range(NBN):
            vec(lambda e, i=i: e.bn_stats(rout[:, i * 6:(i + 1) * 6],
                                          x_ap[:, i * BNW:(i + 1) * BNW]),
                [(mx, xlo + i * BNW, xlo + (i + 1) * BNW)], [(m_rt, i * 6, (i + 1) * 6)],
                drain=True)
        mcol = 6 * NBN
        vec(lambda e: e.bn_aggr(rout[:, mcol:mcol + 2], rout[:, 0:6 * NBN]),
            [(m_rt, 0, 6 * NBN)], [(m_rt, mcol, mcol + 2)], drain=True)
        sca(lambda e: e.activation(smalls[:, 2:3], rout[:, mcol + 1:mcol + 2], Act.Sqrt,
                                   bias=smalls[:, 6:7]),
            [(m_rt, mcol + 1, mcol + 2), (m_sm, 6, 7)], [(m_sm, 2, 3)])
        vec(lambda e: e.reciprocal(smalls[:, 3:4], smalls[:, 2:3]),
            [(m_sm, 2, 3)], [(m_sm, 3, 4)])
        vec(lambda e: e.tensor_scalar_add(smalls[:, 4:5], rout[:, mcol + 1:mcol + 2], cfg.LN_EPS),
            [(m_rt, mcol + 1, mcol + 2)], [(m_sm, 4, 5)])
        vec(lambda e: e.tensor_tensor(smalls[:, 5:6], smalls[:, 3:4], smalls[:, 3:4], Alu.mult),
            [(m_sm, 3, 4)], [(m_sm, 5, 6)])
        vec(lambda e: e.tensor_tensor(smalls[:, 5:6], smalls[:, 5:6], smalls[:, 4:5], Alu.mult),
            [(m_sm, 4, 6)], [(m_sm, 5, 6)])
        vec(lambda e: e.tensor_scalar(smalls[:, 5:6], smalls[:, 5:6], -0.5, 1.5,
                                      Alu.mult, Alu.add),
            [(m_sm, 5, 6)], [(m_sm, 5, 6)])
        vec(lambda e: e.tensor_tensor(smalls[:, 3:4], smalls[:, 3:4], smalls[:, 5:6], Alu.mult),
            [(m_sm, 3, 6)], [(m_sm, 3, 4)])
        vec(lambda e: e.tensor_scalar(hn_ap, x_ap, rout[:, mcol:mcol + 1], smalls[:, 3:4],
                                      Alu.subtract, Alu.mult),
            [(mx, xlo, xlo + D), (m_rt, mcol, mcol + 1), (m_sm, 3, 4)],
            [(m_hn, mhn_lo, mhn_lo + D)])

    stopped = [False]
    do_gb = True
    for s in range(SEQS):
        if stop_after == "setup":
            stopped[0] = True
            break
        for half, rbase in ((0, R_SAP), (1, R_SBP)):
            for ch in range(KC):
                col = half * NK + ch * 2 + s
                mm(PS[6][0:1, 0:128], sap_bf[:, col:col + 1], id_bf[:, :],
                   [(m_sap, col, col + 1), (m_id, *WHOLE)], [(m_ps[6], 0, 128)])
                vec(lambda e, rbase=rbase, ch=ch: e.tensor_copy(
                        rows[0:1, rbase + ch * 128:rbase + (ch + 1) * 128], PS[6][0:1, 0:128]),
                    [(m_ps[6], 0, 128)], [(m_rows, rbase + ch * 128, rbase + (ch + 1) * 128)])
        for t in range(T128):
            sl = t % 2
            xg = arena_g[:, sl * D:(sl + 1) * D]
            dma("sync", f"x{sl}", xg, dx[s, t * 128:(t + 1) * 128, :], [],
                [(m_g, sl * D, (sl + 1) * D)])
            hn = hn_ring[:, sl * D:(sl + 1) * D]
            ln_tile(xg, m_g, sl * D, hn, sl * D)
            for ch in range(KC):
                pb = 2 + (ch % 2)
                mm(PS[pb][:, 0:128], hn[:, ch * 128:(ch + 1) * 128], id_bf[:, :],
                   [(m_hn, sl * D + ch * 128, sl * D + (ch + 1) * 128), (m_id, *WHOLE)],
                   [(m_ps[pb], 0, 128)], start=True, stop=False)
                mm(PS[pb][:, 0:128], rows[0:1, R_SAP + ch * 128:R_SAP + (ch + 1) * 128],
                   ones_r[0:1, 0:128],
                   [(m_rows, R_SAP + ch * 128, R_SAP + (ch + 1) * 128), (m_onr, *WHOLE)],
                   [(m_ps[pb], 0, 128)], start=False, stop=True)
                sca(lambda e, ch=ch, t=t, pb=pb, s=s: e.activation(
                        arena_b[:, ch * S + t * 128:ch * S + (t + 1) * 128],
                        PS[pb][:, 0:128], Act.Copy,
                        scale=mods_fm[:, ch * 2 + s:ch * 2 + s + 1]),
                    [(m_ps[pb], 0, 128), (m_mods, 0, 2 * NK)],
                    [(m_b, ch * S + t * 128, ch * S + (t + 1) * 128)])

        if dbg and s == 0:
            dump_bf(dbg_h, arena_b, m_b, KC * S)
        if stop_after == "ln1":
            stopped[0] = True
            break
        for tb in range(T128):
            vec(lambda e, tb=tb: e.memset(arena_d[:, D_VA + tb * 65 + 64:D_VA + tb * 65 + 65], 1.0),
                [], [(m_d, D_VA + tb * 65 + 64, D_VA + tb * 65 + 65)])
            vec(lambda e, tb=tb: e.memset(arena_d[:, D_VB + tb * 65 + 64:D_VB + tb * 65 + 65], 1.0),
                [], [(m_d, D_VB + tb * 65 + 64, D_VB + tb * 65 + 65)])
        WIN = KC * 384
        for pr in range(NPAIR):
            wsl = pr % 2
            win_lo = wsl * WIN
            dma("gpsimd", f"wi{wsl}",
                arena_a[:, win_lo:win_lo + WIN].rearrange("p (a c) -> p a c", a=KC),
                dwin[:, pr, :].rearrange("(a p) c -> p a c", p=128),
                [], [(m_a, win_lo, win_lo + WIN)])
            win = arena_a[:, win_lo:win_lo + WIN]

            for part in range(2):
                off = part * 128
                dst0 = part * S
                pb = part
                for nt in range(T512):
                    for kc in range(KC):
                        mm(PS[pb][:, 0:512],
                           win[:, kc * 384 + off:kc * 384 + off + 128],
                           arena_b[:, kc * S + nt * 512:kc * S + (nt + 1) * 512],
                           [(m_a, win_lo + kc * 384 + off, win_lo + kc * 384 + off + 128),
                            (m_b, kc * S + nt * 512, kc * S + (nt + 1) * 512)],
                           [(m_ps[pb], 0, 512)],
                           start=(kc == 0), stop=False)
                    bcol = part * D + pr * 128
                    mm(PS[pb][:, 0:512], b_in_row[0:1, bcol:bcol + 128], ones_r[0:1, 0:512],
                       [(m_bir, bcol, bcol + 128), (m_onr, *WHOLE)],
                       [(m_ps[pb], 0, 512)], start=False, stop=True)
                    sca(lambda e, nt=nt, pb=pb, dst0=dst0, scf=(0.125 if part == 0 else 1.0):
                            e.activation(arena_e[:, dst0 + nt * 512:dst0 + (nt + 1) * 512],
                                         PS[pb][:, 0:512], Act.Copy, scale=scf),
                        [(m_ps[pb], 0, 512)],
                        [(m_e, dst0 + nt * 512, dst0 + (nt + 1) * 512)])
            for tb in range(T128):
                pb = 2 + (tb % 2)
                for kc in range(KC):
                    mm(PS[pb][:, 0:128],
                       arena_b[:, kc * S + tb * 128:kc * S + (tb + 1) * 128],
                       win[:, kc * 384 + 256:kc * 384 + 384],
                       [(m_b, kc * S + tb * 128, kc * S + (tb + 1) * 128),
                        (m_a, win_lo + kc * 384 + 256, win_lo + kc * 384 + 384)],
                       [(m_ps[pb], 0, 128)],
                       start=(kc == 0), stop=(kc == KC - 1))
                vec(lambda e, tb=tb, pb=pb, pr=pr: e.tensor_tensor(
                        arena_d[:, D_VA + tb * 65:D_VA + tb * 65 + 64],
                        PS[pb][:, 0:64], bcast[:, pr * 128:pr * 128 + 64], Alu.add),
                    [(m_ps[pb], 0, 64), (m_bc, pr * 128, pr * 128 + 64)],
                    [(m_d, D_VA + tb * 65, D_VA + tb * 65 + 64)])
                vec(lambda e, tb=tb, pb=pb, pr=pr: e.tensor_tensor(
                        arena_d[:, D_VB + tb * 65:D_VB + tb * 65 + 64],
                        PS[pb][:, 64:128], bcast[:, pr * 128 + 64:pr * 128 + 128], Alu.add),
                    [(m_ps[pb], 64, 128), (m_bc, pr * 128 + 64, pr * 128 + 128)],
                    [(m_d, D_VB + tb * 65, D_VB + tb * 65 + 64)])

            for side in range(2):
                src0 = side * S
                for nt in range(T512):
                    qsl = nt % 2
                    sca(lambda e, nt=nt, qsl=qsl, src0=src0: e.activation(
                            arena_d[:, D_SQ + qsl * 512:D_SQ + (qsl + 1) * 512],
                            arena_e[:, src0 + nt * 512:src0 + (nt + 1) * 512], Act.Square),
                        [(m_e, src0 + nt * 512, src0 + (nt + 1) * 512)],
                        [(m_d, D_SQ + qsl * 512, D_SQ + (qsl + 1) * 512)])
                    for hh in range(2):
                        pb = 4 + hh
                        mm(PS[pb][0:1, 0:512],
                           ones_c[hh * 64:(hh + 1) * 64, 0:1],
                           arena_d[hh * 64:(hh + 1) * 64, D_SQ + qsl * 512:D_SQ + (qsl + 1) * 512],
                           [(m_onc, *WHOLE), (m_d, D_SQ + qsl * 512, D_SQ + (qsl + 1) * 512)],
                           [(m_ps[pb], 0, 512)],
                           start=(nt == 0), stop=(nt == T512 - 1), skip=True)
                for hh in range(2):
                    pb = 4 + hh
                    vec(lambda e, side=side, hh=hh, pb=pb: e.reduce_max(
                            brow[0:1, side * 2 + hh:side * 2 + hh + 1], PS[pb][0:1, 0:512], Ax.X),
                        [(m_ps[pb], 0, 512)], [(m_br, side * 2 + hh, side * 2 + hh + 1)],
                        drain=True)
            for hh in range(2):
                vec(lambda e, hh=hh: e.tensor_tensor(
                        brow[0:1, 4 + hh:5 + hh], brow[0:1, hh:hh + 1],
                        brow[0:1, 2 + hh:3 + hh], Alu.add),
                    [(m_br, 0, 4)], [(m_br, 4 + hh, 5 + hh)])
                vec(lambda e, hh=hh: e.tensor_scalar(
                        brow[0:1, 4 + hh:5 + hh], brow[0:1, 4 + hh:5 + hh],
                        -1.0 / 16.0, cfg.SHIFT_CLAMP, Alu.mult, Alu.max),
                    [(m_br, 4 + hh, 5 + hh)], [(m_br, 4 + hh, 5 + hh)])
            vec(lambda e: e.tensor_copy(brow_bf[0:1, 0:2], brow[0:1, 4:6]),
                [(m_br, 4, 6)], [(m_brb, 0, 2)])

            for hh in range(2):
                qrow = arena_e[hh * 64:(hh + 1) * 64, 0:S]
                krow = arena_e[hh * 64:(hh + 1) * 64, S:2 * S]
                voff = D_VA if hh == 0 else D_VB
                mm(PS[6][:, 0:1], ones_r[0:1, 0:128], brow_bf[0:1, hh:hh + 1],
                   [(m_onr, *WHOLE), (m_brb, hh, hh + 1)], [(m_ps[6], 0, 1)])
                vec(lambda e: e.tensor_copy(smalls[:, 0:1], PS[6][:, 0:1]),
                    [(m_ps[6], 0, 1)], [(m_sm, 0, 1)])

                for qt in range(T512):
                    nk = 4 * qt + 4

                    def emit_st(kblk):
                        sl = kblk % 2
                        off = max(0, kblk * 128 - qt * 512)
                        mm(PS[sl][:, off:512],
                           krow[:, kblk * 128:(kblk + 1) * 128],
                           qrow[:, qt * 512 + off:(qt + 1) * 512],
                           [(m_e, S + kblk * 128, S + (kblk + 1) * 128),
                            (m_e, qt * 512 + off, (qt + 1) * 512)],
                           [(m_ps[sl], 0, 512)])
                        if kblk >= 4 * qt:
                            vec(lambda e, sl=sl, off=off: e.tensor_tensor(
                                    PS[sl][:, off:off + 128], PS[sl][:, off:off + 128],
                                    mask_t[:, :], Alu.add),
                                [(m_mask, *WHOLE), (m_ps[sl], off, off + 128)],
                                [(m_ps[sl], off, off + 128)])
                        sca(lambda e, sl=sl, off=off: e.activation(
                                arena_d[:, D_PT + sl * 512 + off:D_PT + (sl + 1) * 512],
                                PS[sl][:, off:512], Act.Exp, bias=smalls[:, 0:1]),
                            [(m_ps[sl], off, 512), (m_sm, 0, 1)],
                            [(m_d, D_PT + sl * 512 + off, D_PT + (sl + 1) * 512)])

                    def emit_pv(kblk):
                        sl = kblk % 2
                        for qb in range(4):
                            if 4 * qt + qb < kblk:
                                continue
                            pb = 2 + qb
                            mm(PS[pb][0:128, 0:65],
                               arena_d[:, D_PT + sl * 512 + qb * 128:D_PT + sl * 512 + (qb + 1) * 128],
                               arena_d[:, voff + kblk * 65:voff + (kblk + 1) * 65],
                               [(m_d, D_PT + sl * 512 + qb * 128, D_PT + sl * 512 + (qb + 1) * 128),
                                (m_d, voff + kblk * 65, voff + (kblk + 1) * 65)],
                               [(m_ps[pb], 0, 65)],
                               start=(kblk == 0), stop=(kblk == 4 * qt + qb), skip=True)

                    emit_st(0)
                    for kblk in range(1, nk):
                        emit_st(kblk)
                        emit_pv(kblk - 1)
                    emit_pv(nk - 1)
                    for qb in range(4):
                        pb = 2 + qb
                        vec(lambda e, pb=pb: e.reciprocal(smalls[:, 1:2], PS[pb][:, 64:65]),
                            [(m_ps[pb], 64, 65)], [(m_sm, 1, 2)])
                        sca(lambda e, pb=pb, qt=qt, qb=qb, hh=hh: e.activation(
                                arena_d[:, D_AT + (qt * 4 + qb) * 128 + hh * 64:
                                        D_AT + (qt * 4 + qb) * 128 + hh * 64 + 64],
                                PS[pb][:, 0:64], Act.Copy, scale=smalls[:, 1:2]),
                            [(m_ps[pb], 0, 64), (m_sm, 1, 2)],
                            [(m_d, D_AT + (qt * 4 + qb) * 128 + hh * 64,
                              D_AT + (qt * 4 + qb) * 128 + hh * 64 + 64)])
            if dbg and s == 0 and pr == 0:
                dma("gpsimd", "dbgk", dbg_bir[0:1, 0:2 * D], b_in_row[0:1, :],
                    [(m_bir, 0, 2 * D)], [(m_dbg, 0, 1)])
                dma("gpsimd", "dbgk", dbg_bir[0:1, 2 * D:2 * D + 512], ones_r[0:1, :],
                    [(m_onr, 0, 512)], [(m_dbg, 0, 1)])
                dma("gpsimd", "dbgk", dbg_bir[0:1, 2 * D + 512:2 * D + 512 + D], bvrow[0:1, :],
                    [(m_bv, 0, D)], [(m_dbg, 0, 1)])
                dump_bf(dbg_qk, arena_e, m_e, 2 * S)
                dump_bf(dbg_v, arena_d, m_d, 4224)
                for col in range(0, T128 * 128, 2048):
                    dma("gpsimd", "dbgk", dbg_at[:, col:col + 2048],
                        arena_d[:, D_AT + col:D_AT + col + 2048],
                        [(m_d, D_AT + col, D_AT + col + 2048)], [(m_dbg, 0, 1)])
                dma("gpsimd", "dbgk", dbg_nc[:, 0:4], smalls[:, 0:4],
                    [(m_sm, 0, 4)], [(m_dbg, 0, 1)])
            for qb in range(T128):
                pb = 6 + (qb % 2)
                mm(PS[pb][:, 0:128], arena_d[:, D_AT + qb * 128:D_AT + (qb + 1) * 128],
                   id_bf[:, :],
                   [(m_d, D_AT + qb * 128, D_AT + (qb + 1) * 128), (m_id, *WHOLE)],
                   [(m_ps[pb], 0, 128)])
                vec(lambda e, qb=qb, pb=pb, pr=pr: e.tensor_copy(
                        arena_c[:, pr * S + qb * 128:pr * S + (qb + 1) * 128], PS[pb][:, 0:128]),
                    [(m_ps[pb], 0, 128)],
                    [(m_c, pr * S + qb * 128, pr * S + (qb + 1) * 128)])

        if dbg and s == 0:
            dump_bf(dbg_attn, arena_c, m_c, KC * S)
        if stop_after == "attn":
            stopped[0] = True
            break
        dma("gpsimd", "wo", arena_a[:, 0:KC * D].rearrange("p (a c) -> p a c", a=KC),
            dwout.rearrange("(a p) c -> p a c", p=128),
            [], [(m_a, 0, KC * D)])
        vec(lambda e, s=s: e.tensor_tensor(rows[0:1, R_GAB:R_GAB + D],
                                           rows[0:1, R_GA + s * D:R_GA + (s + 1) * D],
                                           borow[0:1, :], Alu.mult),
            [(m_rows, R_GA + s * D, R_GA + (s + 1) * D), (m_bo, *WHOLE)],
            [(m_rows, R_GAB, R_GAB + D)])
        for n in range(NOT):
            mm(PS[6][:, 0:OW], ones_r[0:1, 0:128],
               rows[0:1, R_GA + s * D + n * OW:R_GA + s * D + (n + 1) * OW],
               [(m_onr, *WHOLE), (m_rows, R_GA + s * D + n * OW, R_GA + s * D + (n + 1) * OW)],
               [(m_ps[6], 0, OW)])
            vec(lambda e, n=n: e.tensor_copy(bcast[:, D + n * OW:D + (n + 1) * OW],
                                             PS[6][:, 0:OW]),
                [(m_ps[6], 0, OW)], [(m_bc, D + n * OW, D + (n + 1) * OW)])
        for kc in range(KC):
            vec(lambda e, kc=kc: e.tensor_tensor(
                    arena_d[:, kc * D:(kc + 1) * D],
                    arena_a[:, kc * D:(kc + 1) * D],
                    bcast[:, D:2 * D], Alu.mult),
                [(m_a, kc * D, (kc + 1) * D), (m_bc, D, 2 * D)],
                [(m_d, kc * D, (kc + 1) * D)])

        for t in range(T128):
            sl = t % 2
            xg = arena_g[:, sl * D:(sl + 1) * D]
            dma("sync", f"x{sl}", xg, dx[s, t * 128:(t + 1) * 128, :], [],
                [(m_g, sl * D, (sl + 1) * D)])
            for n in range(NOT):
                pb = n % 2
                for kc in range(KC):
                    mm(PS[pb][:, 0:OW],
                       arena_c[:, kc * S + t * 128:kc * S + (t + 1) * 128],
                       arena_d[:, kc * D + n * OW:kc * D + (n + 1) * OW],
                       [(m_c, kc * S + t * 128, kc * S + (t + 1) * 128),
                        (m_d, kc * D + n * OW, kc * D + (n + 1) * OW)],
                       [(m_ps[pb], 0, OW)],
                       start=(kc == 0), stop=False)
                mm(PS[pb][:, 0:OW], ones_r[:, 0:128],
                   rows[0:1, R_GAB + n * OW:R_GAB + (n + 1) * OW],
                   [(m_onr, *WHOLE), (m_rows, R_GAB + n * OW, R_GAB + (n + 1) * OW)],
                   [(m_ps[pb], 0, OW)], start=False, stop=True)
            xb = arena_g[:, (2 + sl) * D:(3 + sl) * D]
            for n in range(NOT):
                vec(lambda e, n=n, xb=xb, xg=xg: e.tensor_tensor(
                        xb[:, n * OW:(n + 1) * OW], PS[n % 2][:, 0:OW],
                        xg[:, n * OW:(n + 1) * OW], Alu.add),
                    [(m_ps[n % 2], 0, OW), (m_g, sl * D + n * OW, sl * D + (n + 1) * OW)],
                    [(m_g, (2 + sl) * D + n * OW, (2 + sl) * D + (n + 1) * OW)])
            dma("sync", f"sp{sl}", dxb[s, t * 128:(t + 1) * 128, :], xb,
                [(m_g, (2 + sl) * D, (3 + sl) * D)],
                [(m_xbs, s * S + t * 128, s * S + (t + 1) * 128)])
            hn = hn_ring[:, sl * D:(sl + 1) * D]
            ln_tile(xb, m_g, (2 + sl) * D, hn, sl * D)
            for ch in range(KC):
                pb = 2 + (ch % 2)
                mm(PS[pb][:, 0:128], hn[:, ch * 128:(ch + 1) * 128], id_bf[:, :],
                   [(m_hn, sl * D + ch * 128, sl * D + (ch + 1) * 128), (m_id, *WHOLE)],
                   [(m_ps[pb], 0, 128)], start=True, stop=False)
                mm(PS[pb][:, 0:128], rows[0:1, R_SBP + ch * 128:R_SBP + (ch + 1) * 128],
                   ones_r[0:1, 0:128],
                   [(m_rows, R_SBP + ch * 128, R_SBP + (ch + 1) * 128), (m_onr, *WHOLE)],
                   [(m_ps[pb], 0, 128)], start=False, stop=True)
                sca(lambda e, ch=ch, t=t, pb=pb, s=s: e.activation(
                        arena_b[:, ch * S + t * 128:ch * S + (t + 1) * 128],
                        PS[pb][:, 0:128], Act.Copy,
                        scale=mods_fm[:, 2 * NK + ch * 2 + s:2 * NK + ch * 2 + s + 1]),
                    [(m_ps[pb], 0, 128), (m_mods, 2 * NK, 4 * NK)],
                    [(m_b, ch * S + t * 128, ch * S + (t + 1) * 128)])

        if stop_after == "proj":
            stopped[0] = True
            break
        for t in range(T128):
            for kc in range(KC):
                mm(PS[4][:, 0:NE],
                   arena_b[:, kc * S + t * 128:kc * S + (t + 1) * 128],
                   gateT[:, kc * NE:(kc + 1) * NE],
                   [(m_b, kc * S + t * 128, kc * S + (t + 1) * 128), (m_gT, kc * NE, (kc + 1) * NE)],
                   [(m_ps[4], 0, NE)],
                   start=(kc == 0), stop=(kc == KC - 1))
            routing_tile(p, cfg, PS[4], m_ps[4], rout, m_rt, sc_bf, m_scb, vec, sca)
            mm(PS[5][0:NE, 0:128], sc_bf[:, 0:NE], id_bf[:, :],
               [(m_scb, 0, NE), (m_id, *WHOLE)], [(m_ps[5], 0, 128)])
            vec(lambda e, t=t: e.tensor_copy(c_fm[0:NE, t * 128:(t + 1) * 128],
                                             PS[5][0:NE, 0:128]),
                [(m_ps[5], 0, 128)], [(m_cfm, t * 128, (t + 1) * 128)])

        if dbg and s == 0:
            dump_bf(dbg_cfm, c_fm, m_cfm, S)
        if stop_after == "gate":
            stopped[0] = True
            break
        do_gb = stop_after != "moe_nogb"
        for ch in (range(KC) if do_gb else ()):
            mm(PS[6][0:1, 0:128], ga_fm[:, NK + ch * 2 + s:NK + ch * 2 + s + 1], id_bf[:, :],
               [(m_gafm, NK + ch * 2 + s, NK + ch * 2 + s + 1), (m_id, *WHOLE)],
               [(m_ps[6], 0, 128)])
            vec(lambda e, ch=ch: e.tensor_copy(rows[0:1, R_GB + ch * 128:R_GB + (ch + 1) * 128],
                                               PS[6][0:1, 0:128]),
                [(m_ps[6], 0, 128)], [(m_rows, R_GB + ch * 128, R_GB + (ch + 1) * 128)])
        for n in (range(NOT) if do_gb else ()):
            mm(PS[6][:, 0:OW], ones_r[0:1, 0:128], rows[0:1, R_GB + n * OW:R_GB + (n + 1) * OW],
               [(m_onr, *WHOLE), (m_rows, R_GB + n * OW, R_GB + (n + 1) * OW)],
               [(m_ps[6], 0, OW)])
            vec(lambda e, n=n: e.tensor_copy(bcast[:, D + n * OW:D + (n + 1) * OW],
                                             PS[6][:, 0:OW]),
                [(m_ps[6], 0, OW)], [(m_bc, D + n * OW, D + (n + 1) * OW)])

        moe_groups = NGRP
        if stop_after in ("moe_dma", "moe_g0t0", "moe_g0", "moe_nogb"):
            moe_groups = 1
        for g in range(moe_groups):
            shared = (g == NGRP - 1)
            if not shared:
                e0 = 2 * g
                for ei in range(2):
                    wiv = [(m_a, A_WA + kc * GWD + ei * INTER, A_WA + kc * GWD + (ei + 1) * INTER)
                           for kc in range(KC)]
                    dma("gpsimd", "wa",
                        arena_a[:, A_WA:A_WA + WAB].rearrange(
                            "p (a e c) -> p a e c", a=KC, e=2)[:, :, ei, :],
                        dw1[e0 + ei].rearrange("(a p) c -> p a c", p=128),
                        [], wiv)
                    wiv = [(m_a, A_WB + kc * GWD + ei * INTER, A_WB + kc * GWD + (ei + 1) * INTER)
                           for kc in range(KC)]
                    dma("gpsimd", "wb",
                        arena_a[:, A_WB:A_WB + WAB].rearrange(
                            "p (a e c) -> p a e c", a=KC, e=2)[:, :, ei, :],
                        dw3[e0 + ei].rearrange("(a p) c -> p a c", p=128),
                        [], wiv)
                    dma("gpsimd", "wc",
                        arena_a[:, A_WC:A_WC + WAB].rearrange(
                            "p (a c) -> p a c", a=ICG)[:, ei * (ICG // 2):(ei + 1) * (ICG // 2), :],
                        dw2[e0 + ei].rearrange("(a p) c -> p a c", p=128),
                        [], [(m_a, A_WC + ei * (ICG // 2) * D, A_WC + (ei + 1) * (ICG // 2) * D)])
            else:
                dma("gpsimd", "wa",
                    arena_a[:, A_WA:A_WA + WAB].rearrange("p (a c) -> p a c", a=KC),
                    dws1.rearrange("(a p) c -> p a c", p=128),
                    [], [(m_a, A_WA, A_WA + WAB)])
                dma("gpsimd", "wb",
                    arena_a[:, A_WB:A_WB + WAB].rearrange("p (a c) -> p a c", a=KC),
                    dws3.rearrange("(a p) c -> p a c", p=128),
                    [], [(m_a, A_WB, A_WB + WAB)])
                dma("gpsimd", "wc",
                    arena_a[:, A_WC:A_WC + WAB].rearrange("p (a c) -> p a c", a=ICG),
                    dws2.rearrange("(a p) c -> p a c", p=128),
                    [], [(m_a, A_WC, A_WC + WAB)])
            for ic in (range(ICG) if do_gb else ()):
                vec(lambda e, ic=ic: e.tensor_tensor(
                        arena_a[:, A_WC + ic * D:A_WC + (ic + 1) * D],
                        arena_a[:, A_WC + ic * D:A_WC + (ic + 1) * D],
                        bcast[:, D:2 * D], Alu.mult),
                    [(m_a, A_WC + ic * D, A_WC + (ic + 1) * D), (m_bc, D, 2 * D)],
                    [(m_a, A_WC + ic * D, A_WC + (ic + 1) * D)])

            if stop_after in ("moe_dma", "moe_nogb"):
                break
            for t in range(T512):
                if stop_after == "moe_g0t0" and t > 0:
                    break
                asl = t % 2
                A0 = asl * ICG * 512
                if not shared:
                    for ei in range(2):
                        mm(PS[6][:, 0:512],
                           sel16[:, (2 * g + ei) * 128:(2 * g + ei + 1) * 128],
                           c_fm[0:NE, t * 512:(t + 1) * 512],
                           [(m_sel, (2 * g + ei) * 128, (2 * g + ei + 1) * 128),
                            (m_cfm, t * 512, (t + 1) * 512)],
                           [(m_ps[6], 0, 512)])
                        vec(lambda e, asl=asl, ei=ei: e.tensor_copy(
                                cb_ring[:, asl * 1024 + ei * 512:asl * 1024 + (ei + 1) * 512],
                                PS[6][:, 0:512]),
                            [(m_ps[6], 0, 512)],
                            [(m_cb, asl * 1024 + ei * 512, asl * 1024 + (ei + 1) * 512)])
                for ic in range(ICG):
                    p1b, p3b = ic % 2, 2 + ic % 2
                    for kc in range(KC):
                        mm(PS[p1b][:, 0:512],
                           arena_a[:, A_WA + kc * GWD + ic * 128:A_WA + kc * GWD + (ic + 1) * 128],
                           arena_b[:, kc * S + t * 512:kc * S + (t + 1) * 512],
                           [(m_a, A_WA + kc * GWD + ic * 128, A_WA + kc * GWD + (ic + 1) * 128),
                            (m_b, kc * S + t * 512, kc * S + (t + 1) * 512)],
                           [(m_ps[p1b], 0, 512)],
                           start=(kc == 0), stop=(kc == KC - 1))
                    for kc in range(KC):
                        mm(PS[p3b][:, 0:512],
                           arena_a[:, A_WB + kc * GWD + ic * 128:A_WB + kc * GWD + (ic + 1) * 128],
                           arena_b[:, kc * S + t * 512:kc * S + (t + 1) * 512],
                           [(m_a, A_WB + kc * GWD + ic * 128, A_WB + kc * GWD + (ic + 1) * 128),
                            (m_b, kc * S + t * 512, kc * S + (t + 1) * 512)],
                           [(m_ps[p3b], 0, 512)],
                           start=(kc == 0), stop=(kc == KC - 1))
                    tsl = ic % 2
                    t1 = moe_tmp[:, tsl * 512:(tsl + 1) * 512]
                    t3 = moe_tmp[:, 1024 + tsl * 512:1024 + (tsl + 1) * 512]
                    if shared:
                        vec(lambda e, t3=t3, p3b=p3b: e.tensor_copy(t3, PS[p3b][:, 0:512]),
                            [(m_ps[p3b], 0, 512)], [(m_mt, 1024 + tsl * 512, 1024 + (tsl + 1) * 512)])
                    else:
                        ei = ic // (ICG // 2)
                        vec(lambda e, t3=t3, p3b=p3b, asl=asl, ei=ei: e.tensor_tensor(
                                t3, PS[p3b][:, 0:512],
                                cb_ring[:, asl * 1024 + ei * 512:asl * 1024 + (ei + 1) * 512],
                                Alu.mult),
                            [(m_ps[p3b], 0, 512),
                             (m_cb, asl * 1024 + ei * 512, asl * 1024 + (ei + 1) * 512)],
                            [(m_mt, 1024 + tsl * 512, 1024 + (tsl + 1) * 512)])
                    sca(lambda e, t1=t1, p1b=p1b: e.activation(t1, PS[p1b][:, 0:512], Act.Silu),
                        [(m_ps[p1b], 0, 512)], [(m_mt, tsl * 512, (tsl + 1) * 512)])
                    vec(lambda e, t1=t1, t3=t3, ic=ic, A0=A0: e.tensor_tensor(
                            arena_d[:, A0 + ic * 512:A0 + (ic + 1) * 512], t1, t3, Alu.mult),
                        [(m_mt, tsl * 512, (tsl + 1) * 512),
                         (m_mt, 1024 + tsl * 512, 1024 + (tsl + 1) * 512)],
                        [(m_d, A0 + ic * 512, A0 + (ic + 1) * 512)])
                for mtile in range(4):
                    for n in range(NOT):
                        pb = 4 + ((mtile * NOT + n) % 2)
                        for ic in range(ICG):
                            mm(PS[pb][:, 0:OW],
                               arena_d[:, A0 + ic * 512 + mtile * 128:A0 + ic * 512 + (mtile + 1) * 128],
                               arena_a[:, A_WC + ic * D + n * OW:A_WC + ic * D + (n + 1) * OW],
                               [(m_d, A0 + ic * 512 + mtile * 128, A0 + ic * 512 + (mtile + 1) * 128),
                                (m_a, A_WC + ic * D + n * OW, A_WC + ic * D + (n + 1) * OW)],
                               [(m_ps[pb], 0, OW)],
                               start=(ic == 0), stop=(ic == ICG - 1))
                        ycol = (t * 4 + mtile) * D + n * OW
                        if g == 0:
                            vec(lambda e, pb=pb, ycol=ycol: e.tensor_copy(
                                    arena_c[:, ycol:ycol + OW], PS[pb][:, 0:OW]),
                                [(m_ps[pb], 0, OW)], [(m_c, ycol, ycol + OW)])
                        else:
                            vec(lambda e, pb=pb, ycol=ycol: e.tensor_tensor(
                                    arena_c[:, ycol:ycol + OW], PS[pb][:, 0:OW],
                                    arena_c[:, ycol:ycol + OW], Alu.add),
                                [(m_ps[pb], 0, OW), (m_c, ycol, ycol + OW)],
                                [(m_c, ycol, ycol + OW)])

        if dbg and s == 0:
            dump_bf(dbg_y, arena_c, m_c, KC * S)
        if stop_after in ("moe", "moe_dma", "moe_g0t0", "moe_g0", "moe_nogb"):
            stopped[0] = True
            break
        for t in range(T128):
            sl = t % 2
            xg = arena_g[:, sl * D:(sl + 1) * D]
            dma("sync", f"x{sl}", xg, dxb[s, t * 128:(t + 1) * 128, :],
                [(m_xbs, s * S + t * 128, s * S + (t + 1) * 128)],
                [(m_g, sl * D, (sl + 1) * D)])
            vec(lambda e, t=t, sl=sl, xg=xg: e.tensor_tensor(
                    arena_g[:, (2 + sl) * D:(3 + sl) * D], xg,
                    arena_c[:, t * D:(t + 1) * D], Alu.add),
                [(m_g, sl * D, (sl + 1) * D), (m_c, t * D, (t + 1) * D)],
                [(m_g, (2 + sl) * D, (3 + sl) * D)])
            dma("sync", "st", dout[s, t * 128:(t + 1) * 128, :],
                arena_g[:, (2 + sl) * D:(3 + sl) * D],
                [(m_g, (2 + sl) * D, (3 + sl) * D)],
                [(m_out, s * S + t * 128, s * S + (t + 1) * 128)])

    if not stopped[0]:
        p.do("sync", None, r=[(m_out, 0, SEQS * S)])

    plans, sem_names = p.finalize()

    sems = {}
    for sn in sem_names:
        g = nc.semaphore(f"sem_{sn}")
        sems[sn] = g.__enter__()
        ctx.append(g)

    def mk(e):
        def run(eng):
            for waits, emit, inc, inc_sem, drain in plans[e]:
                for (sn, v) in waits:
                    eng.wait_ge(sems[sn], v)
                if emit is not None:
                    inst = emit(eng)
                    if inc:
                        inst.then_inc(sems[inc_sem], inc)
                    if drain:
                        eng.drain()
        return run

    blk = nc.Block()
    block = blk.__enter__()
    block.tensor(mk("tensor"))
    block.vector(mk("vector"))
    block.scalar(mk("scalar"))
    block.sync(mk("sync"))
    block.gpsimd(mk("gpsimd"))
    blk.__exit__(None, None, None)

    for g in reversed(ctx):
        g.__exit__(None, None, None)
    return nc


def routing_tile(p, cfg, psl, m_psl, rout, m_rt, sc_bf, m_scb, vec, sca):
    NE = cfg.NE
    G0, W0, E0, S0, M0, T0 = 0, 4, 20, 36, 52, 68
    vec(lambda e: e.reduce_max(rout[:, T0:T0 + 1], psl[:, 0:NE], Ax.X, negate=True),
        [(m_psl, 0, NE)], [(m_rt, T0, T0 + 1)], drain=True)
    sca(lambda e: e.activation(rout[:, E0:E0 + NE], psl[:, 0:NE], Act.Exp,
                               bias=rout[:, T0:T0 + 1]),
        [(m_psl, 0, NE), (m_rt, T0, T0 + 1)], [(m_rt, E0, E0 + NE)])
    vec(lambda e: e.reduce_sum(rout[:, T0 + 1:T0 + 2], rout[:, E0:E0 + NE], Ax.X),
        [(m_rt, E0, E0 + NE)], [(m_rt, T0 + 1, T0 + 2)], drain=True)
    vec(lambda e: e.reciprocal(rout[:, T0 + 2:T0 + 3], rout[:, T0 + 1:T0 + 2]),
        [(m_rt, T0 + 1, T0 + 2)], [(m_rt, T0 + 2, T0 + 3)])
    vec(lambda e: e.tensor_scalar_mul(rout[:, S0:S0 + NE], rout[:, E0:E0 + NE],
                                      rout[:, T0 + 2:T0 + 3]),
        [(m_rt, E0, E0 + NE), (m_rt, T0 + 2, T0 + 3)], [(m_rt, S0, S0 + NE)])
    for r in range(4):
        vec(lambda e, r=r: e.reduce_max(rout[:, G0 + r:G0 + r + 1],
                                        rout[:, S0 + 4 * r:S0 + 4 * r + 4], Ax.X),
            [(m_rt, S0 + 4 * r, S0 + 4 * r + 4)], [(m_rt, G0 + r, G0 + r + 1)], drain=True)
    vec(lambda e: e.reduce_max(rout[:, T0 + 3:T0 + 4], rout[:, G0:G0 + 4], Ax.X),
        [(m_rt, G0, G0 + 4)], [(m_rt, T0 + 3, T0 + 4)], drain=True)
    vec(lambda e: e.tensor_scalar(rout[:, W0:W0 + 4], rout[:, G0:G0 + 4],
                                  rout[:, T0 + 3:T0 + 4], None, Alu.is_ge),
        [(m_rt, G0, G0 + 4), (m_rt, T0 + 3, T0 + 4)], [(m_rt, W0, W0 + 4)])
    vec(lambda e: e.scalar_tensor_tensor(rout[:, W0:W0 + 4], rout[:, W0:W0 + 4], KNOCK_NEG,
                                         rout[:, G0:G0 + 4], Alu.mult, Alu.add),
        [(m_rt, W0, W0 + 4), (m_rt, G0, G0 + 4)], [(m_rt, W0, W0 + 4)])
    vec(lambda e: e.reduce_max(rout[:, T0 + 4:T0 + 5], rout[:, W0:W0 + 4], Ax.X),
        [(m_rt, W0, W0 + 4)], [(m_rt, T0 + 4, T0 + 5)], drain=True)
    vec(lambda e: e.tensor_scalar(rout[:, G0:G0 + 4], rout[:, G0:G0 + 4],
                                  rout[:, T0 + 4:T0 + 5], None, Alu.is_ge),
        [(m_rt, G0, G0 + 4), (m_rt, T0 + 4, T0 + 5)], [(m_rt, G0, G0 + 4)])
    for r in range(4):
        vec(lambda e, r=r: e.tensor_scalar_mul(rout[:, M0 + 4 * r:M0 + 4 * r + 4],
                                               rout[:, S0 + 4 * r:S0 + 4 * r + 4],
                                               rout[:, G0 + r:G0 + r + 1]),
            [(m_rt, S0 + 4 * r, S0 + 4 * r + 4), (m_rt, G0 + r, G0 + r + 1)],
            [(m_rt, M0 + 4 * r, M0 + 4 * r + 4)])
    vec(lambda e: e.tensor_copy(rout[:, W0:W0 + NE], rout[:, M0:M0 + NE]),
        [(m_rt, M0, M0 + NE)], [(m_rt, W0, W0 + NE)])
    for k in range(3):
        vec(lambda e: e.reduce_max(rout[:, T0 + 5:T0 + 6], rout[:, W0:W0 + NE], Ax.X),
            [(m_rt, W0, W0 + NE)], [(m_rt, T0 + 5, T0 + 6)], drain=True)
        vec(lambda e: e.tensor_scalar(rout[:, E0:E0 + NE], rout[:, W0:W0 + NE],
                                      rout[:, T0 + 5:T0 + 6], None, Alu.is_ge),
            [(m_rt, W0, W0 + NE), (m_rt, T0 + 5, T0 + 6)], [(m_rt, E0, E0 + NE)])
        vec(lambda e: e.scalar_tensor_tensor(rout[:, W0:W0 + NE], rout[:, E0:E0 + NE], KNOCK_NEG,
                                             rout[:, W0:W0 + NE], Alu.mult, Alu.add),
            [(m_rt, E0, E0 + NE), (m_rt, W0, W0 + NE)], [(m_rt, W0, W0 + NE)])
    vec(lambda e: e.reduce_max(rout[:, T0 + 5:T0 + 6], rout[:, W0:W0 + NE], Ax.X),
        [(m_rt, W0, W0 + NE)], [(m_rt, T0 + 5, T0 + 6)], drain=True)
    vec(lambda e: e.tensor_scalar(rout[:, E0:E0 + NE], rout[:, M0:M0 + NE],
                                  rout[:, T0 + 5:T0 + 6], None, Alu.is_ge),
        [(m_rt, M0, M0 + NE), (m_rt, T0 + 5, T0 + 6)], [(m_rt, E0, E0 + NE)])
    vec(lambda e: e.tensor_tensor(sc_bf[:, 0:NE], rout[:, S0:S0 + NE],
                                  rout[:, E0:E0 + NE], Alu.mult),
        [(m_rt, S0, S0 + NE), (m_rt, E0, E0 + NE)], [(m_scb, 0, NE)])



def mk_consts(cfg):
    NE = cfg.NE
    id128 = np.eye(128, dtype=ml_dtypes.bfloat16)
    i = np.arange(128)
    mask = np.where(i[None, :] >= i[:, None], 0.0, MASK_NEG).astype(np.float32)
    sel = np.zeros((NE, NE * 128), dtype=ml_dtypes.bfloat16)
    for e in range(NE):
        sel[e, e * 128:(e + 1) * 128] = 1.0
    return id128, mask, sel


def make_inputs(cfg, **a):
    D = cfg.D
    x = np.ascontiguousarray(np.asarray(a["x"], dtype=np.float32))
    w_in = np.asarray(a["w_in"], dtype=np.float32)
    wq = w_in[:, 0:D].reshape(D, cfg.NPAIR, 128)
    wk = w_in[:, D:2 * D].reshape(D, cfg.NPAIR, 128)
    wv = w_in[:, 2 * D:3 * D].reshape(D, cfg.NPAIR, 128)
    w_in_re = np.ascontiguousarray(np.concatenate([wq, wk, wv], axis=2))
    id128, mask, sel = mk_consts(cfg)
    shared = dict(
        w_in_re=w_in_re,
        b_in=np.ascontiguousarray(np.asarray(a["b_in"], dtype=np.float32)),
        w_out=np.ascontiguousarray(np.asarray(a["w_out"], dtype=np.float32)),
        b_out=np.ascontiguousarray(np.asarray(a["b_out"], dtype=np.float32)),
        gate_w=np.ascontiguousarray(np.asarray(a["gate_w"], dtype=np.float32)),
        w1=np.ascontiguousarray(np.asarray(a["w1"], dtype=np.float32)),
        w2=np.ascontiguousarray(np.asarray(a["w2"], dtype=np.float32)),
        w3=np.ascontiguousarray(np.asarray(a["w3"], dtype=np.float32)),
        ws1=np.ascontiguousarray(np.asarray(a["ws1"], dtype=np.float32)),
        ws2=np.ascontiguousarray(np.asarray(a["ws2"], dtype=np.float32)),
        ws3=np.ascontiguousarray(np.asarray(a["ws3"], dtype=np.float32)),
        wmod=np.ascontiguousarray(np.asarray(a["wmod"], dtype=np.float32)),
        id128d=id128, mask_td=mask, sel16d=sel,
    )
    cvec = np.asarray(a["cvec"], dtype=np.float32)
    return x, cvec, shared


def kernel(L0_attn_x, L0_attn_w_in, L0_attn_b_in, L0_attn_w_out, L0_attn_b_out,
           L0_ffn_gate_w, L0_ffn_w1, L0_ffn_w2, L0_ffn_w3, L0_ffn_ws1, L0_ffn_ws2,
           L0_ffn_ws3, L0_c, L0_wmod):
    cfg = CFG()
    x, cvec, shared = make_inputs(
        cfg, x=L0_attn_x, w_in=L0_attn_w_in, b_in=L0_attn_b_in, w_out=L0_attn_w_out,
        b_out=L0_attn_b_out, gate_w=L0_ffn_gate_w, w1=L0_ffn_w1, w2=L0_ffn_w2,
        w3=L0_ffn_w3, ws1=L0_ffn_ws1, ws2=L0_ffn_ws2, ws3=L0_ffn_ws3, cvec=L0_c,
        wmod=L0_wmod)
    nc = build_program(cfg, paranoid={"vector"})
    in_maps = []
    for c in range(NCORES):
        m = dict(shared)
        m["x"] = np.ascontiguousarray(x[c * cfg.SEQS:(c + 1) * cfg.SEQS])
        m["cvec"] = np.ascontiguousarray(cvec[c * cfg.SEQS:(c + 1) * cfg.SEQS])
        in_maps.append(m)
    res = run_bass_kernel_spmd(nc, in_maps, core_ids=list(range(NCORES)))
    out = np.concatenate([res.results[c]["out"] for c in range(NCORES)], axis=0)
    return np.ascontiguousarray(out.astype(np.float32))


# version 09
# speedup vs baseline: 2.1839x; 1.0386x over previous
import os

import numpy as np
import ml_dtypes

import concourse.bass as bass
import concourse.mybir as mybir
from concourse.bass_utils import run_bass_kernel_spmd

F32 = mybir.dt.float32
BF16 = mybir.dt.bfloat16
Alu = mybir.AluOpType
Act = mybir.ActivationFunctionType
Ax = mybir.AxisListType

NCORES = 8
MASK_NEG = -30000.0
KNOCK_NEG = -1e30


class CFG:
    SEQS = 2
    S = 2048
    D = 1024
    H = 16
    NE = 16
    INTER = 512
    LN_EPS = 1e-5
    SHIFT_CLAMP = -60.0

    HD = property(lambda s: s.D // s.H)
    NPAIR = property(lambda s: s.H // 2)
    KC = property(lambda s: s.D // 128)
    T128 = property(lambda s: s.S // 128)
    T512 = property(lambda s: s.S // 512)
    SHARED = property(lambda s: 2 * s.INTER)
    GW = property(lambda s: 2 * s.INTER)
    ICG = property(lambda s: s.GW // 128)
    NGRP = property(lambda s: s.NE // 2 + 1)
    MODC = property(lambda s: 6 * s.D // 128)



ENGINES = ("tensor", "vector", "scalar", "sync", "gpsimd")
DMA_ENGINES = frozenset(("sync", "gpsimd"))


class Tok:
    __slots__ = ("eng", "idx", "gid")

    def __init__(self, eng, idx, gid):
        self.eng = eng
        self.idx = idx
        self.gid = gid


class Mem:

    __slots__ = ("name", "writers", "readers", "coarse")

    def __init__(self, name, coarse=False):
        self.name = name
        self.writers = []
        self.readers = []
        self.coarse = coarse

    def on_read(self, lo, hi, tok, deps):
        if self.coarse:
            lo, hi = 0, 1 << 30
        for (l, h, t) in self.writers:
            if l < hi and lo < h:
                deps.append(t)
        rd = self.readers
        keep = [e for e in rd if not (e[2].eng == tok.eng and lo <= e[0] and e[1] <= hi)]
        keep.append([lo, hi, tok])
        self.readers = keep

    def on_write(self, lo, hi, tok, deps):
        if self.coarse:
            lo, hi = 0, 1 << 30
        for (l, h, t) in self.writers:
            if l < hi and lo < h:
                deps.append(t)
        for (l, h, t) in self.readers:
            if l < hi and lo < h:
                deps.append(t)
        self.writers = [e for e in self.writers if not (lo <= e[0] and e[1] <= hi)]
        self.readers = [e for e in self.readers if not (lo <= e[0] and e[1] <= hi)]
        self.writers.append([lo, hi, tok])


class Prog:

    def __init__(self, paranoid=False):
        if paranoid is True:
            paranoid = set(ENGINES)
        elif not paranoid:
            paranoid = set()
        self.paranoid = paranoid
        self.streams = {e: [] for e in ENGINES}
        self.keys = []
        self.gid = 0

    def do(self, eng, emit, r=(), w=(), key=None, drain=False):
        if eng in DMA_ENGINES and emit is not None:
            assert key is not None, "DMA op requires a semaphore key"
            if key not in self.keys:
                self.keys.append(key)
        self.gid += 1
        tok = Tok(eng, len(self.streams[eng]), self.gid)
        deps = []
        for (m, lo, hi) in r:
            m.on_read(lo, hi, tok, deps)
        for (m, lo, hi) in w:
            m.on_write(lo, hi, tok, deps)
        deps = [d for d in deps if d is not tok]
        self.streams[eng].append([emit, deps, False, key, tok.gid, drain])
        return tok

    def finalize(self):
        import bisect
        for e in ENGINES:
            for op in self.streams[e]:
                if e in DMA_ENGINES and op[0] is not None:
                    op[2] = True
                for d in op[1]:
                    de = d.eng
                    if de in DMA_ENGINES:
                        continue
                    if de != e or de in self.paranoid:
                        self.streams[de][d.idx][2] = True
        key_gids = {k: [] for k in self.keys}
        for e in DMA_ENGINES:
            for op in self.streams[e]:
                if op[0] is not None:
                    key_gids[op[3]].append(op[4])
        for k in key_gids:
            key_gids[k].sort()
        ordinals = {e: [] for e in ENGINES}
        cnt_e = {e: 0 for e in ENGINES}
        for e in ENGINES:
            for op in self.streams[e]:
                if e not in DMA_ENGINES and op[2]:
                    cnt_e[e] += 1
                ordinals[e].append(cnt_e[e])
        plans = {}
        for e in ENGINES:
            plan = []
            last_wait = {}
            for emit, deps, marked, key, gid, drain in self.streams[e]:
                need = {}
                for d in deps:
                    de = d.eng
                    if de in DMA_ENGINES:
                        dk = self.streams[de][d.idx][3]
                        v = bisect.bisect_left(key_gids[dk], gid)
                        if v > 0 and v > need.get(dk, 0):
                            need[dk] = v
                    else:
                        if de == e and de not in self.paranoid:
                            continue
                        v = ordinals[de][d.idx]
                        if v > 0 and v > need.get(de, 0):
                            need[de] = v
                waits = []
                for sn, v in need.items():
                    if last_wait.get(sn, 0) < v:
                        waits.append((sn, v * (16 if sn in key_gids else 1)))
                        last_wait[sn] = v
                inc_sem = key if e in DMA_ENGINES else e
                inc = (16 if e in DMA_ENGINES else 1) if marked else 0
                plan.append((waits, emit, inc, inc_sem, drain))
            plans[e] = plan
        return plans, [e for e in ENGINES if e not in DMA_ENGINES] + self.keys



def build_program(cfg, paranoid=False, stop_after=None, dbg=False):
    nc = bass.Bass("TRN2", target_bir_lowering=False,
                   detect_race_conditions=bool(int(os.environ.get("KBASS_RACED", "0"))),
                   dynamic_dma_scratch_size=16384)
    SEQS, S, D, NE, INTER = cfg.SEQS, cfg.S, cfg.D, cfg.NE, cfg.INTER
    KC, NPAIR, T128, T512 = cfg.KC, cfg.NPAIR, cfg.T128, cfg.T512
    ICG, NGRP, MODC = cfg.ICG, cfg.NGRP, cfg.MODC
    NK = 2 * KC
    OW = min(512, D)
    NOT = D // OW

    dx = nc.dram_tensor("x", [SEQS, S, D], F32, kind="ExternalInput").ap()
    dwin = nc.dram_tensor("w_in_re", [D, NPAIR, 384], F32, kind="ExternalInput").ap()
    dbin = nc.dram_tensor("b_in", [3 * D], F32, kind="ExternalInput").ap()
    dwout = nc.dram_tensor("w_out", [D, D], F32, kind="ExternalInput").ap()
    dbout = nc.dram_tensor("b_out", [D], F32, kind="ExternalInput").ap()
    dgate = nc.dram_tensor("gate_w", [NE, D], F32, kind="ExternalInput").ap()
    dw1 = nc.dram_tensor("w1", [NE, D, INTER], F32, kind="ExternalInput").ap()
    dw2 = nc.dram_tensor("w2", [NE, INTER, D], F32, kind="ExternalInput").ap()
    dw3 = nc.dram_tensor("w3", [NE, D, INTER], F32, kind="ExternalInput").ap()
    dws1 = nc.dram_tensor("ws1", [D, cfg.SHARED], F32, kind="ExternalInput").ap()
    dws2 = nc.dram_tensor("ws2", [cfg.SHARED, D], F32, kind="ExternalInput").ap()
    dws3 = nc.dram_tensor("ws3", [D, cfg.SHARED], F32, kind="ExternalInput").ap()
    dc = nc.dram_tensor("cvec", [SEQS, D], F32, kind="ExternalInput").ap()
    dwmod = nc.dram_tensor("wmod", [D, 6 * D], F32, kind="ExternalInput").ap()
    did = nc.dram_tensor("id128d", [128, 128], BF16, kind="ExternalInput").ap()
    dmask = nc.dram_tensor("mask_td", [128, 128], F32, kind="ExternalInput").ap()
    dsel = nc.dram_tensor("sel16d", [NE, NE * 128], BF16, kind="ExternalInput").ap()
    dout = nc.dram_tensor("out", [SEQS, S, D], F32, kind="ExternalOutput").ap()
    dxb = nc.dram_tensor("xb_scr", [SEQS, S, D], F32,
                         kind="ExternalOutput" if dbg else "Internal").ap()
    if dbg:
        dbg_mods = nc.dram_tensor("dbg_mods", [128, 4 * NK], F32, kind="ExternalOutput").ap()
        dbg_h = nc.dram_tensor("dbg_h", [128, KC * S], F32, kind="ExternalOutput").ap()
        dbg_attn = nc.dram_tensor("dbg_attn", [128, KC * S], F32, kind="ExternalOutput").ap()
        dbg_cfm = nc.dram_tensor("dbg_cfm", [NE, S], F32, kind="ExternalOutput").ap()
        dbg_y = nc.dram_tensor("dbg_y", [128, KC * S], F32, kind="ExternalOutput").ap()
        dbg_f32 = nc.dram_tensor("dbg_f32", [128, KC * S], F32, kind="Internal").ap()
        dbg_qk = nc.dram_tensor("dbg_qk", [128, 2 * S], F32, kind="ExternalOutput").ap()
        dbg_v = nc.dram_tensor("dbg_v", [128, 4224], F32, kind="ExternalOutput").ap()
        dbg_at = nc.dram_tensor("dbg_at", [128, 2048], F32, kind="ExternalOutput").ap()
        dbg_nc = nc.dram_tensor("dbg_nc", [128, 4], F32, kind="ExternalOutput").ap()
        dbg_bir = nc.dram_tensor("dbg_bir", [1, 2 * D + 512 + D], F32, kind="ExternalOutput").ap()
        m_dbg = Mem("dbg")

        def dump_bf(dst, src_ap, mem, width):
            for col in range(0, width, 4096):
                wdt = min(4096, width - col)
                dma("gpsimd", "dbgk", dst[:, col:col + wdt], src_ap[:, col:col + wdt],
                    [(mem, col, col + wdt)], [(m_dbg, 0, 1)])

    p = Prog(paranoid=paranoid)
    ctx = []

    def sb(name, shape, dtype):
        g = nc.sbuf_tensor(name, shape, dtype)
        t = g.__enter__()
        ctx.append(g)
        return t, Mem(name)

    AW = max(3 * KC * cfg.GW, 2 * KC * 384, 2 * 6 * D, KC * D)
    arena_a, m_a = sb("arena_a", [128, AW], BF16)
    GWD = cfg.GW
    assert GWD == D
    WAB = KC * GWD
    A_WA, A_WB, A_WC = 0, WAB, 2 * WAB
    arena_b, m_b = sb("arena_b", [128, KC * S], BF16)
    arena_c, m_c = sb("arena_c", [128, KC * S], BF16)
    D_PT, D_SQ = 0, 1024
    D_VA, D_VB = 2048, 2048 + T128 * 65
    D_AT = 2048 + 2 * T128 * 65
    DW = max(2 * ICG * 512, D_AT + T128 * 128, KC * 1024)
    arena_d, m_d = sb("arena_d", [128, DW], BF16)
    arena_e, m_e = sb("arena_e", [128, 2 * S], BF16)
    arena_g, m_g = sb("arena_g", [128, 4 * D], F32)
    hn_ring, m_hn = sb("hn_ring", [128, 2 * D], BF16)
    moe_tmp, m_mt = sb("moe_tmp", [128, 2048], BF16)
    cb_ring, m_cb = sb("cb_ring", [128, 2048], BF16)
    c_fm, m_cfm = sb("c_fm", [NE, S], BF16)
    bcast, m_bc = sb("bcast", [128, 2 * D], BF16)
    rows, m_rows = sb("rows", [1, 6 * D], BF16)
    R_GA, R_GAB, R_GB, R_SAP, R_SBP = 0, 2 * D, 3 * D, 4 * D, 5 * D
    bvrow, m_bv = sb("bvrow", [1, D], BF16)
    borow, m_bo = sb("borow", [1, D], BF16)
    mods_fm, m_mods = sb("mods_fm", [128, 4 * NK], F32)
    ga_fm, m_gafm = sb("ga_fm", [128, 2 * NK], BF16)
    id_bf, m_id = sb("id_bf", [128, 128], BF16)
    mask_t, m_mask = sb("mask_t", [128, 128], F32)
    sel16, m_sel = sb("sel16", [NE, NE * 128], BF16)
    gw_bf, m_gw = sb("gw_bf", [NE, D], BF16)
    gateT, m_gT = sb("gateT", [128, KC * NE], BF16)
    c2row, m_c2 = sb("c2row", [2, D], F32)
    c2bf, m_c2b = sb("c2bf", [2, D], BF16)
    csilu, m_cs = sb("csilu", [128, NK], BF16)
    ones_r, m_onr = sb("ones_r", [1, 512], BF16)
    ones_c, m_onc = sb("ones_c", [128, 1], BF16)
    rout, m_rt = sb("rout", [128, 128], F32)
    sc_bf, m_scb = sb("sc_bf", [128, NE], BF16)
    smalls, m_sm = sb("smalls", [128, 16], F32)
    brow, m_br = sb("brow", [1, 16], F32)
    brow_bf, m_brb = sb("brow_bf", [1, 4], BF16)
    b_in_row, m_bir = sb("b_in_row", [1, 2 * D], BF16)
    sap_bf, m_sap = sb("sap_bf", [128, 2 * NK], BF16)

    PS = []
    m_ps = []
    for i in range(8):
        g = nc.psum_tensor(f"ps{i}", [128, 512], F32)
        PS.append(g.__enter__())
        ctx.append(g)
        m_ps.append(Mem(f"ps{i}", coarse=True))
    m_xbs = Mem("xb_scr")
    m_out = Mem("out")

    WHOLE = (0, 1 << 30)

    def mm(out_ap, lhsT_ap, rhs_ap, r, w, start=True, stop=True, skip=False):
        lsh, rsh, osh = lhsT_ap.shape, rhs_ap.shape, out_ap.shape
        assert lsh[0] == rsh[0], f"mm K mismatch {lsh} vs {rsh}"
        assert lsh[1] == osh[0], f"mm M mismatch lhsT {lsh} out {osh}"
        assert rsh[1] == osh[1], f"mm N mismatch rhs {rsh} out {osh}"
        return p.do("tensor",
                    lambda e, o=out_ap, l=lhsT_ap, rr=rhs_ap, st=start, sp=stop, sk=skip:
                        e.matmul(o, l, rr, start=st, stop=sp, skip_group_check=sk),
                    r=r, w=w)

    def vec(fn, r, w, drain=False):
        return p.do("vector", fn, r=r, w=w, drain=drain)

    def sca(fn, r, w):
        return p.do("scalar", fn, r=r, w=w)

    def dma(q, key, out_ap, in_ap, r, w):
        return p.do(q, lambda e, o=out_ap, i=in_ap: e.dma_start(o, i), r=r, w=w, key=key)

    dma("sync", "k_id", id_bf[:, :], did, [], [(m_id, *WHOLE)])
    dma("sync", "k_mask", mask_t[:, :], dmask, [], [(m_mask, *WHOLE)])
    dma("sync", "k_sel", sel16[:, :], dsel, [], [(m_sel, *WHOLE)])
    if SEQS < 2:
        vec(lambda e: e.memset(c2row[:, :], 0.0), [], [(m_c2, *WHOLE)])
    dma("sync", "k_c2", c2row[0:SEQS, :], dc[:, :], [],
        [(m_c2, 0, D if SEQS == 2 else 1)])
    dma("gpsimd", "k_bo", borow[0:1, :], dbout.rearrange("(o d) -> o d", o=1), [], [(m_bo, *WHOLE)])
    dma("gpsimd", "k_bir", b_in_row[0:1, :], dbin.rearrange("(o d) -> o d", o=1)[0:1, 0:2 * D],
        [], [(m_bir, *WHOLE)])
    dma("gpsimd", "k_bv", bvrow[0:1, :], dbin.rearrange("(o d) -> o d", o=1)[0:1, 2 * D:3 * D],
        [], [(m_bv, *WHOLE)])
    dma("gpsimd", "k_gw", gw_bf[:, :], dgate, [], [(m_gw, *WHOLE)])

    vec(lambda e: e.memset(ones_r[:, :], 1.0), [], [(m_onr, *WHOLE)])
    vec(lambda e: e.memset(smalls[:, 6:7], cfg.LN_EPS), [], [(m_sm, 6, 7)])
    vec(lambda e: e.memset(ones_c[:, :], 1.0), [], [(m_onc, *WHOLE)])



    for n in range(NOT):
        mm(PS[6][:, 0:OW], ones_r[0:1, 0:128], bvrow[0:1, n * OW:(n + 1) * OW],
           [(m_onr, *WHOLE), (m_bv, n * OW, (n + 1) * OW)], [(m_ps[6], 0, OW)])
        vec(lambda e, n=n: e.tensor_copy(bcast[:, n * OW:(n + 1) * OW], PS[6][:, 0:OW]),
            [(m_ps[6], 0, OW)], [(m_bc, n * OW, (n + 1) * OW)])

    sca(lambda e: e.activation(c2row[0:2, :], c2row[0:2, :], Act.Silu),
        [(m_c2, *WHOLE)], [(m_c2, *WHOLE)])
    vec(lambda e: e.tensor_copy(c2bf[0:2, :], c2row[0:2, :]),
        [(m_c2, *WHOLE)], [(m_c2b, *WHOLE)])
    for ch in range(KC):
        mm(PS[6][0:128, 0:2], c2bf[0:2, ch * 128:(ch + 1) * 128], id_bf[0:2, 0:2],
           [(m_c2b, ch * 128, (ch + 1) * 128), (m_id, 0, 2)], [(m_ps[6], 0, 2)])
        vec(lambda e, ch=ch: e.tensor_copy(csilu[:, ch * 2:ch * 2 + 2], PS[6][0:128, 0:2]),
            [(m_ps[6], 0, 2)], [(m_cs, ch * 2, ch * 2 + 2)])

    for ch in range(KC):
        mm(PS[6][0:128, 0:NE], gw_bf[0:NE, ch * 128:(ch + 1) * 128], id_bf[0:NE, 0:NE],
           [(m_gw, ch * 128, (ch + 1) * 128), (m_id, 0, NE)], [(m_ps[6], 0, NE)])
        vec(lambda e, ch=ch: e.tensor_copy(gateT[:, ch * NE:(ch + 1) * NE], PS[6][0:128, 0:NE]),
            [(m_ps[6], 0, NE)], [(m_gT, ch * NE, (ch + 1) * NE)])

    WM = 6 * D
    for kc in range(KC):
        sl = kc % 2
        dma("gpsimd", f"wm{sl}", arena_a[:, sl * WM:(sl + 1) * WM], dwmod[kc * 128:(kc + 1) * 128, :],
            [], [(m_a, sl * WM, (sl + 1) * WM)])
        for mc in range(MODC):
            mm(PS[7][:, mc * 2:mc * 2 + 2],
               arena_a[:, sl * WM + mc * 128: sl * WM + (mc + 1) * 128],
               csilu[:, kc * 2:kc * 2 + 2],
               [(m_a, sl * WM + mc * 128, sl * WM + (mc + 1) * 128), (m_cs, kc * 2, kc * 2 + 2)],
               [(m_ps[7], mc * 2, mc * 2 + 2)],
               start=(kc == 0 and mc == 0), stop=(kc == KC - 1), skip=True)
    CA0, SA0, GA0, CB0, SB0, GB0 = (NK, 0, 2 * NK, 4 * NK, 3 * NK, 5 * NK)
    vec(lambda e: e.tensor_scalar_add(mods_fm[:, 0:NK], PS[7][:, CA0:CA0 + NK], 1.0),
        [(m_ps[7], CA0, CA0 + NK)], [(m_mods, 0, NK)])
    vec(lambda e: e.tensor_copy(mods_fm[:, NK:2 * NK], PS[7][:, SA0:SA0 + NK]),
        [(m_ps[7], SA0, SA0 + NK)], [(m_mods, NK, 2 * NK)])
    vec(lambda e: e.tensor_scalar_add(mods_fm[:, 2 * NK:3 * NK], PS[7][:, CB0:CB0 + NK], 1.0),
        [(m_ps[7], CB0, CB0 + NK)], [(m_mods, 2 * NK, 3 * NK)])
    vec(lambda e: e.tensor_copy(mods_fm[:, 3 * NK:4 * NK], PS[7][:, SB0:SB0 + NK]),
        [(m_ps[7], SB0, SB0 + NK)], [(m_mods, 3 * NK, 4 * NK)])
    vec(lambda e: e.reciprocal(rout[:, NK:2 * NK], mods_fm[:, 0:NK]),
        [(m_mods, 0, NK)], [(m_rt, NK, 2 * NK)])
    vec(lambda e: e.tensor_tensor(rout[:, 0:NK], mods_fm[:, NK:2 * NK],
                                  rout[:, NK:2 * NK], Alu.mult),
        [(m_mods, NK, 2 * NK), (m_rt, NK, 2 * NK)], [(m_rt, 0, NK)])
    vec(lambda e: e.tensor_copy(sap_bf[:, 0:NK], rout[:, 0:NK]),
        [(m_rt, 0, NK)], [(m_sap, 0, NK)])
    vec(lambda e: e.reciprocal(rout[:, NK:2 * NK], mods_fm[:, 2 * NK:3 * NK]),
        [(m_mods, 2 * NK, 3 * NK)], [(m_rt, NK, 2 * NK)])
    vec(lambda e: e.tensor_tensor(rout[:, 0:NK], mods_fm[:, 3 * NK:4 * NK],
                                  rout[:, NK:2 * NK], Alu.mult),
        [(m_mods, 3 * NK, 4 * NK), (m_rt, NK, 2 * NK)], [(m_rt, 0, NK)])
    vec(lambda e: e.tensor_copy(sap_bf[:, NK:2 * NK], rout[:, 0:NK]),
        [(m_rt, 0, NK)], [(m_sap, NK, 2 * NK)])
    for i, off in enumerate((GA0, GB0)):
        vec(lambda e, off=off: e.tensor_scalar(rout[:, 0:NK], PS[7][:, off:off + NK],
                                               0.0, None, Alu.is_ge),
            [(m_ps[7], off, off + NK)], [(m_rt, 0, NK)])
        vec(lambda e: e.tensor_scalar(rout[:, 0:NK], rout[:, 0:NK], 2.0, -1.0,
                                      Alu.mult, Alu.add),
            [(m_rt, 0, NK)], [(m_rt, 0, NK)])
        vec(lambda e, off=off: e.tensor_scalar_mul(rout[:, NK:2 * NK], PS[7][:, off:off + NK], -1.0),
            [(m_ps[7], off, off + NK)], [(m_rt, NK, 2 * NK)])
        vec(lambda e, off=off: e.tensor_tensor(rout[:, NK:2 * NK], PS[7][:, off:off + NK],
                                               rout[:, NK:2 * NK], Alu.max),
            [(m_ps[7], off, off + NK), (m_rt, NK, 2 * NK)], [(m_rt, NK, 2 * NK)])
        vec(lambda e: e.tensor_scalar_max(rout[:, NK:2 * NK], rout[:, NK:2 * NK], 0.25),
            [(m_rt, NK, 2 * NK)], [(m_rt, NK, 2 * NK)])
        vec(lambda e, i=i: e.tensor_tensor(ga_fm[:, i * NK:(i + 1) * NK], rout[:, 0:NK],
                                           rout[:, NK:2 * NK], Alu.mult),
            [(m_rt, 0, 2 * NK)], [(m_gafm, i * NK, (i + 1) * NK)])
    for s in range(SEQS):
        for ch in range(KC):
            mm(PS[6][0:1, 0:128], ga_fm[:, ch * 2 + s:ch * 2 + s + 1], id_bf[:, :],
               [(m_gafm, ch * 2 + s, ch * 2 + s + 1), (m_id, *WHOLE)], [(m_ps[6], 0, 128)])
            vec(lambda e, s=s, ch=ch: e.tensor_copy(
                    rows[0:1, R_GA + s * D + ch * 128:R_GA + s * D + (ch + 1) * 128],
                    PS[6][0:1, 0:128]),
                [(m_ps[6], 0, 128)], [(m_rows, R_GA + s * D + ch * 128, R_GA + s * D + (ch + 1) * 128)])

    if dbg:
        dump_bf(dbg_mods, mods_fm, m_mods, 4 * NK)

    BNW = min(512, D)
    NBN = D // BNW

    def ln_tile(x_ap, mx, xlo, hn_ap, mhn_lo):
        for i in range(NBN):
            vec(lambda e, i=i: e.bn_stats(rout[:, i * 6:(i + 1) * 6],
                                          x_ap[:, i * BNW:(i + 1) * BNW]),
                [(mx, xlo + i * BNW, xlo + (i + 1) * BNW)], [(m_rt, i * 6, (i + 1) * 6)],
                drain=True)
        mcol = 6 * NBN
        vec(lambda e: e.bn_aggr(rout[:, mcol:mcol + 2], rout[:, 0:6 * NBN]),
            [(m_rt, 0, 6 * NBN)], [(m_rt, mcol, mcol + 2)], drain=True)
        sca(lambda e: e.activation(smalls[:, 2:3], rout[:, mcol + 1:mcol + 2], Act.Sqrt,
                                   bias=smalls[:, 6:7]),
            [(m_rt, mcol + 1, mcol + 2), (m_sm, 6, 7)], [(m_sm, 2, 3)])
        vec(lambda e: e.reciprocal(smalls[:, 3:4], smalls[:, 2:3]),
            [(m_sm, 2, 3)], [(m_sm, 3, 4)])
        vec(lambda e: e.tensor_scalar_add(smalls[:, 4:5], rout[:, mcol + 1:mcol + 2], cfg.LN_EPS),
            [(m_rt, mcol + 1, mcol + 2)], [(m_sm, 4, 5)])
        vec(lambda e: e.tensor_tensor(smalls[:, 5:6], smalls[:, 3:4], smalls[:, 3:4], Alu.mult),
            [(m_sm, 3, 4)], [(m_sm, 5, 6)])
        vec(lambda e: e.tensor_tensor(smalls[:, 5:6], smalls[:, 5:6], smalls[:, 4:5], Alu.mult),
            [(m_sm, 4, 6)], [(m_sm, 5, 6)])
        vec(lambda e: e.tensor_scalar(smalls[:, 5:6], smalls[:, 5:6], -0.5, 1.5,
                                      Alu.mult, Alu.add),
            [(m_sm, 5, 6)], [(m_sm, 5, 6)])
        vec(lambda e: e.tensor_tensor(smalls[:, 3:4], smalls[:, 3:4], smalls[:, 5:6], Alu.mult),
            [(m_sm, 3, 6)], [(m_sm, 3, 4)])
        vec(lambda e: e.tensor_scalar(hn_ap, x_ap, rout[:, mcol:mcol + 1], smalls[:, 3:4],
                                      Alu.subtract, Alu.mult),
            [(mx, xlo, xlo + D), (m_rt, mcol, mcol + 1), (m_sm, 3, 4)],
            [(m_hn, mhn_lo, mhn_lo + D)])

    stopped = [False]
    do_gb = True
    for s in range(SEQS):
        if stop_after == "setup":
            stopped[0] = True
            break
        for half, rbase in ((0, R_SAP), (1, R_SBP)):
            for ch in range(KC):
                col = half * NK + ch * 2 + s
                mm(PS[6][0:1, 0:128], sap_bf[:, col:col + 1], id_bf[:, :],
                   [(m_sap, col, col + 1), (m_id, *WHOLE)], [(m_ps[6], 0, 128)])
                vec(lambda e, rbase=rbase, ch=ch: e.tensor_copy(
                        rows[0:1, rbase + ch * 128:rbase + (ch + 1) * 128], PS[6][0:1, 0:128]),
                    [(m_ps[6], 0, 128)], [(m_rows, rbase + ch * 128, rbase + (ch + 1) * 128)])
        for t in range(T128):
            sl = t % 2
            xg = arena_g[:, sl * D:(sl + 1) * D]
            dma("sync", f"x{sl}", xg, dx[s, t * 128:(t + 1) * 128, :], [],
                [(m_g, sl * D, (sl + 1) * D)])
            hn = hn_ring[:, sl * D:(sl + 1) * D]
            ln_tile(xg, m_g, sl * D, hn, sl * D)
            for ch in range(KC):
                pb = 2 + (ch % 2)
                mm(PS[pb][:, 0:128], hn[:, ch * 128:(ch + 1) * 128], id_bf[:, :],
                   [(m_hn, sl * D + ch * 128, sl * D + (ch + 1) * 128), (m_id, *WHOLE)],
                   [(m_ps[pb], 0, 128)], start=True, stop=False)
                mm(PS[pb][:, 0:128], rows[0:1, R_SAP + ch * 128:R_SAP + (ch + 1) * 128],
                   ones_r[0:1, 0:128],
                   [(m_rows, R_SAP + ch * 128, R_SAP + (ch + 1) * 128), (m_onr, *WHOLE)],
                   [(m_ps[pb], 0, 128)], start=False, stop=True)
                sca(lambda e, ch=ch, t=t, pb=pb, s=s: e.activation(
                        arena_b[:, ch * S + t * 128:ch * S + (t + 1) * 128],
                        PS[pb][:, 0:128], Act.Copy,
                        scale=mods_fm[:, ch * 2 + s:ch * 2 + s + 1]),
                    [(m_ps[pb], 0, 128), (m_mods, 0, 2 * NK)],
                    [(m_b, ch * S + t * 128, ch * S + (t + 1) * 128)])

        if dbg and s == 0:
            dump_bf(dbg_h, arena_b, m_b, KC * S)
        if stop_after == "ln1":
            stopped[0] = True
            break
        for tb in range(T128):
            vec(lambda e, tb=tb: e.memset(arena_d[:, D_VA + tb * 65 + 64:D_VA + tb * 65 + 65], 1.0),
                [], [(m_d, D_VA + tb * 65 + 64, D_VA + tb * 65 + 65)])
            vec(lambda e, tb=tb: e.memset(arena_d[:, D_VB + tb * 65 + 64:D_VB + tb * 65 + 65], 1.0),
                [], [(m_d, D_VB + tb * 65 + 64, D_VB + tb * 65 + 65)])
        WIN = KC * 384
        for pr in range(NPAIR):
            wsl = pr % 2
            win_lo = wsl * WIN
            dma("gpsimd", f"wi{wsl}",
                arena_a[:, win_lo:win_lo + WIN].rearrange("p (a c) -> p a c", a=KC),
                dwin[:, pr, :].rearrange("(a p) c -> p a c", p=128),
                [], [(m_a, win_lo, win_lo + WIN)])
            win = arena_a[:, win_lo:win_lo + WIN]

            for part in range(2):
                off = part * 128
                dst0 = part * S
                pb = part
                for nt in range(T512):
                    for kc in range(KC):
                        mm(PS[pb][:, 0:512],
                           win[:, kc * 384 + off:kc * 384 + off + 128],
                           arena_b[:, kc * S + nt * 512:kc * S + (nt + 1) * 512],
                           [(m_a, win_lo + kc * 384 + off, win_lo + kc * 384 + off + 128),
                            (m_b, kc * S + nt * 512, kc * S + (nt + 1) * 512)],
                           [(m_ps[pb], 0, 512)],
                           start=(kc == 0), stop=False)
                    bcol = part * D + pr * 128
                    mm(PS[pb][:, 0:512], b_in_row[0:1, bcol:bcol + 128], ones_r[0:1, 0:512],
                       [(m_bir, bcol, bcol + 128), (m_onr, *WHOLE)],
                       [(m_ps[pb], 0, 512)], start=False, stop=True)
                    sca(lambda e, nt=nt, pb=pb, dst0=dst0, scf=(0.125 if part == 0 else 1.0):
                            e.activation(arena_e[:, dst0 + nt * 512:dst0 + (nt + 1) * 512],
                                         PS[pb][:, 0:512], Act.Copy, scale=scf),
                        [(m_ps[pb], 0, 512)],
                        [(m_e, dst0 + nt * 512, dst0 + (nt + 1) * 512)])
            for tb in range(T128):
                pb = 2 + (tb % 2)
                for kc in range(KC):
                    mm(PS[pb][:, 0:128],
                       arena_b[:, kc * S + tb * 128:kc * S + (tb + 1) * 128],
                       win[:, kc * 384 + 256:kc * 384 + 384],
                       [(m_b, kc * S + tb * 128, kc * S + (tb + 1) * 128),
                        (m_a, win_lo + kc * 384 + 256, win_lo + kc * 384 + 384)],
                       [(m_ps[pb], 0, 128)],
                       start=(kc == 0), stop=(kc == KC - 1))
                vec(lambda e, tb=tb, pb=pb, pr=pr: e.tensor_tensor(
                        arena_d[:, D_VA + tb * 65:D_VA + tb * 65 + 64],
                        PS[pb][:, 0:64], bcast[:, pr * 128:pr * 128 + 64], Alu.add),
                    [(m_ps[pb], 0, 64), (m_bc, pr * 128, pr * 128 + 64)],
                    [(m_d, D_VA + tb * 65, D_VA + tb * 65 + 64)])
                vec(lambda e, tb=tb, pb=pb, pr=pr: e.tensor_tensor(
                        arena_d[:, D_VB + tb * 65:D_VB + tb * 65 + 64],
                        PS[pb][:, 64:128], bcast[:, pr * 128 + 64:pr * 128 + 128], Alu.add),
                    [(m_ps[pb], 64, 128), (m_bc, pr * 128 + 64, pr * 128 + 128)],
                    [(m_d, D_VB + tb * 65, D_VB + tb * 65 + 64)])

            for side in range(2):
                src0 = side * S
                for nt in range(T512):
                    qsl = nt % 2
                    sca(lambda e, nt=nt, qsl=qsl, src0=src0: e.activation(
                            arena_d[:, D_SQ + qsl * 512:D_SQ + (qsl + 1) * 512],
                            arena_e[:, src0 + nt * 512:src0 + (nt + 1) * 512], Act.Square),
                        [(m_e, src0 + nt * 512, src0 + (nt + 1) * 512)],
                        [(m_d, D_SQ + qsl * 512, D_SQ + (qsl + 1) * 512)])
                    for hh in range(2):
                        pb = 4 + hh
                        mm(PS[pb][0:1, 0:512],
                           ones_c[hh * 64:(hh + 1) * 64, 0:1],
                           arena_d[hh * 64:(hh + 1) * 64, D_SQ + qsl * 512:D_SQ + (qsl + 1) * 512],
                           [(m_onc, *WHOLE), (m_d, D_SQ + qsl * 512, D_SQ + (qsl + 1) * 512)],
                           [(m_ps[pb], 0, 512)],
                           start=(nt == 0), stop=(nt == T512 - 1), skip=True)
                for hh in range(2):
                    pb = 4 + hh
                    vec(lambda e, side=side, hh=hh, pb=pb: e.reduce_max(
                            brow[0:1, side * 2 + hh:side * 2 + hh + 1], PS[pb][0:1, 0:512], Ax.X),
                        [(m_ps[pb], 0, 512)], [(m_br, side * 2 + hh, side * 2 + hh + 1)],
                        drain=True)
            for hh in range(2):
                vec(lambda e, hh=hh: e.tensor_tensor(
                        brow[0:1, 4 + hh:5 + hh], brow[0:1, hh:hh + 1],
                        brow[0:1, 2 + hh:3 + hh], Alu.add),
                    [(m_br, 0, 4)], [(m_br, 4 + hh, 5 + hh)])
                vec(lambda e, hh=hh: e.tensor_scalar(
                        brow[0:1, 4 + hh:5 + hh], brow[0:1, 4 + hh:5 + hh],
                        -1.0 / 16.0, cfg.SHIFT_CLAMP, Alu.mult, Alu.max),
                    [(m_br, 4 + hh, 5 + hh)], [(m_br, 4 + hh, 5 + hh)])
            vec(lambda e: e.tensor_copy(brow_bf[0:1, 0:2], brow[0:1, 4:6]),
                [(m_br, 4, 6)], [(m_brb, 0, 2)])

            for hh in range(2):
                qrow = arena_e[hh * 64:(hh + 1) * 64, 0:S]
                krow = arena_e[hh * 64:(hh + 1) * 64, S:2 * S]
                voff = D_VA if hh == 0 else D_VB
                mm(PS[6][:, 0:1], ones_r[0:1, 0:128], brow_bf[0:1, hh:hh + 1],
                   [(m_onr, *WHOLE), (m_brb, hh, hh + 1)], [(m_ps[6], 0, 1)])
                vec(lambda e: e.tensor_copy(smalls[:, 0:1], PS[6][:, 0:1]),
                    [(m_ps[6], 0, 1)], [(m_sm, 0, 1)])

                for qt in range(T512):
                    nk = 4 * qt + 4

                    def emit_st(kblk):
                        sl = kblk % 2
                        off = max(0, kblk * 128 - qt * 512)
                        mm(PS[sl][:, off:512],
                           krow[:, kblk * 128:(kblk + 1) * 128],
                           qrow[:, qt * 512 + off:(qt + 1) * 512],
                           [(m_e, S + kblk * 128, S + (kblk + 1) * 128),
                            (m_e, qt * 512 + off, (qt + 1) * 512)],
                           [(m_ps[sl], 0, 512)])
                        if kblk >= 4 * qt:
                            vec(lambda e, sl=sl, off=off: e.tensor_tensor(
                                    PS[sl][:, off:off + 128], PS[sl][:, off:off + 128],
                                    mask_t[:, :], Alu.add),
                                [(m_mask, *WHOLE), (m_ps[sl], off, off + 128)],
                                [(m_ps[sl], off, off + 128)])
                        sca(lambda e, sl=sl, off=off: e.activation(
                                arena_d[:, D_PT + sl * 512 + off:D_PT + (sl + 1) * 512],
                                PS[sl][:, off:512], Act.Exp, bias=smalls[:, 0:1]),
                            [(m_ps[sl], off, 512), (m_sm, 0, 1)],
                            [(m_d, D_PT + sl * 512 + off, D_PT + (sl + 1) * 512)])

                    def emit_pv(kblk):
                        sl = kblk % 2
                        for qb in range(4):
                            if 4 * qt + qb < kblk:
                                continue
                            pb = 2 + qb
                            mm(PS[pb][0:128, 0:65],
                               arena_d[:, D_PT + sl * 512 + qb * 128:D_PT + sl * 512 + (qb + 1) * 128],
                               arena_d[:, voff + kblk * 65:voff + (kblk + 1) * 65],
                               [(m_d, D_PT + sl * 512 + qb * 128, D_PT + sl * 512 + (qb + 1) * 128),
                                (m_d, voff + kblk * 65, voff + (kblk + 1) * 65)],
                               [(m_ps[pb], 0, 65)],
                               start=(kblk == 0), stop=(kblk == 4 * qt + qb), skip=True)

                    emit_st(0)
                    for kblk in range(1, nk):
                        emit_st(kblk)
                        emit_pv(kblk - 1)
                    emit_pv(nk - 1)
                    for qb in range(4):
                        pb = 2 + qb
                        vec(lambda e, pb=pb: e.reciprocal(smalls[:, 1:2], PS[pb][:, 64:65]),
                            [(m_ps[pb], 64, 65)], [(m_sm, 1, 2)])
                        sca(lambda e, pb=pb, qt=qt, qb=qb, hh=hh: e.activation(
                                arena_d[:, D_AT + (qt * 4 + qb) * 128 + hh * 64:
                                        D_AT + (qt * 4 + qb) * 128 + hh * 64 + 64],
                                PS[pb][:, 0:64], Act.Copy, scale=smalls[:, 1:2]),
                            [(m_ps[pb], 0, 64), (m_sm, 1, 2)],
                            [(m_d, D_AT + (qt * 4 + qb) * 128 + hh * 64,
                              D_AT + (qt * 4 + qb) * 128 + hh * 64 + 64)])
            if dbg and s == 0 and pr == 0:
                dma("gpsimd", "dbgk", dbg_bir[0:1, 0:2 * D], b_in_row[0:1, :],
                    [(m_bir, 0, 2 * D)], [(m_dbg, 0, 1)])
                dma("gpsimd", "dbgk", dbg_bir[0:1, 2 * D:2 * D + 512], ones_r[0:1, :],
                    [(m_onr, 0, 512)], [(m_dbg, 0, 1)])
                dma("gpsimd", "dbgk", dbg_bir[0:1, 2 * D + 512:2 * D + 512 + D], bvrow[0:1, :],
                    [(m_bv, 0, D)], [(m_dbg, 0, 1)])
                dump_bf(dbg_qk, arena_e, m_e, 2 * S)
                dump_bf(dbg_v, arena_d, m_d, 4224)
                for col in range(0, T128 * 128, 2048):
                    dma("gpsimd", "dbgk", dbg_at[:, col:col + 2048],
                        arena_d[:, D_AT + col:D_AT + col + 2048],
                        [(m_d, D_AT + col, D_AT + col + 2048)], [(m_dbg, 0, 1)])
                dma("gpsimd", "dbgk", dbg_nc[:, 0:4], smalls[:, 0:4],
                    [(m_sm, 0, 4)], [(m_dbg, 0, 1)])
            for qb in range(T128):
                pb = 6 + (qb % 2)
                mm(PS[pb][:, 0:128], arena_d[:, D_AT + qb * 128:D_AT + (qb + 1) * 128],
                   id_bf[:, :],
                   [(m_d, D_AT + qb * 128, D_AT + (qb + 1) * 128), (m_id, *WHOLE)],
                   [(m_ps[pb], 0, 128)])
                vec(lambda e, qb=qb, pb=pb, pr=pr: e.tensor_copy(
                        arena_c[:, pr * S + qb * 128:pr * S + (qb + 1) * 128], PS[pb][:, 0:128]),
                    [(m_ps[pb], 0, 128)],
                    [(m_c, pr * S + qb * 128, pr * S + (qb + 1) * 128)])

        if dbg and s == 0:
            dump_bf(dbg_attn, arena_c, m_c, KC * S)
        if stop_after == "attn":
            stopped[0] = True
            break
        dma("gpsimd", "wo", arena_a[:, 0:KC * D].rearrange("p (a c) -> p a c", a=KC),
            dwout.rearrange("(a p) c -> p a c", p=128),
            [], [(m_a, 0, KC * D)])
        vec(lambda e, s=s: e.tensor_tensor(rows[0:1, R_GAB:R_GAB + D],
                                           rows[0:1, R_GA + s * D:R_GA + (s + 1) * D],
                                           borow[0:1, :], Alu.mult),
            [(m_rows, R_GA + s * D, R_GA + (s + 1) * D), (m_bo, *WHOLE)],
            [(m_rows, R_GAB, R_GAB + D)])
        for n in range(NOT):
            mm(PS[6][:, 0:OW], ones_r[0:1, 0:128],
               rows[0:1, R_GA + s * D + n * OW:R_GA + s * D + (n + 1) * OW],
               [(m_onr, *WHOLE), (m_rows, R_GA + s * D + n * OW, R_GA + s * D + (n + 1) * OW)],
               [(m_ps[6], 0, OW)])
            vec(lambda e, n=n: e.tensor_copy(bcast[:, D + n * OW:D + (n + 1) * OW],
                                             PS[6][:, 0:OW]),
                [(m_ps[6], 0, OW)], [(m_bc, D + n * OW, D + (n + 1) * OW)])
        for kc in range(KC):
            vec(lambda e, kc=kc: e.tensor_tensor(
                    arena_d[:, kc * D:(kc + 1) * D],
                    arena_a[:, kc * D:(kc + 1) * D],
                    bcast[:, D:2 * D], Alu.mult),
                [(m_a, kc * D, (kc + 1) * D), (m_bc, D, 2 * D)],
                [(m_d, kc * D, (kc + 1) * D)])

        for t in range(T128):
            sl = t % 2
            xg = arena_g[:, sl * D:(sl + 1) * D]
            dma("sync", f"x{sl}", xg, dx[s, t * 128:(t + 1) * 128, :], [],
                [(m_g, sl * D, (sl + 1) * D)])
            for n in range(NOT):
                pb = n % 2
                for kc in range(KC):
                    mm(PS[pb][:, 0:OW],
                       arena_c[:, kc * S + t * 128:kc * S + (t + 1) * 128],
                       arena_d[:, kc * D + n * OW:kc * D + (n + 1) * OW],
                       [(m_c, kc * S + t * 128, kc * S + (t + 1) * 128),
                        (m_d, kc * D + n * OW, kc * D + (n + 1) * OW)],
                       [(m_ps[pb], 0, OW)],
                       start=(kc == 0), stop=False)
                mm(PS[pb][:, 0:OW], ones_r[:, 0:128],
                   rows[0:1, R_GAB + n * OW:R_GAB + (n + 1) * OW],
                   [(m_onr, *WHOLE), (m_rows, R_GAB + n * OW, R_GAB + (n + 1) * OW)],
                   [(m_ps[pb], 0, OW)], start=False, stop=True)
            xb = arena_g[:, (2 + sl) * D:(3 + sl) * D]
            for n in range(NOT):
                vec(lambda e, n=n, xb=xb, xg=xg: e.tensor_tensor(
                        xb[:, n * OW:(n + 1) * OW], PS[n % 2][:, 0:OW],
                        xg[:, n * OW:(n + 1) * OW], Alu.add),
                    [(m_ps[n % 2], 0, OW), (m_g, sl * D + n * OW, sl * D + (n + 1) * OW)],
                    [(m_g, (2 + sl) * D + n * OW, (2 + sl) * D + (n + 1) * OW)])
            dma("sync", f"sp{sl}", dxb[s, t * 128:(t + 1) * 128, :], xb,
                [(m_g, (2 + sl) * D, (3 + sl) * D)],
                [(m_xbs, s * S + t * 128, s * S + (t + 1) * 128)])
            hn = hn_ring[:, sl * D:(sl + 1) * D]
            ln_tile(xb, m_g, (2 + sl) * D, hn, sl * D)
            for ch in range(KC):
                pb = 2 + (ch % 2)
                mm(PS[pb][:, 0:128], hn[:, ch * 128:(ch + 1) * 128], id_bf[:, :],
                   [(m_hn, sl * D + ch * 128, sl * D + (ch + 1) * 128), (m_id, *WHOLE)],
                   [(m_ps[pb], 0, 128)], start=True, stop=False)
                mm(PS[pb][:, 0:128], rows[0:1, R_SBP + ch * 128:R_SBP + (ch + 1) * 128],
                   ones_r[0:1, 0:128],
                   [(m_rows, R_SBP + ch * 128, R_SBP + (ch + 1) * 128), (m_onr, *WHOLE)],
                   [(m_ps[pb], 0, 128)], start=False, stop=True)
                sca(lambda e, ch=ch, t=t, pb=pb, s=s: e.activation(
                        arena_b[:, ch * S + t * 128:ch * S + (t + 1) * 128],
                        PS[pb][:, 0:128], Act.Copy,
                        scale=mods_fm[:, 2 * NK + ch * 2 + s:2 * NK + ch * 2 + s + 1]),
                    [(m_ps[pb], 0, 128), (m_mods, 2 * NK, 4 * NK)],
                    [(m_b, ch * S + t * 128, ch * S + (t + 1) * 128)])

        if stop_after == "proj":
            stopped[0] = True
            break
        def _emit_routing_all():
         for t in range(T128):
            for kc in range(KC):
                mm(PS[4][:, 0:NE],
                   arena_b[:, kc * S + t * 128:kc * S + (t + 1) * 128],
                   gateT[:, kc * NE:(kc + 1) * NE],
                   [(m_b, kc * S + t * 128, kc * S + (t + 1) * 128), (m_gT, kc * NE, (kc + 1) * NE)],
                   [(m_ps[4], 0, NE)],
                   start=(kc == 0), stop=(kc == KC - 1))
            routing_tile(p, cfg, PS[4], m_ps[4], rout, m_rt, sc_bf, m_scb, vec, sca)
            mm(PS[5][0:NE, 0:128], sc_bf[:, 0:NE], id_bf[:, :],
               [(m_scb, 0, NE), (m_id, *WHOLE)], [(m_ps[5], 0, 128)])
            vec(lambda e, t=t: e.tensor_copy(c_fm[0:NE, t * 128:(t + 1) * 128],
                                             PS[5][0:NE, 0:128]),
                [(m_ps[5], 0, 128)], [(m_cfm, t * 128, (t + 1) * 128)])

        if dbg and s == 0:
            dump_bf(dbg_cfm, c_fm, m_cfm, S)
        if stop_after == "gate":
            stopped[0] = True
            break
        do_gb = stop_after != "moe_nogb"
        for ch in (range(KC) if do_gb else ()):
            mm(PS[6][0:1, 0:128], ga_fm[:, NK + ch * 2 + s:NK + ch * 2 + s + 1], id_bf[:, :],
               [(m_gafm, NK + ch * 2 + s, NK + ch * 2 + s + 1), (m_id, *WHOLE)],
               [(m_ps[6], 0, 128)])
            vec(lambda e, ch=ch: e.tensor_copy(rows[0:1, R_GB + ch * 128:R_GB + (ch + 1) * 128],
                                               PS[6][0:1, 0:128]),
                [(m_ps[6], 0, 128)], [(m_rows, R_GB + ch * 128, R_GB + (ch + 1) * 128)])
        for n in (range(NOT) if do_gb else ()):
            mm(PS[6][:, 0:OW], ones_r[0:1, 0:128], rows[0:1, R_GB + n * OW:R_GB + (n + 1) * OW],
               [(m_onr, *WHOLE), (m_rows, R_GB + n * OW, R_GB + (n + 1) * OW)],
               [(m_ps[6], 0, OW)])
            vec(lambda e, n=n: e.tensor_copy(bcast[:, D + n * OW:D + (n + 1) * OW],
                                             PS[6][:, 0:OW]),
                [(m_ps[6], 0, OW)], [(m_bc, D + n * OW, D + (n + 1) * OW)])

        SW = INTER
        ICL = SW // 128
        SLAB = KC * SW + KC * SW + ICL * D
        W1O, W3O, W2O = 0, KC * SW, 2 * KC * SW
        slabs = [("sh", 0), ("sh", 1)] + [("e", e) for e in range(NE)]
        if stop_after in ("moe_dma", "moe_g0t0", "moe_g0", "moe_nogb"):
            slabs = slabs[:1]
        if stop_after is not None:
            _emit_routing_all()
        for si, (kind, idx) in enumerate(slabs):
            sl = si % 2
            base = sl * SLAB
            key = f"sl{sl}"
            if kind == "e":
                srcs = (dw1[idx], dw3[idx], dw2[idx])
            else:
                h0 = idx * SW
                srcs = (dws1[:, h0:h0 + SW], dws3[:, h0:h0 + SW], dws2[h0:h0 + SW, :])
            for (wo, wa_dim, wid, src) in ((W1O, KC, KC * SW, srcs[0]),
                                           (W3O, KC, KC * SW, srcs[1]),
                                           (W2O, ICL, ICL * D, srcs[2])):
                dma("gpsimd", key,
                    arena_a[:, base + wo:base + wo + wid].rearrange(
                        "p (a c) -> p a c", a=wa_dim),
                    src.rearrange("(a p) c -> p a c", p=128),
                    [], [(m_a, base + wo, base + wo + wid)])
            for icl in range(ICL):
                vec(lambda e, base=base, icl=icl: e.tensor_tensor(
                        arena_a[:, base + W2O + icl * D:base + W2O + (icl + 1) * D],
                        arena_a[:, base + W2O + icl * D:base + W2O + (icl + 1) * D],
                        bcast[:, D:2 * D], Alu.mult),
                    [(m_a, base + W2O + icl * D, base + W2O + (icl + 1) * D), (m_bc, D, 2 * D)],
                    [(m_a, base + W2O + icl * D, base + W2O + (icl + 1) * D)])
            if si == 1 and stop_after is None:
                _emit_routing_all()
            for t in range(T512):
                asl = t % 2
                A0 = asl * ICL * 512
                if kind == "e":
                    mm(PS[6][:, 0:512],
                       sel16[:, idx * 128:(idx + 1) * 128],
                       c_fm[0:NE, t * 512:(t + 1) * 512],
                       [(m_sel, idx * 128, (idx + 1) * 128), (m_cfm, t * 512, (t + 1) * 512)],
                       [(m_ps[6], 0, 512)])
                    vec(lambda e, asl=asl: e.tensor_copy(
                            cb_ring[:, asl * 512:(asl + 1) * 512], PS[6][:, 0:512]),
                        [(m_ps[6], 0, 512)], [(m_cb, asl * 512, (asl + 1) * 512)])
                for icl in range(ICL):
                    p1b, p3b = icl % 2, 2 + icl % 2
                    for kc in range(KC):
                        mm(PS[p1b][:, 0:512],
                           arena_a[:, base + W1O + kc * SW + icl * 128:base + W1O + kc * SW + (icl + 1) * 128],
                           arena_b[:, kc * S + t * 512:kc * S + (t + 1) * 512],
                           [(m_a, base + W1O + kc * SW + icl * 128, base + W1O + kc * SW + (icl + 1) * 128),
                            (m_b, kc * S + t * 512, kc * S + (t + 1) * 512)],
                           [(m_ps[p1b], 0, 512)],
                           start=(kc == 0), stop=(kc == KC - 1))
                    for kc in range(KC):
                        mm(PS[p3b][:, 0:512],
                           arena_a[:, base + W3O + kc * SW + icl * 128:base + W3O + kc * SW + (icl + 1) * 128],
                           arena_b[:, kc * S + t * 512:kc * S + (t + 1) * 512],
                           [(m_a, base + W3O + kc * SW + icl * 128, base + W3O + kc * SW + (icl + 1) * 128),
                            (m_b, kc * S + t * 512, kc * S + (t + 1) * 512)],
                           [(m_ps[p3b], 0, 512)],
                           start=(kc == 0), stop=(kc == KC - 1))
                    tsl = icl % 2
                    t1 = moe_tmp[:, tsl * 512:(tsl + 1) * 512]
                    t3 = moe_tmp[:, 1024 + tsl * 512:1024 + (tsl + 1) * 512]
                    if kind == "sh":
                        vec(lambda e, t3=t3, p3b=p3b: e.tensor_copy(t3, PS[p3b][:, 0:512]),
                            [(m_ps[p3b], 0, 512)], [(m_mt, 1024 + tsl * 512, 1024 + (tsl + 1) * 512)])
                    else:
                        vec(lambda e, t3=t3, p3b=p3b, asl=asl: e.tensor_tensor(
                                t3, PS[p3b][:, 0:512],
                                cb_ring[:, asl * 512:(asl + 1) * 512], Alu.mult),
                            [(m_ps[p3b], 0, 512), (m_cb, asl * 512, (asl + 1) * 512)],
                            [(m_mt, 1024 + tsl * 512, 1024 + (tsl + 1) * 512)])
                    sca(lambda e, t1=t1, p1b=p1b: e.activation(t1, PS[p1b][:, 0:512], Act.Silu),
                        [(m_ps[p1b], 0, 512)], [(m_mt, tsl * 512, (tsl + 1) * 512)])
                    vec(lambda e, t1=t1, t3=t3, icl=icl, A0=A0: e.tensor_tensor(
                            arena_d[:, A0 + icl * 512:A0 + (icl + 1) * 512], t1, t3, Alu.mult),
                        [(m_mt, tsl * 512, (tsl + 1) * 512),
                         (m_mt, 1024 + tsl * 512, 1024 + (tsl + 1) * 512)],
                        [(m_d, A0 + icl * 512, A0 + (icl + 1) * 512)])
                for mtile in range(4):
                    for n in range(NOT):
                        pb = 4 + ((mtile * NOT + n) % 2)
                        for icl in range(ICL):
                            mm(PS[pb][:, 0:OW],
                               arena_d[:, A0 + icl * 512 + mtile * 128:A0 + icl * 512 + (mtile + 1) * 128],
                               arena_a[:, base + W2O + icl * D + n * OW:base + W2O + icl * D + (n + 1) * OW],
                               [(m_d, A0 + icl * 512 + mtile * 128, A0 + icl * 512 + (mtile + 1) * 128),
                                (m_a, base + W2O + icl * D + n * OW, base + W2O + icl * D + (n + 1) * OW)],
                               [(m_ps[pb], 0, OW)],
                               start=(icl == 0), stop=(icl == ICL - 1))
                        ycol = (t * 4 + mtile) * D + n * OW
                        if si == 0:
                            vec(lambda e, pb=pb, ycol=ycol: e.tensor_copy(
                                    arena_c[:, ycol:ycol + OW], PS[pb][:, 0:OW]),
                                [(m_ps[pb], 0, OW)], [(m_c, ycol, ycol + OW)])
                        else:
                            vec(lambda e, pb=pb, ycol=ycol: e.tensor_tensor(
                                    arena_c[:, ycol:ycol + OW], PS[pb][:, 0:OW],
                                    arena_c[:, ycol:ycol + OW], Alu.add),
                                [(m_ps[pb], 0, OW), (m_c, ycol, ycol + OW)],
                                [(m_c, ycol, ycol + OW)])

        for t in range(T128):
            sl = t % 2
            xg = arena_g[:, sl * D:(sl + 1) * D]
            dma("sync", f"x{sl}", xg, dxb[s, t * 128:(t + 1) * 128, :],
                [(m_xbs, s * S + t * 128, s * S + (t + 1) * 128)],
                [(m_g, sl * D, (sl + 1) * D)])
            vec(lambda e, t=t, sl=sl, xg=xg: e.tensor_tensor(
                    arena_g[:, (2 + sl) * D:(3 + sl) * D], xg,
                    arena_c[:, t * D:(t + 1) * D], Alu.add),
                [(m_g, sl * D, (sl + 1) * D), (m_c, t * D, (t + 1) * D)],
                [(m_g, (2 + sl) * D, (3 + sl) * D)])
            dma("sync", "st", dout[s, t * 128:(t + 1) * 128, :],
                arena_g[:, (2 + sl) * D:(3 + sl) * D],
                [(m_g, (2 + sl) * D, (3 + sl) * D)],
                [(m_out, s * S + t * 128, s * S + (t + 1) * 128)])

    if not stopped[0]:
        p.do("sync", None, r=[(m_out, 0, SEQS * S)])

    plans, sem_names = p.finalize()

    sems = {}
    for sn in sem_names:
        g = nc.semaphore(f"sem_{sn}")
        sems[sn] = g.__enter__()
        ctx.append(g)

    def mk(e):
        def run(eng):
            for waits, emit, inc, inc_sem, drain in plans[e]:
                for (sn, v) in waits:
                    eng.wait_ge(sems[sn], v)
                if emit is not None:
                    inst = emit(eng)
                    if inc:
                        inst.then_inc(sems[inc_sem], inc)
                    if drain:
                        eng.drain()
        return run

    blk = nc.Block()
    block = blk.__enter__()
    block.tensor(mk("tensor"))
    block.vector(mk("vector"))
    block.scalar(mk("scalar"))
    block.sync(mk("sync"))
    block.gpsimd(mk("gpsimd"))
    blk.__exit__(None, None, None)

    for g in reversed(ctx):
        g.__exit__(None, None, None)
    return nc


def routing_tile(p, cfg, psl, m_psl, rout, m_rt, sc_bf, m_scb, vec, sca):
    NE = cfg.NE
    G0, W0, E0, S0, M0, T0 = 0, 4, 20, 36, 52, 68
    vec(lambda e: e.reduce_max(rout[:, T0:T0 + 1], psl[:, 0:NE], Ax.X, negate=True),
        [(m_psl, 0, NE)], [(m_rt, T0, T0 + 1)], drain=True)
    sca(lambda e: e.activation(rout[:, E0:E0 + NE], psl[:, 0:NE], Act.Exp,
                               bias=rout[:, T0:T0 + 1]),
        [(m_psl, 0, NE), (m_rt, T0, T0 + 1)], [(m_rt, E0, E0 + NE)])
    vec(lambda e: e.reduce_sum(rout[:, T0 + 1:T0 + 2], rout[:, E0:E0 + NE], Ax.X),
        [(m_rt, E0, E0 + NE)], [(m_rt, T0 + 1, T0 + 2)], drain=True)
    vec(lambda e: e.reciprocal(rout[:, T0 + 2:T0 + 3], rout[:, T0 + 1:T0 + 2]),
        [(m_rt, T0 + 1, T0 + 2)], [(m_rt, T0 + 2, T0 + 3)])
    vec(lambda e: e.tensor_scalar_mul(rout[:, S0:S0 + NE], rout[:, E0:E0 + NE],
                                      rout[:, T0 + 2:T0 + 3]),
        [(m_rt, E0, E0 + NE), (m_rt, T0 + 2, T0 + 3)], [(m_rt, S0, S0 + NE)])
    for r in range(4):
        vec(lambda e, r=r: e.reduce_max(rout[:, G0 + r:G0 + r + 1],
                                        rout[:, S0 + 4 * r:S0 + 4 * r + 4], Ax.X),
            [(m_rt, S0 + 4 * r, S0 + 4 * r + 4)], [(m_rt, G0 + r, G0 + r + 1)], drain=True)
    vec(lambda e: e.reduce_max(rout[:, T0 + 3:T0 + 4], rout[:, G0:G0 + 4], Ax.X),
        [(m_rt, G0, G0 + 4)], [(m_rt, T0 + 3, T0 + 4)], drain=True)
    vec(lambda e: e.tensor_scalar(rout[:, W0:W0 + 4], rout[:, G0:G0 + 4],
                                  rout[:, T0 + 3:T0 + 4], None, Alu.is_ge),
        [(m_rt, G0, G0 + 4), (m_rt, T0 + 3, T0 + 4)], [(m_rt, W0, W0 + 4)])
    vec(lambda e: e.scalar_tensor_tensor(rout[:, W0:W0 + 4], rout[:, W0:W0 + 4], KNOCK_NEG,
                                         rout[:, G0:G0 + 4], Alu.mult, Alu.add),
        [(m_rt, W0, W0 + 4), (m_rt, G0, G0 + 4)], [(m_rt, W0, W0 + 4)])
    vec(lambda e: e.reduce_max(rout[:, T0 + 4:T0 + 5], rout[:, W0:W0 + 4], Ax.X),
        [(m_rt, W0, W0 + 4)], [(m_rt, T0 + 4, T0 + 5)], drain=True)
    vec(lambda e: e.tensor_scalar(rout[:, G0:G0 + 4], rout[:, G0:G0 + 4],
                                  rout[:, T0 + 4:T0 + 5], None, Alu.is_ge),
        [(m_rt, G0, G0 + 4), (m_rt, T0 + 4, T0 + 5)], [(m_rt, G0, G0 + 4)])
    for r in range(4):
        vec(lambda e, r=r: e.tensor_scalar_mul(rout[:, M0 + 4 * r:M0 + 4 * r + 4],
                                               rout[:, S0 + 4 * r:S0 + 4 * r + 4],
                                               rout[:, G0 + r:G0 + r + 1]),
            [(m_rt, S0 + 4 * r, S0 + 4 * r + 4), (m_rt, G0 + r, G0 + r + 1)],
            [(m_rt, M0 + 4 * r, M0 + 4 * r + 4)])
    vec(lambda e: e.tensor_copy(rout[:, W0:W0 + NE], rout[:, M0:M0 + NE]),
        [(m_rt, M0, M0 + NE)], [(m_rt, W0, W0 + NE)])
    for k in range(3):
        vec(lambda e: e.reduce_max(rout[:, T0 + 5:T0 + 6], rout[:, W0:W0 + NE], Ax.X),
            [(m_rt, W0, W0 + NE)], [(m_rt, T0 + 5, T0 + 6)], drain=True)
        vec(lambda e: e.tensor_scalar(rout[:, E0:E0 + NE], rout[:, W0:W0 + NE],
                                      rout[:, T0 + 5:T0 + 6], None, Alu.is_ge),
            [(m_rt, W0, W0 + NE), (m_rt, T0 + 5, T0 + 6)], [(m_rt, E0, E0 + NE)])
        vec(lambda e: e.scalar_tensor_tensor(rout[:, W0:W0 + NE], rout[:, E0:E0 + NE], KNOCK_NEG,
                                             rout[:, W0:W0 + NE], Alu.mult, Alu.add),
            [(m_rt, E0, E0 + NE), (m_rt, W0, W0 + NE)], [(m_rt, W0, W0 + NE)])
    vec(lambda e: e.reduce_max(rout[:, T0 + 5:T0 + 6], rout[:, W0:W0 + NE], Ax.X),
        [(m_rt, W0, W0 + NE)], [(m_rt, T0 + 5, T0 + 6)], drain=True)
    vec(lambda e: e.tensor_scalar(rout[:, E0:E0 + NE], rout[:, M0:M0 + NE],
                                  rout[:, T0 + 5:T0 + 6], None, Alu.is_ge),
        [(m_rt, M0, M0 + NE), (m_rt, T0 + 5, T0 + 6)], [(m_rt, E0, E0 + NE)])
    vec(lambda e: e.tensor_tensor(sc_bf[:, 0:NE], rout[:, S0:S0 + NE],
                                  rout[:, E0:E0 + NE], Alu.mult),
        [(m_rt, S0, S0 + NE), (m_rt, E0, E0 + NE)], [(m_scb, 0, NE)])



def mk_consts(cfg):
    NE = cfg.NE
    id128 = np.eye(128, dtype=ml_dtypes.bfloat16)
    i = np.arange(128)
    mask = np.where(i[None, :] >= i[:, None], 0.0, MASK_NEG).astype(np.float32)
    sel = np.zeros((NE, NE * 128), dtype=ml_dtypes.bfloat16)
    for e in range(NE):
        sel[e, e * 128:(e + 1) * 128] = 1.0
    return id128, mask, sel


def make_inputs(cfg, **a):
    D = cfg.D
    x = np.ascontiguousarray(np.asarray(a["x"], dtype=np.float32))
    w_in = np.asarray(a["w_in"], dtype=np.float32)
    wq = w_in[:, 0:D].reshape(D, cfg.NPAIR, 128)
    wk = w_in[:, D:2 * D].reshape(D, cfg.NPAIR, 128)
    wv = w_in[:, 2 * D:3 * D].reshape(D, cfg.NPAIR, 128)
    w_in_re = np.ascontiguousarray(np.concatenate([wq, wk, wv], axis=2))
    id128, mask, sel = mk_consts(cfg)
    shared = dict(
        w_in_re=w_in_re,
        b_in=np.ascontiguousarray(np.asarray(a["b_in"], dtype=np.float32)),
        w_out=np.ascontiguousarray(np.asarray(a["w_out"], dtype=np.float32)),
        b_out=np.ascontiguousarray(np.asarray(a["b_out"], dtype=np.float32)),
        gate_w=np.ascontiguousarray(np.asarray(a["gate_w"], dtype=np.float32)),
        w1=np.ascontiguousarray(np.asarray(a["w1"], dtype=np.float32)),
        w2=np.ascontiguousarray(np.asarray(a["w2"], dtype=np.float32)),
        w3=np.ascontiguousarray(np.asarray(a["w3"], dtype=np.float32)),
        ws1=np.ascontiguousarray(np.asarray(a["ws1"], dtype=np.float32)),
        ws2=np.ascontiguousarray(np.asarray(a["ws2"], dtype=np.float32)),
        ws3=np.ascontiguousarray(np.asarray(a["ws3"], dtype=np.float32)),
        wmod=np.ascontiguousarray(np.asarray(a["wmod"], dtype=np.float32)),
        id128d=id128, mask_td=mask, sel16d=sel,
    )
    cvec = np.asarray(a["cvec"], dtype=np.float32)
    return x, cvec, shared


def kernel(L0_attn_x, L0_attn_w_in, L0_attn_b_in, L0_attn_w_out, L0_attn_b_out,
           L0_ffn_gate_w, L0_ffn_w1, L0_ffn_w2, L0_ffn_w3, L0_ffn_ws1, L0_ffn_ws2,
           L0_ffn_ws3, L0_c, L0_wmod):
    cfg = CFG()
    x, cvec, shared = make_inputs(
        cfg, x=L0_attn_x, w_in=L0_attn_w_in, b_in=L0_attn_b_in, w_out=L0_attn_w_out,
        b_out=L0_attn_b_out, gate_w=L0_ffn_gate_w, w1=L0_ffn_w1, w2=L0_ffn_w2,
        w3=L0_ffn_w3, ws1=L0_ffn_ws1, ws2=L0_ffn_ws2, ws3=L0_ffn_ws3, cvec=L0_c,
        wmod=L0_wmod)
    nc = build_program(cfg, paranoid={"vector"})
    in_maps = []
    for c in range(NCORES):
        m = dict(shared)
        m["x"] = np.ascontiguousarray(x[c * cfg.SEQS:(c + 1) * cfg.SEQS])
        m["cvec"] = np.ascontiguousarray(cvec[c * cfg.SEQS:(c + 1) * cfg.SEQS])
        in_maps.append(m)
    res = run_bass_kernel_spmd(nc, in_maps, core_ids=list(range(NCORES)))
    out = np.concatenate([res.results[c]["out"] for c in range(NCORES)], axis=0)
    return np.ascontiguousarray(out.astype(np.float32))


# version 10
# speedup vs baseline: 2.3326x; 1.0681x over previous
import os

import numpy as np
import ml_dtypes

import concourse.bass as bass
import concourse.mybir as mybir
from concourse.bass_utils import run_bass_kernel_spmd

F32 = mybir.dt.float32
BF16 = mybir.dt.bfloat16
Alu = mybir.AluOpType
Act = mybir.ActivationFunctionType
Ax = mybir.AxisListType

NCORES = 8
MASK_NEG = -30000.0
KNOCK_NEG = -1e30


class CFG:
    SEQS = 2
    S = 2048
    D = 1024
    H = 16
    NE = 16
    INTER = 512
    LN_EPS = 1e-5
    SHIFT_CLAMP = -60.0

    HD = property(lambda s: s.D // s.H)
    NPAIR = property(lambda s: s.H // 2)
    KC = property(lambda s: s.D // 128)
    T128 = property(lambda s: s.S // 128)
    T512 = property(lambda s: s.S // 512)
    SHARED = property(lambda s: 2 * s.INTER)
    GW = property(lambda s: 2 * s.INTER)
    ICG = property(lambda s: s.GW // 128)
    NGRP = property(lambda s: s.NE // 2 + 1)
    MODC = property(lambda s: 6 * s.D // 128)



ENGINES = ("tensor", "vector", "scalar", "sync", "gpsimd")
DMA_ENGINES = frozenset(("sync", "gpsimd"))


class Tok:
    __slots__ = ("eng", "idx", "gid")

    def __init__(self, eng, idx, gid):
        self.eng = eng
        self.idx = idx
        self.gid = gid


class Mem:

    __slots__ = ("name", "writers", "readers", "coarse")

    def __init__(self, name, coarse=False):
        self.name = name
        self.writers = []
        self.readers = []
        self.coarse = coarse

    def on_read(self, lo, hi, tok, deps):
        if self.coarse:
            lo, hi = 0, 1 << 30
        for (l, h, t) in self.writers:
            if l < hi and lo < h:
                deps.append(t)
        rd = self.readers
        keep = [e for e in rd if not (e[2].eng == tok.eng and lo <= e[0] and e[1] <= hi)]
        keep.append([lo, hi, tok])
        self.readers = keep

    def on_write(self, lo, hi, tok, deps):
        if self.coarse:
            lo, hi = 0, 1 << 30
        for (l, h, t) in self.writers:
            if l < hi and lo < h:
                deps.append(t)
        for (l, h, t) in self.readers:
            if l < hi and lo < h:
                deps.append(t)
        self.writers = [e for e in self.writers if not (lo <= e[0] and e[1] <= hi)]
        self.readers = [e for e in self.readers if not (lo <= e[0] and e[1] <= hi)]
        self.writers.append([lo, hi, tok])


class Prog:

    def __init__(self, paranoid=False):
        if paranoid is True:
            paranoid = set(ENGINES)
        elif not paranoid:
            paranoid = set()
        self.paranoid = paranoid
        self.streams = {e: [] for e in ENGINES}
        self.keys = []
        self.gid = 0

    def do(self, eng, emit, r=(), w=(), key=None, drain=False):
        if eng in DMA_ENGINES and emit is not None:
            assert key is not None, "DMA op requires a semaphore key"
            if key not in self.keys:
                self.keys.append(key)
        self.gid += 1
        tok = Tok(eng, len(self.streams[eng]), self.gid)
        deps = []
        for (m, lo, hi) in r:
            m.on_read(lo, hi, tok, deps)
        for (m, lo, hi) in w:
            m.on_write(lo, hi, tok, deps)
        deps = [d for d in deps if d is not tok]
        self.streams[eng].append([emit, deps, False, key, tok.gid, drain])
        return tok

    def finalize(self):
        import bisect
        for e in ENGINES:
            for op in self.streams[e]:
                if e in DMA_ENGINES and op[0] is not None:
                    op[2] = True
                for d in op[1]:
                    de = d.eng
                    if de in DMA_ENGINES:
                        continue
                    if de != e or de in self.paranoid:
                        self.streams[de][d.idx][2] = True
        key_gids = {k: [] for k in self.keys}
        for e in DMA_ENGINES:
            for op in self.streams[e]:
                if op[0] is not None:
                    key_gids[op[3]].append(op[4])
        for k in key_gids:
            key_gids[k].sort()
        ordinals = {e: [] for e in ENGINES}
        cnt_e = {e: 0 for e in ENGINES}
        for e in ENGINES:
            for op in self.streams[e]:
                if e not in DMA_ENGINES and op[2]:
                    cnt_e[e] += 1
                ordinals[e].append(cnt_e[e])
        plans = {}
        for e in ENGINES:
            plan = []
            last_wait = {}
            for emit, deps, marked, key, gid, drain in self.streams[e]:
                need = {}
                for d in deps:
                    de = d.eng
                    if de in DMA_ENGINES:
                        dk = self.streams[de][d.idx][3]
                        v = bisect.bisect_left(key_gids[dk], gid)
                        if v > 0 and v > need.get(dk, 0):
                            need[dk] = v
                    else:
                        if de == e and de not in self.paranoid:
                            continue
                        v = ordinals[de][d.idx]
                        if v > 0 and v > need.get(de, 0):
                            need[de] = v
                waits = []
                for sn, v in need.items():
                    if last_wait.get(sn, 0) < v:
                        waits.append((sn, v * (16 if sn in key_gids else 1)))
                        last_wait[sn] = v
                inc_sem = key if e in DMA_ENGINES else e
                inc = (16 if e in DMA_ENGINES else 1) if marked else 0
                plan.append((waits, emit, inc, inc_sem, drain))
            plans[e] = plan
        return plans, [e for e in ENGINES if e not in DMA_ENGINES] + self.keys



def build_program(cfg, paranoid=False, stop_after=None, dbg=False):
    nc = bass.Bass("TRN2", target_bir_lowering=False,
                   detect_race_conditions=bool(int(os.environ.get("KBASS_RACED", "0"))),
                   dynamic_dma_scratch_size=16384)
    SEQS, S, D, NE, INTER = cfg.SEQS, cfg.S, cfg.D, cfg.NE, cfg.INTER
    KC, NPAIR, T128, T512 = cfg.KC, cfg.NPAIR, cfg.T128, cfg.T512
    ICG, NGRP, MODC = cfg.ICG, cfg.NGRP, cfg.MODC
    NK = 2 * KC
    OW = min(512, D)
    NOT = D // OW

    dx = nc.dram_tensor("x", [SEQS, S, D], F32, kind="ExternalInput").ap()
    dwin = nc.dram_tensor("w_in_re", [D, NPAIR, 384], F32, kind="ExternalInput").ap()
    dbin = nc.dram_tensor("b_in", [3 * D], F32, kind="ExternalInput").ap()
    dwout = nc.dram_tensor("w_out", [D, D], F32, kind="ExternalInput").ap()
    dbout = nc.dram_tensor("b_out", [D], F32, kind="ExternalInput").ap()
    dgate = nc.dram_tensor("gate_w", [NE, D], F32, kind="ExternalInput").ap()
    dw1 = nc.dram_tensor("w1", [NE, D, INTER], F32, kind="ExternalInput").ap()
    dw2 = nc.dram_tensor("w2", [NE, INTER, D], F32, kind="ExternalInput").ap()
    dw3 = nc.dram_tensor("w3", [NE, D, INTER], F32, kind="ExternalInput").ap()
    dws1 = nc.dram_tensor("ws1", [D, cfg.SHARED], F32, kind="ExternalInput").ap()
    dws2 = nc.dram_tensor("ws2", [cfg.SHARED, D], F32, kind="ExternalInput").ap()
    dws3 = nc.dram_tensor("ws3", [D, cfg.SHARED], F32, kind="ExternalInput").ap()
    dc = nc.dram_tensor("cvec", [SEQS, D], F32, kind="ExternalInput").ap()
    dwmod = nc.dram_tensor("wmod", [D, 6 * D], F32, kind="ExternalInput").ap()
    did = nc.dram_tensor("id128d", [128, 128], BF16, kind="ExternalInput").ap()
    dmask = nc.dram_tensor("mask_td", [128, 128], F32, kind="ExternalInput").ap()
    dsel = nc.dram_tensor("sel16d", [NE, NE * 128], BF16, kind="ExternalInput").ap()
    dout = nc.dram_tensor("out", [SEQS, S, D], F32, kind="ExternalOutput").ap()
    dxb = nc.dram_tensor("xb_scr", [SEQS, S, D], F32,
                         kind="ExternalOutput" if dbg else "Internal").ap()
    if dbg:
        dbg_mods = nc.dram_tensor("dbg_mods", [128, 4 * NK], F32, kind="ExternalOutput").ap()
        dbg_h = nc.dram_tensor("dbg_h", [128, KC * S], F32, kind="ExternalOutput").ap()
        dbg_attn = nc.dram_tensor("dbg_attn", [128, KC * S], F32, kind="ExternalOutput").ap()
        dbg_cfm = nc.dram_tensor("dbg_cfm", [NE, S], F32, kind="ExternalOutput").ap()
        dbg_y = nc.dram_tensor("dbg_y", [128, KC * S], F32, kind="ExternalOutput").ap()
        dbg_f32 = nc.dram_tensor("dbg_f32", [128, KC * S], F32, kind="Internal").ap()
        dbg_qk = nc.dram_tensor("dbg_qk", [128, 2 * S], F32, kind="ExternalOutput").ap()
        dbg_v = nc.dram_tensor("dbg_v", [128, 4224], F32, kind="ExternalOutput").ap()
        dbg_at = nc.dram_tensor("dbg_at", [128, 2048], F32, kind="ExternalOutput").ap()
        dbg_nc = nc.dram_tensor("dbg_nc", [128, 4], F32, kind="ExternalOutput").ap()
        dbg_bir = nc.dram_tensor("dbg_bir", [1, 2 * D + 512 + D], F32, kind="ExternalOutput").ap()
        m_dbg = Mem("dbg")

        def dump_bf(dst, src_ap, mem, width):
            for col in range(0, width, 4096):
                wdt = min(4096, width - col)
                dma("gpsimd", "dbgk", dst[:, col:col + wdt], src_ap[:, col:col + wdt],
                    [(mem, col, col + wdt)], [(m_dbg, 0, 1)])

    p = Prog(paranoid=paranoid)
    ctx = []

    def sb(name, shape, dtype):
        g = nc.sbuf_tensor(name, shape, dtype)
        t = g.__enter__()
        ctx.append(g)
        return t, Mem(name)

    AW = max(3 * KC * cfg.GW, 2 * KC * 384, 2 * 6 * D, KC * D)
    arena_a, m_a = sb("arena_a", [128, AW], BF16)
    GWD = cfg.GW
    assert GWD == D
    WAB = KC * GWD
    A_WA, A_WB, A_WC = 0, WAB, 2 * WAB
    arena_b, m_b = sb("arena_b", [128, KC * S], BF16)
    arena_c, m_c = sb("arena_c", [128, KC * S], BF16)
    D_PT, D_SQ = 0, 1536
    D_VA, D_VB = 2560, 2560 + T128 * 65
    D_AT = 2560 + 2 * T128 * 65
    DW = max(2 * ICG * 512, D_AT + T128 * 128, KC * 1024)
    arena_d, m_d = sb("arena_d", [128, DW], BF16)
    arena_e, m_e = sb("arena_e", [128, 2 * S], BF16)
    arena_g, m_g = sb("arena_g", [128, 4 * D], F32)
    hn_ring, m_hn = sb("hn_ring", [128, 2 * D], BF16)
    moe_tmp, m_mt = sb("moe_tmp", [128, 2048], BF16)
    cb_ring, m_cb = sb("cb_ring", [128, 2048], BF16)
    c_fm, m_cfm = sb("c_fm", [NE, S], BF16)
    bcast, m_bc = sb("bcast", [128, 2 * D], BF16)
    rows, m_rows = sb("rows", [1, 6 * D], BF16)
    R_GA, R_GAB, R_GB, R_SAP, R_SBP = 0, 2 * D, 3 * D, 4 * D, 5 * D
    bvrow, m_bv = sb("bvrow", [1, D], BF16)
    borow, m_bo = sb("borow", [1, D], BF16)
    mods_fm, m_mods = sb("mods_fm", [128, 4 * NK], F32)
    ga_fm, m_gafm = sb("ga_fm", [128, 2 * NK], BF16)
    id_bf, m_id = sb("id_bf", [128, 128], BF16)
    mask_t, m_mask = sb("mask_t", [128, 128], F32)
    sel16, m_sel = sb("sel16", [NE, NE * 128], BF16)
    gw_bf, m_gw = sb("gw_bf", [NE, D], BF16)
    gateT, m_gT = sb("gateT", [128, KC * NE], BF16)
    c2row, m_c2 = sb("c2row", [2, D], F32)
    c2bf, m_c2b = sb("c2bf", [2, D], BF16)
    csilu, m_cs = sb("csilu", [128, NK], BF16)
    ones_r, m_onr = sb("ones_r", [1, 512], BF16)
    ones_c, m_onc = sb("ones_c", [128, 1], BF16)
    rout, m_rt = sb("rout", [128, 128], F32)
    sc_bf, m_scb = sb("sc_bf", [128, NE], BF16)
    smalls, m_sm = sb("smalls", [128, 16], F32)
    brow, m_br = sb("brow", [1, 16], F32)
    brow_bf, m_brb = sb("brow_bf", [1, 4], BF16)
    b_in_row, m_bir = sb("b_in_row", [1, 2 * D], BF16)
    sap_bf, m_sap = sb("sap_bf", [128, 2 * NK], BF16)

    PS = []
    m_ps = []
    for i in range(8):
        g = nc.psum_tensor(f"ps{i}", [128, 512], F32)
        PS.append(g.__enter__())
        ctx.append(g)
        m_ps.append(Mem(f"ps{i}", coarse=True))
    m_xbs = Mem("xb_scr")
    m_out = Mem("out")

    WHOLE = (0, 1 << 30)

    def mm(out_ap, lhsT_ap, rhs_ap, r, w, start=True, stop=True, skip=False):
        lsh, rsh, osh = lhsT_ap.shape, rhs_ap.shape, out_ap.shape
        assert lsh[0] == rsh[0], f"mm K mismatch {lsh} vs {rsh}"
        assert lsh[1] == osh[0], f"mm M mismatch lhsT {lsh} out {osh}"
        assert rsh[1] == osh[1], f"mm N mismatch rhs {rsh} out {osh}"
        return p.do("tensor",
                    lambda e, o=out_ap, l=lhsT_ap, rr=rhs_ap, st=start, sp=stop, sk=skip:
                        e.matmul(o, l, rr, start=st, stop=sp, skip_group_check=sk),
                    r=r, w=w)

    def vec(fn, r, w, drain=False):
        return p.do("vector", fn, r=r, w=w, drain=drain)

    def sca(fn, r, w):
        return p.do("scalar", fn, r=r, w=w)

    def dma(q, key, out_ap, in_ap, r, w):
        return p.do(q, lambda e, o=out_ap, i=in_ap: e.dma_start(o, i), r=r, w=w, key=key)

    dma("sync", "k_id", id_bf[:, :], did, [], [(m_id, *WHOLE)])
    dma("sync", "k_mask", mask_t[:, :], dmask, [], [(m_mask, *WHOLE)])
    dma("sync", "k_sel", sel16[:, :], dsel, [], [(m_sel, *WHOLE)])
    if SEQS < 2:
        vec(lambda e: e.memset(c2row[:, :], 0.0), [], [(m_c2, *WHOLE)])
    dma("sync", "k_c2", c2row[0:SEQS, :], dc[:, :], [],
        [(m_c2, 0, D if SEQS == 2 else 1)])
    dma("gpsimd", "k_bo", borow[0:1, :], dbout.rearrange("(o d) -> o d", o=1), [], [(m_bo, *WHOLE)])
    dma("gpsimd", "k_bir", b_in_row[0:1, :], dbin.rearrange("(o d) -> o d", o=1)[0:1, 0:2 * D],
        [], [(m_bir, *WHOLE)])
    dma("gpsimd", "k_bv", bvrow[0:1, :], dbin.rearrange("(o d) -> o d", o=1)[0:1, 2 * D:3 * D],
        [], [(m_bv, *WHOLE)])
    dma("gpsimd", "k_gw", gw_bf[:, :], dgate, [], [(m_gw, *WHOLE)])

    vec(lambda e: e.memset(ones_r[:, :], 1.0), [], [(m_onr, *WHOLE)])
    vec(lambda e: e.memset(smalls[:, 6:7], cfg.LN_EPS), [], [(m_sm, 6, 7)])
    vec(lambda e: e.memset(ones_c[:, :], 1.0), [], [(m_onc, *WHOLE)])



    for n in range(NOT):
        mm(PS[6][:, 0:OW], ones_r[0:1, 0:128], bvrow[0:1, n * OW:(n + 1) * OW],
           [(m_onr, *WHOLE), (m_bv, n * OW, (n + 1) * OW)], [(m_ps[6], 0, OW)])
        vec(lambda e, n=n: e.tensor_copy(bcast[:, n * OW:(n + 1) * OW], PS[6][:, 0:OW]),
            [(m_ps[6], 0, OW)], [(m_bc, n * OW, (n + 1) * OW)])

    sca(lambda e: e.activation(c2row[0:2, :], c2row[0:2, :], Act.Silu),
        [(m_c2, *WHOLE)], [(m_c2, *WHOLE)])
    vec(lambda e: e.tensor_copy(c2bf[0:2, :], c2row[0:2, :]),
        [(m_c2, *WHOLE)], [(m_c2b, *WHOLE)])
    for ch in range(KC):
        mm(PS[6][0:128, 0:2], c2bf[0:2, ch * 128:(ch + 1) * 128], id_bf[0:2, 0:2],
           [(m_c2b, ch * 128, (ch + 1) * 128), (m_id, 0, 2)], [(m_ps[6], 0, 2)])
        vec(lambda e, ch=ch: e.tensor_copy(csilu[:, ch * 2:ch * 2 + 2], PS[6][0:128, 0:2]),
            [(m_ps[6], 0, 2)], [(m_cs, ch * 2, ch * 2 + 2)])

    for ch in range(KC):
        mm(PS[6][0:128, 0:NE], gw_bf[0:NE, ch * 128:(ch + 1) * 128], id_bf[0:NE, 0:NE],
           [(m_gw, ch * 128, (ch + 1) * 128), (m_id, 0, NE)], [(m_ps[6], 0, NE)])
        vec(lambda e, ch=ch: e.tensor_copy(gateT[:, ch * NE:(ch + 1) * NE], PS[6][0:128, 0:NE]),
            [(m_ps[6], 0, NE)], [(m_gT, ch * NE, (ch + 1) * NE)])

    WM = 6 * D
    for kc in range(KC):
        sl = kc % 2
        dma("gpsimd", f"wm{sl}", arena_a[:, sl * WM:(sl + 1) * WM], dwmod[kc * 128:(kc + 1) * 128, :],
            [], [(m_a, sl * WM, (sl + 1) * WM)])
        for mc in range(MODC):
            mm(PS[7][:, mc * 2:mc * 2 + 2],
               arena_a[:, sl * WM + mc * 128: sl * WM + (mc + 1) * 128],
               csilu[:, kc * 2:kc * 2 + 2],
               [(m_a, sl * WM + mc * 128, sl * WM + (mc + 1) * 128), (m_cs, kc * 2, kc * 2 + 2)],
               [(m_ps[7], mc * 2, mc * 2 + 2)],
               start=(kc == 0 and mc == 0), stop=(kc == KC - 1), skip=True)
    CA0, SA0, GA0, CB0, SB0, GB0 = (NK, 0, 2 * NK, 4 * NK, 3 * NK, 5 * NK)
    vec(lambda e: e.tensor_scalar_add(mods_fm[:, 0:NK], PS[7][:, CA0:CA0 + NK], 1.0),
        [(m_ps[7], CA0, CA0 + NK)], [(m_mods, 0, NK)])
    vec(lambda e: e.tensor_copy(mods_fm[:, NK:2 * NK], PS[7][:, SA0:SA0 + NK]),
        [(m_ps[7], SA0, SA0 + NK)], [(m_mods, NK, 2 * NK)])
    vec(lambda e: e.tensor_scalar_add(mods_fm[:, 2 * NK:3 * NK], PS[7][:, CB0:CB0 + NK], 1.0),
        [(m_ps[7], CB0, CB0 + NK)], [(m_mods, 2 * NK, 3 * NK)])
    vec(lambda e: e.tensor_copy(mods_fm[:, 3 * NK:4 * NK], PS[7][:, SB0:SB0 + NK]),
        [(m_ps[7], SB0, SB0 + NK)], [(m_mods, 3 * NK, 4 * NK)])
    vec(lambda e: e.reciprocal(rout[:, NK:2 * NK], mods_fm[:, 0:NK]),
        [(m_mods, 0, NK)], [(m_rt, NK, 2 * NK)])
    vec(lambda e: e.tensor_tensor(rout[:, 0:NK], mods_fm[:, NK:2 * NK],
                                  rout[:, NK:2 * NK], Alu.mult),
        [(m_mods, NK, 2 * NK), (m_rt, NK, 2 * NK)], [(m_rt, 0, NK)])
    vec(lambda e: e.tensor_copy(sap_bf[:, 0:NK], rout[:, 0:NK]),
        [(m_rt, 0, NK)], [(m_sap, 0, NK)])
    vec(lambda e: e.reciprocal(rout[:, NK:2 * NK], mods_fm[:, 2 * NK:3 * NK]),
        [(m_mods, 2 * NK, 3 * NK)], [(m_rt, NK, 2 * NK)])
    vec(lambda e: e.tensor_tensor(rout[:, 0:NK], mods_fm[:, 3 * NK:4 * NK],
                                  rout[:, NK:2 * NK], Alu.mult),
        [(m_mods, 3 * NK, 4 * NK), (m_rt, NK, 2 * NK)], [(m_rt, 0, NK)])
    vec(lambda e: e.tensor_copy(sap_bf[:, NK:2 * NK], rout[:, 0:NK]),
        [(m_rt, 0, NK)], [(m_sap, NK, 2 * NK)])
    for i, off in enumerate((GA0, GB0)):
        vec(lambda e, off=off: e.tensor_scalar(rout[:, 0:NK], PS[7][:, off:off + NK],
                                               0.0, None, Alu.is_ge),
            [(m_ps[7], off, off + NK)], [(m_rt, 0, NK)])
        vec(lambda e: e.tensor_scalar(rout[:, 0:NK], rout[:, 0:NK], 2.0, -1.0,
                                      Alu.mult, Alu.add),
            [(m_rt, 0, NK)], [(m_rt, 0, NK)])
        vec(lambda e, off=off: e.tensor_scalar_mul(rout[:, NK:2 * NK], PS[7][:, off:off + NK], -1.0),
            [(m_ps[7], off, off + NK)], [(m_rt, NK, 2 * NK)])
        vec(lambda e, off=off: e.tensor_tensor(rout[:, NK:2 * NK], PS[7][:, off:off + NK],
                                               rout[:, NK:2 * NK], Alu.max),
            [(m_ps[7], off, off + NK), (m_rt, NK, 2 * NK)], [(m_rt, NK, 2 * NK)])
        vec(lambda e: e.tensor_scalar_max(rout[:, NK:2 * NK], rout[:, NK:2 * NK], 0.25),
            [(m_rt, NK, 2 * NK)], [(m_rt, NK, 2 * NK)])
        vec(lambda e, i=i: e.tensor_tensor(ga_fm[:, i * NK:(i + 1) * NK], rout[:, 0:NK],
                                           rout[:, NK:2 * NK], Alu.mult),
            [(m_rt, 0, 2 * NK)], [(m_gafm, i * NK, (i + 1) * NK)])
    for s in range(SEQS):
        for ch in range(KC):
            mm(PS[6][0:1, 0:128], ga_fm[:, ch * 2 + s:ch * 2 + s + 1], id_bf[:, :],
               [(m_gafm, ch * 2 + s, ch * 2 + s + 1), (m_id, *WHOLE)], [(m_ps[6], 0, 128)])
            vec(lambda e, s=s, ch=ch: e.tensor_copy(
                    rows[0:1, R_GA + s * D + ch * 128:R_GA + s * D + (ch + 1) * 128],
                    PS[6][0:1, 0:128]),
                [(m_ps[6], 0, 128)], [(m_rows, R_GA + s * D + ch * 128, R_GA + s * D + (ch + 1) * 128)])

    if dbg:
        dump_bf(dbg_mods, mods_fm, m_mods, 4 * NK)

    BNW = min(512, D)
    NBN = D // BNW

    def ln_tile(x_ap, mx, xlo, hn_ap, mhn_lo):
        for i in range(NBN):
            vec(lambda e, i=i: e.bn_stats(rout[:, i * 6:(i + 1) * 6],
                                          x_ap[:, i * BNW:(i + 1) * BNW]),
                [(mx, xlo + i * BNW, xlo + (i + 1) * BNW)], [(m_rt, i * 6, (i + 1) * 6)],
                drain=True)
        mcol = 6 * NBN
        vec(lambda e: e.bn_aggr(rout[:, mcol:mcol + 2], rout[:, 0:6 * NBN]),
            [(m_rt, 0, 6 * NBN)], [(m_rt, mcol, mcol + 2)], drain=True)
        sca(lambda e: e.activation(smalls[:, 2:3], rout[:, mcol + 1:mcol + 2], Act.Sqrt,
                                   bias=smalls[:, 6:7]),
            [(m_rt, mcol + 1, mcol + 2), (m_sm, 6, 7)], [(m_sm, 2, 3)])
        vec(lambda e: e.reciprocal(smalls[:, 3:4], smalls[:, 2:3]),
            [(m_sm, 2, 3)], [(m_sm, 3, 4)])
        vec(lambda e: e.tensor_scalar_add(smalls[:, 4:5], rout[:, mcol + 1:mcol + 2], cfg.LN_EPS),
            [(m_rt, mcol + 1, mcol + 2)], [(m_sm, 4, 5)])
        vec(lambda e: e.tensor_tensor(smalls[:, 5:6], smalls[:, 3:4], smalls[:, 3:4], Alu.mult),
            [(m_sm, 3, 4)], [(m_sm, 5, 6)])
        vec(lambda e: e.tensor_tensor(smalls[:, 5:6], smalls[:, 5:6], smalls[:, 4:5], Alu.mult),
            [(m_sm, 4, 6)], [(m_sm, 5, 6)])
        vec(lambda e: e.tensor_scalar(smalls[:, 5:6], smalls[:, 5:6], -0.5, 1.5,
                                      Alu.mult, Alu.add),
            [(m_sm, 5, 6)], [(m_sm, 5, 6)])
        vec(lambda e: e.tensor_tensor(smalls[:, 3:4], smalls[:, 3:4], smalls[:, 5:6], Alu.mult),
            [(m_sm, 3, 6)], [(m_sm, 3, 4)])
        vec(lambda e: e.tensor_scalar(hn_ap, x_ap, rout[:, mcol:mcol + 1], smalls[:, 3:4],
                                      Alu.subtract, Alu.mult),
            [(mx, xlo, xlo + D), (m_rt, mcol, mcol + 1), (m_sm, 3, 4)],
            [(m_hn, mhn_lo, mhn_lo + D)])

    stopped = [False]
    do_gb = True
    pending_final = None
    for s in range(SEQS):
        if stop_after == "setup":
            stopped[0] = True
            break
        for half, rbase in ((0, R_SAP), (1, R_SBP)):
            for ch in range(KC):
                col = half * NK + ch * 2 + s
                mm(PS[6][0:1, 0:128], sap_bf[:, col:col + 1], id_bf[:, :],
                   [(m_sap, col, col + 1), (m_id, *WHOLE)], [(m_ps[6], 0, 128)])
                vec(lambda e, rbase=rbase, ch=ch: e.tensor_copy(
                        rows[0:1, rbase + ch * 128:rbase + (ch + 1) * 128], PS[6][0:1, 0:128]),
                    [(m_ps[6], 0, 128)], [(m_rows, rbase + ch * 128, rbase + (ch + 1) * 128)])
        for t in range(T128):
            if pending_final is not None:
                pending_final(t)
            sl = t % 2
            xg = arena_g[:, sl * D:(sl + 1) * D]
            dma("sync", f"x{sl}", xg, dx[s, t * 128:(t + 1) * 128, :], [],
                [(m_g, sl * D, (sl + 1) * D)])
            hn = hn_ring[:, sl * D:(sl + 1) * D]
            ln_tile(xg, m_g, sl * D, hn, sl * D)
            for ch in range(KC):
                pb = 2 + (ch % 2)
                mm(PS[pb][:, 0:128], hn[:, ch * 128:(ch + 1) * 128], id_bf[:, :],
                   [(m_hn, sl * D + ch * 128, sl * D + (ch + 1) * 128), (m_id, *WHOLE)],
                   [(m_ps[pb], 0, 128)], start=True, stop=False)
                mm(PS[pb][:, 0:128], rows[0:1, R_SAP + ch * 128:R_SAP + (ch + 1) * 128],
                   ones_r[0:1, 0:128],
                   [(m_rows, R_SAP + ch * 128, R_SAP + (ch + 1) * 128), (m_onr, *WHOLE)],
                   [(m_ps[pb], 0, 128)], start=False, stop=True)
                sca(lambda e, ch=ch, t=t, pb=pb, s=s: e.activation(
                        arena_b[:, ch * S + t * 128:ch * S + (t + 1) * 128],
                        PS[pb][:, 0:128], Act.Copy,
                        scale=mods_fm[:, ch * 2 + s:ch * 2 + s + 1]),
                    [(m_ps[pb], 0, 128), (m_mods, 0, 2 * NK)],
                    [(m_b, ch * S + t * 128, ch * S + (t + 1) * 128)])

        if dbg and s == 0:
            dump_bf(dbg_h, arena_b, m_b, KC * S)
        if stop_after == "ln1":
            stopped[0] = True
            break
        for tb in range(T128):
            vec(lambda e, tb=tb: e.memset(arena_d[:, D_VA + tb * 65 + 64:D_VA + tb * 65 + 65], 1.0),
                [], [(m_d, D_VA + tb * 65 + 64, D_VA + tb * 65 + 65)])
            vec(lambda e, tb=tb: e.memset(arena_d[:, D_VB + tb * 65 + 64:D_VB + tb * 65 + 65], 1.0),
                [], [(m_d, D_VB + tb * 65 + 64, D_VB + tb * 65 + 65)])
        WIN = KC * 384
        for pr in range(NPAIR):
            wsl = pr % 2
            win_lo = wsl * WIN
            dma("gpsimd", f"wi{wsl}",
                arena_a[:, win_lo:win_lo + WIN].rearrange("p (a c) -> p a c", a=KC),
                dwin[:, pr, :].rearrange("(a p) c -> p a c", p=128),
                [], [(m_a, win_lo, win_lo + WIN)])
            win = arena_a[:, win_lo:win_lo + WIN]

            for part in range(2):
                off = part * 128
                dst0 = part * S
                for nt in range(T512):
                    pb = part * 2 + nt % 2
                    for kc in range(KC):
                        mm(PS[pb][:, 0:512],
                           win[:, kc * 384 + off:kc * 384 + off + 128],
                           arena_b[:, kc * S + nt * 512:kc * S + (nt + 1) * 512],
                           [(m_a, win_lo + kc * 384 + off, win_lo + kc * 384 + off + 128),
                            (m_b, kc * S + nt * 512, kc * S + (nt + 1) * 512)],
                           [(m_ps[pb], 0, 512)],
                           start=(kc == 0), stop=False)
                    bcol = part * D + pr * 128
                    mm(PS[pb][:, 0:512], b_in_row[0:1, bcol:bcol + 128], ones_r[0:1, 0:512],
                       [(m_bir, bcol, bcol + 128), (m_onr, *WHOLE)],
                       [(m_ps[pb], 0, 512)], start=False, stop=True)
                    sca(lambda e, nt=nt, pb=pb, dst0=dst0, scf=(0.125 if part == 0 else 1.0):
                            e.activation(arena_e[:, dst0 + nt * 512:dst0 + (nt + 1) * 512],
                                         PS[pb][:, 0:512], Act.Copy, scale=scf),
                        [(m_ps[pb], 0, 512)],
                        [(m_e, dst0 + nt * 512, dst0 + (nt + 1) * 512)])
            for tb in range(T128):
                pb = 2 + (tb % 2)
                for kc in range(KC):
                    mm(PS[pb][:, 0:128],
                       arena_b[:, kc * S + tb * 128:kc * S + (tb + 1) * 128],
                       win[:, kc * 384 + 256:kc * 384 + 384],
                       [(m_b, kc * S + tb * 128, kc * S + (tb + 1) * 128),
                        (m_a, win_lo + kc * 384 + 256, win_lo + kc * 384 + 384)],
                       [(m_ps[pb], 0, 128)],
                       start=(kc == 0), stop=(kc == KC - 1))
                vec(lambda e, tb=tb, pb=pb, pr=pr: e.tensor_tensor(
                        arena_d[:, D_VA + tb * 65:D_VA + tb * 65 + 64],
                        PS[pb][:, 0:64], bcast[:, pr * 128:pr * 128 + 64], Alu.add),
                    [(m_ps[pb], 0, 64), (m_bc, pr * 128, pr * 128 + 64)],
                    [(m_d, D_VA + tb * 65, D_VA + tb * 65 + 64)])
                vec(lambda e, tb=tb, pb=pb, pr=pr: e.tensor_tensor(
                        arena_d[:, D_VB + tb * 65:D_VB + tb * 65 + 64],
                        PS[pb][:, 64:128], bcast[:, pr * 128 + 64:pr * 128 + 128], Alu.add),
                    [(m_ps[pb], 64, 128), (m_bc, pr * 128 + 64, pr * 128 + 128)],
                    [(m_d, D_VB + tb * 65, D_VB + tb * 65 + 64)])

            for side in range(2):
                src0 = side * S
                for nt in range(T512):
                    qsl = nt % 2
                    sca(lambda e, nt=nt, qsl=qsl, src0=src0: e.activation(
                            arena_d[:, D_SQ + qsl * 512:D_SQ + (qsl + 1) * 512],
                            arena_e[:, src0 + nt * 512:src0 + (nt + 1) * 512], Act.Square),
                        [(m_e, src0 + nt * 512, src0 + (nt + 1) * 512)],
                        [(m_d, D_SQ + qsl * 512, D_SQ + (qsl + 1) * 512)])
                    for hh in range(2):
                        pb = 4 + hh
                        mm(PS[pb][0:1, 0:512],
                           ones_c[hh * 64:(hh + 1) * 64, 0:1],
                           arena_d[hh * 64:(hh + 1) * 64, D_SQ + qsl * 512:D_SQ + (qsl + 1) * 512],
                           [(m_onc, *WHOLE), (m_d, D_SQ + qsl * 512, D_SQ + (qsl + 1) * 512)],
                           [(m_ps[pb], 0, 512)],
                           start=(nt == 0), stop=(nt == T512 - 1), skip=True)
                for hh in range(2):
                    pb = 4 + hh
                    vec(lambda e, side=side, hh=hh, pb=pb: e.reduce_max(
                            brow[0:1, side * 2 + hh:side * 2 + hh + 1], PS[pb][0:1, 0:512], Ax.X),
                        [(m_ps[pb], 0, 512)], [(m_br, side * 2 + hh, side * 2 + hh + 1)],
                        drain=True)
            for hh in range(2):
                vec(lambda e, hh=hh: e.tensor_tensor(
                        brow[0:1, 4 + hh:5 + hh], brow[0:1, hh:hh + 1],
                        brow[0:1, 2 + hh:3 + hh], Alu.add),
                    [(m_br, 0, 4)], [(m_br, 4 + hh, 5 + hh)])
                vec(lambda e, hh=hh: e.tensor_scalar(
                        brow[0:1, 4 + hh:5 + hh], brow[0:1, 4 + hh:5 + hh],
                        -1.0 / 16.0, cfg.SHIFT_CLAMP, Alu.mult, Alu.max),
                    [(m_br, 4 + hh, 5 + hh)], [(m_br, 4 + hh, 5 + hh)])
            vec(lambda e: e.tensor_copy(brow_bf[0:1, 0:2], brow[0:1, 4:6]),
                [(m_br, 4, 6)], [(m_brb, 0, 2)])

            for hh in range(2):
                qrow = arena_e[hh * 64:(hh + 1) * 64, 0:S]
                krow = arena_e[hh * 64:(hh + 1) * 64, S:2 * S]
                voff = D_VA if hh == 0 else D_VB
                mm(PS[6][:, 0:1], ones_r[0:1, 0:128], brow_bf[0:1, hh:hh + 1],
                   [(m_onr, *WHOLE), (m_brb, hh, hh + 1)], [(m_ps[6], 0, 1)])
                vec(lambda e: e.tensor_copy(smalls[:, 0:1], PS[6][:, 0:1]),
                    [(m_ps[6], 0, 1)], [(m_sm, 0, 1)])

                STB = (0, 1, 6)

                for qt in range(T512):
                    nk = 4 * qt + 4

                    def emit_st(kblk):
                        sl = STB[kblk % 3]
                        ptslot = kblk % 3
                        off = max(0, kblk * 128 - qt * 512)
                        mm(PS[sl][:, off:512],
                           krow[:, kblk * 128:(kblk + 1) * 128],
                           qrow[:, qt * 512 + off:(qt + 1) * 512],
                           [(m_e, S + kblk * 128, S + (kblk + 1) * 128),
                            (m_e, qt * 512 + off, (qt + 1) * 512)],
                           [(m_ps[sl], 0, 512)])
                        if kblk >= 4 * qt:
                            vec(lambda e, sl=sl, off=off: e.tensor_tensor(
                                    PS[sl][:, off:off + 128], PS[sl][:, off:off + 128],
                                    mask_t[:, :], Alu.add),
                                [(m_mask, *WHOLE), (m_ps[sl], off, off + 128)],
                                [(m_ps[sl], off, off + 128)])
                        sca(lambda e, sl=sl, ptslot=ptslot, off=off: e.activation(
                                arena_d[:, D_PT + ptslot * 512 + off:D_PT + (ptslot + 1) * 512],
                                PS[sl][:, off:512], Act.Exp, bias=smalls[:, 0:1]),
                            [(m_ps[sl], off, 512), (m_sm, 0, 1)],
                            [(m_d, D_PT + ptslot * 512 + off, D_PT + (ptslot + 1) * 512)])

                    def emit_pv(kblk):
                        ptslot = kblk % 3
                        for qb in range(4):
                            if 4 * qt + qb < kblk:
                                continue
                            pb = 2 + qb
                            mm(PS[pb][0:128, 0:65],
                               arena_d[:, D_PT + ptslot * 512 + qb * 128:D_PT + ptslot * 512 + (qb + 1) * 128],
                               arena_d[:, voff + kblk * 65:voff + (kblk + 1) * 65],
                               [(m_d, D_PT + ptslot * 512 + qb * 128, D_PT + ptslot * 512 + (qb + 1) * 128),
                                (m_d, voff + kblk * 65, voff + (kblk + 1) * 65)],
                               [(m_ps[pb], 0, 65)],
                               start=(kblk == 0), stop=(kblk == 4 * qt + qb), skip=True)

                    emit_st(0)
                    if nk > 1:
                        emit_st(1)
                    for kblk in range(2, nk):
                        emit_st(kblk)
                        emit_pv(kblk - 2)
                    emit_pv(nk - 2)
                    emit_pv(nk - 1)
                    for qb in range(4):
                        pb = 2 + qb
                        vec(lambda e, pb=pb: e.reciprocal(smalls[:, 1:2], PS[pb][:, 64:65]),
                            [(m_ps[pb], 64, 65)], [(m_sm, 1, 2)])
                        sca(lambda e, pb=pb, qt=qt, qb=qb, hh=hh: e.activation(
                                arena_d[:, D_AT + (qt * 4 + qb) * 128 + hh * 64:
                                        D_AT + (qt * 4 + qb) * 128 + hh * 64 + 64],
                                PS[pb][:, 0:64], Act.Copy, scale=smalls[:, 1:2]),
                            [(m_ps[pb], 0, 64), (m_sm, 1, 2)],
                            [(m_d, D_AT + (qt * 4 + qb) * 128 + hh * 64,
                              D_AT + (qt * 4 + qb) * 128 + hh * 64 + 64)])
            if dbg and s == 0 and pr == 0:
                dma("gpsimd", "dbgk", dbg_bir[0:1, 0:2 * D], b_in_row[0:1, :],
                    [(m_bir, 0, 2 * D)], [(m_dbg, 0, 1)])
                dma("gpsimd", "dbgk", dbg_bir[0:1, 2 * D:2 * D + 512], ones_r[0:1, :],
                    [(m_onr, 0, 512)], [(m_dbg, 0, 1)])
                dma("gpsimd", "dbgk", dbg_bir[0:1, 2 * D + 512:2 * D + 512 + D], bvrow[0:1, :],
                    [(m_bv, 0, D)], [(m_dbg, 0, 1)])
                dump_bf(dbg_qk, arena_e, m_e, 2 * S)
                dump_bf(dbg_v, arena_d, m_d, 4224)
                for col in range(0, T128 * 128, 2048):
                    dma("gpsimd", "dbgk", dbg_at[:, col:col + 2048],
                        arena_d[:, D_AT + col:D_AT + col + 2048],
                        [(m_d, D_AT + col, D_AT + col + 2048)], [(m_dbg, 0, 1)])
                dma("gpsimd", "dbgk", dbg_nc[:, 0:4], smalls[:, 0:4],
                    [(m_sm, 0, 4)], [(m_dbg, 0, 1)])
            for qb in range(T128):
                pb = 6 + (qb % 2)
                mm(PS[pb][:, 0:128], arena_d[:, D_AT + qb * 128:D_AT + (qb + 1) * 128],
                   id_bf[:, :],
                   [(m_d, D_AT + qb * 128, D_AT + (qb + 1) * 128), (m_id, *WHOLE)],
                   [(m_ps[pb], 0, 128)])
                vec(lambda e, qb=qb, pb=pb, pr=pr: e.tensor_copy(
                        arena_c[:, pr * S + qb * 128:pr * S + (qb + 1) * 128], PS[pb][:, 0:128]),
                    [(m_ps[pb], 0, 128)],
                    [(m_c, pr * S + qb * 128, pr * S + (qb + 1) * 128)])

        if dbg and s == 0:
            dump_bf(dbg_attn, arena_c, m_c, KC * S)
        if stop_after == "attn":
            stopped[0] = True
            break
        dma("gpsimd", "wo", arena_a[:, 0:KC * D].rearrange("p (a c) -> p a c", a=KC),
            dwout.rearrange("(a p) c -> p a c", p=128),
            [], [(m_a, 0, KC * D)])
        vec(lambda e, s=s: e.tensor_tensor(rows[0:1, R_GAB:R_GAB + D],
                                           rows[0:1, R_GA + s * D:R_GA + (s + 1) * D],
                                           borow[0:1, :], Alu.mult),
            [(m_rows, R_GA + s * D, R_GA + (s + 1) * D), (m_bo, *WHOLE)],
            [(m_rows, R_GAB, R_GAB + D)])
        for n in range(NOT):
            mm(PS[6][:, 0:OW], ones_r[0:1, 0:128],
               rows[0:1, R_GA + s * D + n * OW:R_GA + s * D + (n + 1) * OW],
               [(m_onr, *WHOLE), (m_rows, R_GA + s * D + n * OW, R_GA + s * D + (n + 1) * OW)],
               [(m_ps[6], 0, OW)])
            vec(lambda e, n=n: e.tensor_copy(bcast[:, D + n * OW:D + (n + 1) * OW],
                                             PS[6][:, 0:OW]),
                [(m_ps[6], 0, OW)], [(m_bc, D + n * OW, D + (n + 1) * OW)])
        for kc in range(KC):
            vec(lambda e, kc=kc: e.tensor_tensor(
                    arena_d[:, kc * D:(kc + 1) * D],
                    arena_a[:, kc * D:(kc + 1) * D],
                    bcast[:, D:2 * D], Alu.mult),
                [(m_a, kc * D, (kc + 1) * D), (m_bc, D, 2 * D)],
                [(m_d, kc * D, (kc + 1) * D)])

        def emit_h2_transposes(t, s=s):
            sl = t % 2
            hn = hn_ring[:, sl * D:(sl + 1) * D]
            for ch in range(KC):
                pb = 2 + (ch % 2)
                mm(PS[pb][:, 0:128], hn[:, ch * 128:(ch + 1) * 128], id_bf[:, :],
                   [(m_hn, sl * D + ch * 128, sl * D + (ch + 1) * 128), (m_id, *WHOLE)],
                   [(m_ps[pb], 0, 128)], start=True, stop=False)
                mm(PS[pb][:, 0:128], rows[0:1, R_SBP + ch * 128:R_SBP + (ch + 1) * 128],
                   ones_r[0:1, 0:128],
                   [(m_rows, R_SBP + ch * 128, R_SBP + (ch + 1) * 128), (m_onr, *WHOLE)],
                   [(m_ps[pb], 0, 128)], start=False, stop=True)
                sca(lambda e, ch=ch, t=t, pb=pb, s=s: e.activation(
                        arena_b[:, ch * S + t * 128:ch * S + (t + 1) * 128],
                        PS[pb][:, 0:128], Act.Copy,
                        scale=mods_fm[:, 2 * NK + ch * 2 + s:2 * NK + ch * 2 + s + 1]),
                    [(m_ps[pb], 0, 128), (m_mods, 2 * NK, 4 * NK)],
                    [(m_b, ch * S + t * 128, ch * S + (t + 1) * 128)])

        for t in range(T128):
            sl = t % 2
            xg = arena_g[:, sl * D:(sl + 1) * D]
            dma("sync", f"x{sl}", xg, dx[s, t * 128:(t + 1) * 128, :], [],
                [(m_g, sl * D, (sl + 1) * D)])
            for n in range(NOT):
                pb = n % 2
                for kc in range(KC):
                    mm(PS[pb][:, 0:OW],
                       arena_c[:, kc * S + t * 128:kc * S + (t + 1) * 128],
                       arena_d[:, kc * D + n * OW:kc * D + (n + 1) * OW],
                       [(m_c, kc * S + t * 128, kc * S + (t + 1) * 128),
                        (m_d, kc * D + n * OW, kc * D + (n + 1) * OW)],
                       [(m_ps[pb], 0, OW)],
                       start=(kc == 0), stop=False)
                mm(PS[pb][:, 0:OW], ones_r[:, 0:128],
                   rows[0:1, R_GAB + n * OW:R_GAB + (n + 1) * OW],
                   [(m_onr, *WHOLE), (m_rows, R_GAB + n * OW, R_GAB + (n + 1) * OW)],
                   [(m_ps[pb], 0, OW)], start=False, stop=True)
            xb = arena_g[:, (2 + sl) * D:(3 + sl) * D]
            for n in range(NOT):
                vec(lambda e, n=n, xb=xb, xg=xg: e.tensor_tensor(
                        xb[:, n * OW:(n + 1) * OW], PS[n % 2][:, 0:OW],
                        xg[:, n * OW:(n + 1) * OW], Alu.add),
                    [(m_ps[n % 2], 0, OW), (m_g, sl * D + n * OW, sl * D + (n + 1) * OW)],
                    [(m_g, (2 + sl) * D + n * OW, (2 + sl) * D + (n + 1) * OW)])
            dma("sync", f"sp{sl}", dxb[s, t * 128:(t + 1) * 128, :], xb,
                [(m_g, (2 + sl) * D, (3 + sl) * D)],
                [(m_xbs, s * S + t * 128, s * S + (t + 1) * 128)])
            hn = hn_ring[:, sl * D:(sl + 1) * D]
            ln_tile(xb, m_g, (2 + sl) * D, hn, sl * D)
            if t > 0:
                emit_h2_transposes(t - 1)
            if t == T128 - 1:
                emit_h2_transposes(t)

        if stop_after == "proj":
            stopped[0] = True
            break
        def _emit_routing_all():
         for t in range(T128):
            for kc in range(KC):
                mm(PS[4][:, 0:NE],
                   arena_b[:, kc * S + t * 128:kc * S + (t + 1) * 128],
                   gateT[:, kc * NE:(kc + 1) * NE],
                   [(m_b, kc * S + t * 128, kc * S + (t + 1) * 128), (m_gT, kc * NE, (kc + 1) * NE)],
                   [(m_ps[4], 0, NE)],
                   start=(kc == 0), stop=(kc == KC - 1))
            routing_tile(p, cfg, PS[4], m_ps[4], rout, m_rt, sc_bf, m_scb, vec, sca)
            mm(PS[5][0:NE, 0:128], sc_bf[:, 0:NE], id_bf[:, :],
               [(m_scb, 0, NE), (m_id, *WHOLE)], [(m_ps[5], 0, 128)])
            vec(lambda e, t=t: e.tensor_copy(c_fm[0:NE, t * 128:(t + 1) * 128],
                                             PS[5][0:NE, 0:128]),
                [(m_ps[5], 0, 128)], [(m_cfm, t * 128, (t + 1) * 128)])

        if dbg and s == 0:
            dump_bf(dbg_cfm, c_fm, m_cfm, S)
        if stop_after == "gate":
            stopped[0] = True
            break
        do_gb = stop_after != "moe_nogb"
        for ch in (range(KC) if do_gb else ()):
            mm(PS[6][0:1, 0:128], ga_fm[:, NK + ch * 2 + s:NK + ch * 2 + s + 1], id_bf[:, :],
               [(m_gafm, NK + ch * 2 + s, NK + ch * 2 + s + 1), (m_id, *WHOLE)],
               [(m_ps[6], 0, 128)])
            vec(lambda e, ch=ch: e.tensor_copy(rows[0:1, R_GB + ch * 128:R_GB + (ch + 1) * 128],
                                               PS[6][0:1, 0:128]),
                [(m_ps[6], 0, 128)], [(m_rows, R_GB + ch * 128, R_GB + (ch + 1) * 128)])
        for n in (range(NOT) if do_gb else ()):
            mm(PS[6][:, 0:OW], ones_r[0:1, 0:128], rows[0:1, R_GB + n * OW:R_GB + (n + 1) * OW],
               [(m_onr, *WHOLE), (m_rows, R_GB + n * OW, R_GB + (n + 1) * OW)],
               [(m_ps[6], 0, OW)])
            vec(lambda e, n=n: e.tensor_copy(bcast[:, D + n * OW:D + (n + 1) * OW],
                                             PS[6][:, 0:OW]),
                [(m_ps[6], 0, OW)], [(m_bc, D + n * OW, D + (n + 1) * OW)])

        SW = INTER
        ICL = SW // 128
        SLAB = KC * SW + KC * SW + ICL * D
        W1O, W3O, W2O = 0, KC * SW, 2 * KC * SW
        slabs = [("sh", 0), ("sh", 1)] + [("e", e) for e in range(NE)]
        if stop_after in ("moe_dma", "moe_g0t0", "moe_g0", "moe_nogb"):
            slabs = slabs[:1]
        if stop_after is not None:
            _emit_routing_all()
        for si, (kind, idx) in enumerate(slabs):
            sl = si % 2
            base = sl * SLAB
            key = f"sl{sl}"
            if kind == "e":
                srcs = (dw1[idx], dw3[idx], dw2[idx])
            else:
                h0 = idx * SW
                srcs = (dws1[:, h0:h0 + SW], dws3[:, h0:h0 + SW], dws2[h0:h0 + SW, :])
            for (wo, wa_dim, wid, src) in ((W1O, KC, KC * SW, srcs[0]),
                                           (W3O, KC, KC * SW, srcs[1]),
                                           (W2O, ICL, ICL * D, srcs[2])):
                dma("gpsimd", key,
                    arena_a[:, base + wo:base + wo + wid].rearrange(
                        "p (a c) -> p a c", a=wa_dim),
                    src.rearrange("(a p) c -> p a c", p=128),
                    [], [(m_a, base + wo, base + wo + wid)])
            for icl in range(ICL):
                vec(lambda e, base=base, icl=icl: e.tensor_tensor(
                        arena_a[:, base + W2O + icl * D:base + W2O + (icl + 1) * D],
                        arena_a[:, base + W2O + icl * D:base + W2O + (icl + 1) * D],
                        bcast[:, D:2 * D], Alu.mult),
                    [(m_a, base + W2O + icl * D, base + W2O + (icl + 1) * D), (m_bc, D, 2 * D)],
                    [(m_a, base + W2O + icl * D, base + W2O + (icl + 1) * D)])
            if si == 1 and stop_after is None:
                _emit_routing_all()
            for t in range(T512):
                asl = t % 2
                A0 = asl * ICL * 512
                if kind == "e":
                    mm(PS[6][:, 0:512],
                       sel16[:, idx * 128:(idx + 1) * 128],
                       c_fm[0:NE, t * 512:(t + 1) * 512],
                       [(m_sel, idx * 128, (idx + 1) * 128), (m_cfm, t * 512, (t + 1) * 512)],
                       [(m_ps[6], 0, 512)])
                    vec(lambda e, asl=asl: e.tensor_copy(
                            cb_ring[:, asl * 512:(asl + 1) * 512], PS[6][:, 0:512]),
                        [(m_ps[6], 0, 512)], [(m_cb, asl * 512, (asl + 1) * 512)])
                for icl in range(ICL):
                    p1b, p3b = icl % 2, 2 + icl % 2
                    for kc in range(KC):
                        mm(PS[p1b][:, 0:512],
                           arena_a[:, base + W1O + kc * SW + icl * 128:base + W1O + kc * SW + (icl + 1) * 128],
                           arena_b[:, kc * S + t * 512:kc * S + (t + 1) * 512],
                           [(m_a, base + W1O + kc * SW + icl * 128, base + W1O + kc * SW + (icl + 1) * 128),
                            (m_b, kc * S + t * 512, kc * S + (t + 1) * 512)],
                           [(m_ps[p1b], 0, 512)],
                           start=(kc == 0), stop=(kc == KC - 1))
                    for kc in range(KC):
                        mm(PS[p3b][:, 0:512],
                           arena_a[:, base + W3O + kc * SW + icl * 128:base + W3O + kc * SW + (icl + 1) * 128],
                           arena_b[:, kc * S + t * 512:kc * S + (t + 1) * 512],
                           [(m_a, base + W3O + kc * SW + icl * 128, base + W3O + kc * SW + (icl + 1) * 128),
                            (m_b, kc * S + t * 512, kc * S + (t + 1) * 512)],
                           [(m_ps[p3b], 0, 512)],
                           start=(kc == 0), stop=(kc == KC - 1))
                    tsl = icl % 2
                    t1 = moe_tmp[:, tsl * 512:(tsl + 1) * 512]
                    t3 = moe_tmp[:, 1024 + tsl * 512:1024 + (tsl + 1) * 512]
                    if kind == "sh":
                        vec(lambda e, t3=t3, p3b=p3b: e.tensor_copy(t3, PS[p3b][:, 0:512]),
                            [(m_ps[p3b], 0, 512)], [(m_mt, 1024 + tsl * 512, 1024 + (tsl + 1) * 512)])
                    else:
                        vec(lambda e, t3=t3, p3b=p3b, asl=asl: e.tensor_tensor(
                                t3, PS[p3b][:, 0:512],
                                cb_ring[:, asl * 512:(asl + 1) * 512], Alu.mult),
                            [(m_ps[p3b], 0, 512), (m_cb, asl * 512, (asl + 1) * 512)],
                            [(m_mt, 1024 + tsl * 512, 1024 + (tsl + 1) * 512)])
                    sca(lambda e, t1=t1, p1b=p1b: e.activation(t1, PS[p1b][:, 0:512], Act.Silu),
                        [(m_ps[p1b], 0, 512)], [(m_mt, tsl * 512, (tsl + 1) * 512)])
                    vec(lambda e, t1=t1, t3=t3, icl=icl, A0=A0: e.tensor_tensor(
                            arena_d[:, A0 + icl * 512:A0 + (icl + 1) * 512], t1, t3, Alu.mult),
                        [(m_mt, tsl * 512, (tsl + 1) * 512),
                         (m_mt, 1024 + tsl * 512, 1024 + (tsl + 1) * 512)],
                        [(m_d, A0 + icl * 512, A0 + (icl + 1) * 512)])
                for mtile in range(4):
                    for n in range(NOT):
                        pb = 4 + ((mtile * NOT + n) % 2)
                        for icl in range(ICL):
                            mm(PS[pb][:, 0:OW],
                               arena_d[:, A0 + icl * 512 + mtile * 128:A0 + icl * 512 + (mtile + 1) * 128],
                               arena_a[:, base + W2O + icl * D + n * OW:base + W2O + icl * D + (n + 1) * OW],
                               [(m_d, A0 + icl * 512 + mtile * 128, A0 + icl * 512 + (mtile + 1) * 128),
                                (m_a, base + W2O + icl * D + n * OW, base + W2O + icl * D + (n + 1) * OW)],
                               [(m_ps[pb], 0, OW)],
                               start=(icl == 0), stop=(icl == ICL - 1))
                        ycol = (t * 4 + mtile) * D + n * OW
                        if si == 0:
                            vec(lambda e, pb=pb, ycol=ycol: e.tensor_copy(
                                    arena_c[:, ycol:ycol + OW], PS[pb][:, 0:OW]),
                                [(m_ps[pb], 0, OW)], [(m_c, ycol, ycol + OW)])
                        else:
                            vec(lambda e, pb=pb, ycol=ycol: e.tensor_tensor(
                                    arena_c[:, ycol:ycol + OW], PS[pb][:, 0:OW],
                                    arena_c[:, ycol:ycol + OW], Alu.add),
                                [(m_ps[pb], 0, OW), (m_c, ycol, ycol + OW)],
                                [(m_c, ycol, ycol + OW)])

        def emit_final_tile(t, s=s):
            rsl = 2 + t % 2
            osl = 2 + (t + 1) % 2
            xg = arena_g[:, rsl * D:(rsl + 1) * D]
            dma("sync", f"fx{t % 2}", xg, dxb[s, t * 128:(t + 1) * 128, :],
                [(m_xbs, s * S + t * 128, s * S + (t + 1) * 128)],
                [(m_g, rsl * D, (rsl + 1) * D)])
            vec(lambda e, t=t, xg=xg, osl=osl: e.tensor_tensor(
                    arena_g[:, osl * D:(osl + 1) * D], xg,
                    arena_c[:, t * D:(t + 1) * D], Alu.add),
                [(m_g, rsl * D, (rsl + 1) * D), (m_c, t * D, (t + 1) * D)],
                [(m_g, osl * D, (osl + 1) * D)])
            dma("sync", "st", dout[s, t * 128:(t + 1) * 128, :],
                arena_g[:, osl * D:(osl + 1) * D],
                [(m_g, osl * D, (osl + 1) * D)],
                [(m_out, s * S + t * 128, s * S + (t + 1) * 128)])

        pending_final = emit_final_tile
        if s == SEQS - 1 or stop_after is not None:
            for t in range(T128):
                emit_final_tile(t)
            pending_final = None

    if not stopped[0]:
        p.do("sync", None, r=[(m_out, 0, SEQS * S)])

    plans, sem_names = p.finalize()

    sems = {}
    for sn in sem_names:
        g = nc.semaphore(f"sem_{sn}")
        sems[sn] = g.__enter__()
        ctx.append(g)

    def mk(e):
        def run(eng):
            for waits, emit, inc, inc_sem, drain in plans[e]:
                for (sn, v) in waits:
                    eng.wait_ge(sems[sn], v)
                if emit is not None:
                    inst = emit(eng)
                    if inc:
                        inst.then_inc(sems[inc_sem], inc)
                    if drain:
                        eng.drain()
        return run

    blk = nc.Block()
    block = blk.__enter__()
    block.tensor(mk("tensor"))
    block.vector(mk("vector"))
    block.scalar(mk("scalar"))
    block.sync(mk("sync"))
    block.gpsimd(mk("gpsimd"))
    blk.__exit__(None, None, None)

    for g in reversed(ctx):
        g.__exit__(None, None, None)
    return nc


def routing_tile(p, cfg, psl, m_psl, rout, m_rt, sc_bf, m_scb, vec, sca):
    NE = cfg.NE
    G0, W0, E0, S0, M0, T0 = 0, 4, 20, 36, 52, 68
    vec(lambda e: e.reduce_max(rout[:, T0:T0 + 1], psl[:, 0:NE], Ax.X, negate=True),
        [(m_psl, 0, NE)], [(m_rt, T0, T0 + 1)], drain=True)
    sca(lambda e: e.activation(rout[:, E0:E0 + NE], psl[:, 0:NE], Act.Exp,
                               bias=rout[:, T0:T0 + 1]),
        [(m_psl, 0, NE), (m_rt, T0, T0 + 1)], [(m_rt, E0, E0 + NE)])
    vec(lambda e: e.reduce_sum(rout[:, T0 + 1:T0 + 2], rout[:, E0:E0 + NE], Ax.X),
        [(m_rt, E0, E0 + NE)], [(m_rt, T0 + 1, T0 + 2)], drain=True)
    vec(lambda e: e.reciprocal(rout[:, T0 + 2:T0 + 3], rout[:, T0 + 1:T0 + 2]),
        [(m_rt, T0 + 1, T0 + 2)], [(m_rt, T0 + 2, T0 + 3)])
    vec(lambda e: e.tensor_scalar_mul(rout[:, S0:S0 + NE], rout[:, E0:E0 + NE],
                                      rout[:, T0 + 2:T0 + 3]),
        [(m_rt, E0, E0 + NE), (m_rt, T0 + 2, T0 + 3)], [(m_rt, S0, S0 + NE)])
    for r in range(4):
        vec(lambda e, r=r: e.reduce_max(rout[:, G0 + r:G0 + r + 1],
                                        rout[:, S0 + 4 * r:S0 + 4 * r + 4], Ax.X),
            [(m_rt, S0 + 4 * r, S0 + 4 * r + 4)], [(m_rt, G0 + r, G0 + r + 1)], drain=True)
    vec(lambda e: e.reduce_max(rout[:, T0 + 3:T0 + 4], rout[:, G0:G0 + 4], Ax.X),
        [(m_rt, G0, G0 + 4)], [(m_rt, T0 + 3, T0 + 4)], drain=True)
    vec(lambda e: e.tensor_scalar(rout[:, W0:W0 + 4], rout[:, G0:G0 + 4],
                                  rout[:, T0 + 3:T0 + 4], None, Alu.is_ge),
        [(m_rt, G0, G0 + 4), (m_rt, T0 + 3, T0 + 4)], [(m_rt, W0, W0 + 4)])
    vec(lambda e: e.scalar_tensor_tensor(rout[:, W0:W0 + 4], rout[:, W0:W0 + 4], KNOCK_NEG,
                                         rout[:, G0:G0 + 4], Alu.mult, Alu.add),
        [(m_rt, W0, W0 + 4), (m_rt, G0, G0 + 4)], [(m_rt, W0, W0 + 4)])
    vec(lambda e: e.reduce_max(rout[:, T0 + 4:T0 + 5], rout[:, W0:W0 + 4], Ax.X),
        [(m_rt, W0, W0 + 4)], [(m_rt, T0 + 4, T0 + 5)], drain=True)
    vec(lambda e: e.tensor_scalar(rout[:, G0:G0 + 4], rout[:, G0:G0 + 4],
                                  rout[:, T0 + 4:T0 + 5], None, Alu.is_ge),
        [(m_rt, G0, G0 + 4), (m_rt, T0 + 4, T0 + 5)], [(m_rt, G0, G0 + 4)])
    for r in range(4):
        vec(lambda e, r=r: e.tensor_scalar_mul(rout[:, M0 + 4 * r:M0 + 4 * r + 4],
                                               rout[:, S0 + 4 * r:S0 + 4 * r + 4],
                                               rout[:, G0 + r:G0 + r + 1]),
            [(m_rt, S0 + 4 * r, S0 + 4 * r + 4), (m_rt, G0 + r, G0 + r + 1)],
            [(m_rt, M0 + 4 * r, M0 + 4 * r + 4)])
    vec(lambda e: e.tensor_copy(rout[:, W0:W0 + NE], rout[:, M0:M0 + NE]),
        [(m_rt, M0, M0 + NE)], [(m_rt, W0, W0 + NE)])
    for k in range(3):
        vec(lambda e: e.reduce_max(rout[:, T0 + 5:T0 + 6], rout[:, W0:W0 + NE], Ax.X),
            [(m_rt, W0, W0 + NE)], [(m_rt, T0 + 5, T0 + 6)], drain=True)
        vec(lambda e: e.tensor_scalar(rout[:, E0:E0 + NE], rout[:, W0:W0 + NE],
                                      rout[:, T0 + 5:T0 + 6], None, Alu.is_ge),
            [(m_rt, W0, W0 + NE), (m_rt, T0 + 5, T0 + 6)], [(m_rt, E0, E0 + NE)])
        vec(lambda e: e.scalar_tensor_tensor(rout[:, W0:W0 + NE], rout[:, E0:E0 + NE], KNOCK_NEG,
                                             rout[:, W0:W0 + NE], Alu.mult, Alu.add),
            [(m_rt, E0, E0 + NE), (m_rt, W0, W0 + NE)], [(m_rt, W0, W0 + NE)])
    vec(lambda e: e.reduce_max(rout[:, T0 + 5:T0 + 6], rout[:, W0:W0 + NE], Ax.X),
        [(m_rt, W0, W0 + NE)], [(m_rt, T0 + 5, T0 + 6)], drain=True)
    vec(lambda e: e.tensor_scalar(rout[:, E0:E0 + NE], rout[:, M0:M0 + NE],
                                  rout[:, T0 + 5:T0 + 6], None, Alu.is_ge),
        [(m_rt, M0, M0 + NE), (m_rt, T0 + 5, T0 + 6)], [(m_rt, E0, E0 + NE)])
    vec(lambda e: e.tensor_tensor(sc_bf[:, 0:NE], rout[:, S0:S0 + NE],
                                  rout[:, E0:E0 + NE], Alu.mult),
        [(m_rt, S0, S0 + NE), (m_rt, E0, E0 + NE)], [(m_scb, 0, NE)])



def mk_consts(cfg):
    NE = cfg.NE
    id128 = np.eye(128, dtype=ml_dtypes.bfloat16)
    i = np.arange(128)
    mask = np.where(i[None, :] >= i[:, None], 0.0, MASK_NEG).astype(np.float32)
    sel = np.zeros((NE, NE * 128), dtype=ml_dtypes.bfloat16)
    for e in range(NE):
        sel[e, e * 128:(e + 1) * 128] = 1.0
    return id128, mask, sel


def make_inputs(cfg, **a):
    D = cfg.D
    x = np.ascontiguousarray(np.asarray(a["x"], dtype=np.float32))
    w_in = np.asarray(a["w_in"], dtype=np.float32)
    wq = w_in[:, 0:D].reshape(D, cfg.NPAIR, 128)
    wk = w_in[:, D:2 * D].reshape(D, cfg.NPAIR, 128)
    wv = w_in[:, 2 * D:3 * D].reshape(D, cfg.NPAIR, 128)
    w_in_re = np.ascontiguousarray(np.concatenate([wq, wk, wv], axis=2))
    id128, mask, sel = mk_consts(cfg)
    shared = dict(
        w_in_re=w_in_re,
        b_in=np.ascontiguousarray(np.asarray(a["b_in"], dtype=np.float32)),
        w_out=np.ascontiguousarray(np.asarray(a["w_out"], dtype=np.float32)),
        b_out=np.ascontiguousarray(np.asarray(a["b_out"], dtype=np.float32)),
        gate_w=np.ascontiguousarray(np.asarray(a["gate_w"], dtype=np.float32)),
        w1=np.ascontiguousarray(np.asarray(a["w1"], dtype=np.float32)),
        w2=np.ascontiguousarray(np.asarray(a["w2"], dtype=np.float32)),
        w3=np.ascontiguousarray(np.asarray(a["w3"], dtype=np.float32)),
        ws1=np.ascontiguousarray(np.asarray(a["ws1"], dtype=np.float32)),
        ws2=np.ascontiguousarray(np.asarray(a["ws2"], dtype=np.float32)),
        ws3=np.ascontiguousarray(np.asarray(a["ws3"], dtype=np.float32)),
        wmod=np.ascontiguousarray(np.asarray(a["wmod"], dtype=np.float32)),
        id128d=id128, mask_td=mask, sel16d=sel,
    )
    cvec = np.asarray(a["cvec"], dtype=np.float32)
    return x, cvec, shared


def kernel(L0_attn_x, L0_attn_w_in, L0_attn_b_in, L0_attn_w_out, L0_attn_b_out,
           L0_ffn_gate_w, L0_ffn_w1, L0_ffn_w2, L0_ffn_w3, L0_ffn_ws1, L0_ffn_ws2,
           L0_ffn_ws3, L0_c, L0_wmod):
    cfg = CFG()
    x, cvec, shared = make_inputs(
        cfg, x=L0_attn_x, w_in=L0_attn_w_in, b_in=L0_attn_b_in, w_out=L0_attn_w_out,
        b_out=L0_attn_b_out, gate_w=L0_ffn_gate_w, w1=L0_ffn_w1, w2=L0_ffn_w2,
        w3=L0_ffn_w3, ws1=L0_ffn_ws1, ws2=L0_ffn_ws2, ws3=L0_ffn_ws3, cvec=L0_c,
        wmod=L0_wmod)
    nc = build_program(cfg, paranoid={"vector"})
    in_maps = []
    for c in range(NCORES):
        m = dict(shared)
        m["x"] = np.ascontiguousarray(x[c * cfg.SEQS:(c + 1) * cfg.SEQS])
        m["cvec"] = np.ascontiguousarray(cvec[c * cfg.SEQS:(c + 1) * cfg.SEQS])
        in_maps.append(m)
    res = run_bass_kernel_spmd(nc, in_maps, core_ids=list(range(NCORES)))
    out = np.concatenate([res.results[c]["out"] for c in range(NCORES)], axis=0)
    return np.ascontiguousarray(out.astype(np.float32))


# version 11 (final)
# speedup vs baseline: 2.3904x; 1.0248x over previous
import os

import numpy as np
import ml_dtypes

import concourse.bass as bass
import concourse.mybir as mybir
from concourse.bass_utils import run_bass_kernel_spmd

F32 = mybir.dt.float32
BF16 = mybir.dt.bfloat16
Alu = mybir.AluOpType
Act = mybir.ActivationFunctionType
Ax = mybir.AxisListType

NCORES = 8
MASK_NEG = -30000.0
KNOCK_NEG = -1e30


class CFG:
    SEQS = 2
    S = 2048
    D = 1024
    H = 16
    NE = 16
    INTER = 512
    LN_EPS = 1e-5
    SHIFT_CLAMP = -60.0

    HD = property(lambda s: s.D // s.H)
    NPAIR = property(lambda s: s.H // 2)
    KC = property(lambda s: s.D // 128)
    T128 = property(lambda s: s.S // 128)
    T512 = property(lambda s: s.S // 512)
    SHARED = property(lambda s: 2 * s.INTER)
    GW = property(lambda s: 2 * s.INTER)
    ICG = property(lambda s: s.GW // 128)
    NGRP = property(lambda s: s.NE // 2 + 1)
    MODC = property(lambda s: 6 * s.D // 128)



ENGINES = ("tensor", "vector", "scalar", "sync", "gpsimd")
DMA_ENGINES = frozenset(("sync", "gpsimd"))


class Tok:
    __slots__ = ("eng", "idx", "gid")

    def __init__(self, eng, idx, gid):
        self.eng = eng
        self.idx = idx
        self.gid = gid


class Mem:

    __slots__ = ("name", "writers", "readers", "coarse")

    def __init__(self, name, coarse=False):
        self.name = name
        self.writers = []
        self.readers = []
        self.coarse = coarse

    def on_read(self, lo, hi, tok, deps):
        if self.coarse:
            lo, hi = 0, 1 << 30
        for (l, h, t) in self.writers:
            if l < hi and lo < h:
                deps.append(t)
        rd = self.readers
        keep = [e for e in rd if not (e[2].eng == tok.eng and lo <= e[0] and e[1] <= hi)]
        keep.append([lo, hi, tok])
        self.readers = keep

    def on_write(self, lo, hi, tok, deps):
        if self.coarse:
            lo, hi = 0, 1 << 30
        for (l, h, t) in self.writers:
            if l < hi and lo < h:
                deps.append(t)
        for (l, h, t) in self.readers:
            if l < hi and lo < h:
                deps.append(t)
        self.writers = [e for e in self.writers if not (lo <= e[0] and e[1] <= hi)]
        self.readers = [e for e in self.readers if not (lo <= e[0] and e[1] <= hi)]
        self.writers.append([lo, hi, tok])


class Prog:

    def __init__(self, paranoid=False):
        if paranoid is True:
            paranoid = set(ENGINES)
        elif not paranoid:
            paranoid = set()
        self.paranoid = paranoid
        self.streams = {e: [] for e in ENGINES}
        self.keys = []
        self.gid = 0

    def do(self, eng, emit, r=(), w=(), key=None, drain=False):
        if eng in DMA_ENGINES and emit is not None:
            assert key is not None, "DMA op requires a semaphore key"
            if key not in self.keys:
                self.keys.append(key)
        self.gid += 1
        tok = Tok(eng, len(self.streams[eng]), self.gid)
        deps = []
        for (m, lo, hi) in r:
            m.on_read(lo, hi, tok, deps)
        for (m, lo, hi) in w:
            m.on_write(lo, hi, tok, deps)
        deps = [d for d in deps if d is not tok]
        self.streams[eng].append([emit, deps, False, key, tok.gid, drain])
        return tok

    def finalize(self):
        import bisect
        for e in ENGINES:
            for op in self.streams[e]:
                if e in DMA_ENGINES and op[0] is not None:
                    op[2] = True
                for d in op[1]:
                    de = d.eng
                    if de in DMA_ENGINES:
                        continue
                    if de != e or de in self.paranoid:
                        self.streams[de][d.idx][2] = True
        key_gids = {k: [] for k in self.keys}
        for e in DMA_ENGINES:
            for op in self.streams[e]:
                if op[0] is not None:
                    key_gids[op[3]].append(op[4])
        for k in key_gids:
            key_gids[k].sort()
        ordinals = {e: [] for e in ENGINES}
        cnt_e = {e: 0 for e in ENGINES}
        for e in ENGINES:
            for op in self.streams[e]:
                if e not in DMA_ENGINES and op[2]:
                    cnt_e[e] += 1
                ordinals[e].append(cnt_e[e])
        plans = {}
        for e in ENGINES:
            plan = []
            last_wait = {}
            for emit, deps, marked, key, gid, drain in self.streams[e]:
                need = {}
                for d in deps:
                    de = d.eng
                    if de in DMA_ENGINES:
                        dk = self.streams[de][d.idx][3]
                        v = bisect.bisect_left(key_gids[dk], gid)
                        if v > 0 and v > need.get(dk, 0):
                            need[dk] = v
                    else:
                        if de == e and de not in self.paranoid:
                            continue
                        v = ordinals[de][d.idx]
                        if v > 0 and v > need.get(de, 0):
                            need[de] = v
                waits = []
                for sn, v in need.items():
                    if last_wait.get(sn, 0) < v:
                        waits.append((sn, v * (16 if sn in key_gids else 1)))
                        last_wait[sn] = v
                inc_sem = key if e in DMA_ENGINES else e
                inc = (16 if e in DMA_ENGINES else 1) if marked else 0
                plan.append((waits, emit, inc, inc_sem, drain))
            plans[e] = plan
        return plans, [e for e in ENGINES if e not in DMA_ENGINES] + self.keys



def build_program(cfg, paranoid=False, stop_after=None, dbg=False):
    nc = bass.Bass("TRN2", target_bir_lowering=False,
                   detect_race_conditions=bool(int(os.environ.get("KBASS_RACED", "0"))),
                   dynamic_dma_scratch_size=16384)
    SEQS, S, D, NE, INTER = cfg.SEQS, cfg.S, cfg.D, cfg.NE, cfg.INTER
    KC, NPAIR, T128, T512 = cfg.KC, cfg.NPAIR, cfg.T128, cfg.T512
    ICG, NGRP, MODC = cfg.ICG, cfg.NGRP, cfg.MODC
    NK = 2 * KC
    OW = min(512, D)
    NOT = D // OW

    dx = nc.dram_tensor("x", [SEQS, S, D], F32, kind="ExternalInput").ap()
    dwin = nc.dram_tensor("w_in_re", [D, NPAIR, 384], F32, kind="ExternalInput").ap()
    dbin = nc.dram_tensor("b_in", [3 * D], F32, kind="ExternalInput").ap()
    dwout = nc.dram_tensor("w_out", [D, D], F32, kind="ExternalInput").ap()
    dbout = nc.dram_tensor("b_out", [D], F32, kind="ExternalInput").ap()
    dgate = nc.dram_tensor("gate_w", [NE, D], F32, kind="ExternalInput").ap()
    dw1 = nc.dram_tensor("w1", [NE, D, INTER], F32, kind="ExternalInput").ap()
    dw2 = nc.dram_tensor("w2", [NE, INTER, D], F32, kind="ExternalInput").ap()
    dw3 = nc.dram_tensor("w3", [NE, D, INTER], F32, kind="ExternalInput").ap()
    dws1 = nc.dram_tensor("ws1", [D, cfg.SHARED], F32, kind="ExternalInput").ap()
    dws2 = nc.dram_tensor("ws2", [cfg.SHARED, D], F32, kind="ExternalInput").ap()
    dws3 = nc.dram_tensor("ws3", [D, cfg.SHARED], F32, kind="ExternalInput").ap()
    dc = nc.dram_tensor("cvec", [SEQS, D], F32, kind="ExternalInput").ap()
    dwmod = nc.dram_tensor("wmod", [D, 6 * D], F32, kind="ExternalInput").ap()
    did = nc.dram_tensor("id128d", [128, 128], BF16, kind="ExternalInput").ap()
    dmask = nc.dram_tensor("mask_td", [128, 128], F32, kind="ExternalInput").ap()
    dsel = nc.dram_tensor("sel16d", [NE, NE * 128], BF16, kind="ExternalInput").ap()
    dout = nc.dram_tensor("out", [SEQS, S, D], F32, kind="ExternalOutput").ap()
    dxb = nc.dram_tensor("xb_scr", [SEQS, S, D], F32,
                         kind="ExternalOutput" if dbg else "Internal").ap()
    if dbg:
        dbg_mods = nc.dram_tensor("dbg_mods", [128, 4 * NK], F32, kind="ExternalOutput").ap()
        dbg_h = nc.dram_tensor("dbg_h", [128, KC * S], F32, kind="ExternalOutput").ap()
        dbg_attn = nc.dram_tensor("dbg_attn", [128, KC * S], F32, kind="ExternalOutput").ap()
        dbg_cfm = nc.dram_tensor("dbg_cfm", [NE, S], F32, kind="ExternalOutput").ap()
        dbg_y = nc.dram_tensor("dbg_y", [128, KC * S], F32, kind="ExternalOutput").ap()
        dbg_f32 = nc.dram_tensor("dbg_f32", [128, KC * S], F32, kind="Internal").ap()
        dbg_qk = nc.dram_tensor("dbg_qk", [128, 2 * S], F32, kind="ExternalOutput").ap()
        dbg_v = nc.dram_tensor("dbg_v", [128, 4224], F32, kind="ExternalOutput").ap()
        dbg_at = nc.dram_tensor("dbg_at", [128, 2048], F32, kind="ExternalOutput").ap()
        dbg_nc = nc.dram_tensor("dbg_nc", [128, 4], F32, kind="ExternalOutput").ap()
        dbg_bir = nc.dram_tensor("dbg_bir", [1, 2 * D + 512 + D], F32, kind="ExternalOutput").ap()
        m_dbg = Mem("dbg")

        def dump_bf(dst, src_ap, mem, width):
            for col in range(0, width, 4096):
                wdt = min(4096, width - col)
                dma("gpsimd", "dbgk", dst[:, col:col + wdt], src_ap[:, col:col + wdt],
                    [(mem, col, col + wdt)], [(m_dbg, 0, 1)])

    p = Prog(paranoid=paranoid)
    ctx = []

    def sb(name, shape, dtype):
        g = nc.sbuf_tensor(name, shape, dtype)
        t = g.__enter__()
        ctx.append(g)
        return t, Mem(name)

    AW = max(3 * KC * cfg.GW, 2 * KC * 384, 2 * 6 * D, KC * D)
    arena_a, m_a = sb("arena_a", [128, AW], BF16)
    GWD = cfg.GW
    assert GWD == D
    WAB = KC * GWD
    A_WA, A_WB, A_WC = 0, WAB, 2 * WAB
    arena_b, m_b = sb("arena_b", [128, KC * S], BF16)
    arena_c, m_c = sb("arena_c", [128, KC * S], BF16)
    D_PT, D_SQ = 0, 1536
    D_VA, D_VB = 2560, 2560 + T128 * 65
    D_AT = 2560 + 2 * T128 * 65
    DW = max(2 * ICG * 512, D_AT + T128 * 128, KC * 1024)
    arena_d, m_d = sb("arena_d", [128, DW], BF16)
    arena_e, m_e = sb("arena_e", [128, 2 * S], BF16)
    arena_g, m_g = sb("arena_g", [128, 4 * D], F32)
    hn_ring, m_hn = sb("hn_ring", [128, 2 * D], BF16)
    moe_tmp, m_mt = sb("moe_tmp", [128, 2048], BF16)
    cb_ring, m_cb = sb("cb_ring", [128, 2048], BF16)
    c_fm, m_cfm = sb("c_fm", [NE, S], BF16)
    bcast, m_bc = sb("bcast", [128, 2 * D], BF16)
    rows, m_rows = sb("rows", [1, 6 * D], BF16)
    R_GA, R_GAB, R_GB, R_SAP, R_SBP = 0, 2 * D, 3 * D, 4 * D, 5 * D
    bvrow, m_bv = sb("bvrow", [1, D], BF16)
    borow, m_bo = sb("borow", [1, D], BF16)
    mods_fm, m_mods = sb("mods_fm", [128, 4 * NK], F32)
    ga_fm, m_gafm = sb("ga_fm", [128, 2 * NK], BF16)
    id_bf, m_id = sb("id_bf", [128, 128], BF16)
    mask_t, m_mask = sb("mask_t", [128, 128], F32)
    sel16, m_sel = sb("sel16", [NE, NE * 128], BF16)
    gw_bf, m_gw = sb("gw_bf", [NE, D], BF16)
    gateT, m_gT = sb("gateT", [128, KC * NE], BF16)
    c2row, m_c2 = sb("c2row", [2, D], F32)
    c2bf, m_c2b = sb("c2bf", [2, D], BF16)
    csilu, m_cs = sb("csilu", [128, NK], BF16)
    ones_r, m_onr = sb("ones_r", [1, 512], BF16)
    ones_c, m_onc = sb("ones_c", [128, 1], BF16)
    rout, m_rt = sb("rout", [128, 128], F32)
    sc_bf, m_scb = sb("sc_bf", [128, NE], BF16)
    smalls, m_sm = sb("smalls", [128, 16], F32)
    brow, m_br = sb("brow", [1, 16], F32)
    brow_bf, m_brb = sb("brow_bf", [1, 4], BF16)
    b_in_row, m_bir = sb("b_in_row", [1, 2 * D], BF16)
    sap_bf, m_sap = sb("sap_bf", [128, 2 * NK], BF16)

    PS = []
    m_ps = []
    for i in range(8):
        g = nc.psum_tensor(f"ps{i}", [128, 512], F32)
        PS.append(g.__enter__())
        ctx.append(g)
        m_ps.append(Mem(f"ps{i}", coarse=True))
    m_xbs = Mem("xb_scr")
    m_out = Mem("out")

    WHOLE = (0, 1 << 30)

    def mm(out_ap, lhsT_ap, rhs_ap, r, w, start=True, stop=True, skip=False):
        lsh, rsh, osh = lhsT_ap.shape, rhs_ap.shape, out_ap.shape
        assert lsh[0] == rsh[0], f"mm K mismatch {lsh} vs {rsh}"
        assert lsh[1] == osh[0], f"mm M mismatch lhsT {lsh} out {osh}"
        assert rsh[1] == osh[1], f"mm N mismatch rhs {rsh} out {osh}"
        return p.do("tensor",
                    lambda e, o=out_ap, l=lhsT_ap, rr=rhs_ap, st=start, sp=stop, sk=skip:
                        e.matmul(o, l, rr, start=st, stop=sp, skip_group_check=sk),
                    r=r, w=w)

    def vec(fn, r, w, drain=False):
        return p.do("vector", fn, r=r, w=w, drain=drain)

    def sca(fn, r, w):
        return p.do("scalar", fn, r=r, w=w)

    def dma(q, key, out_ap, in_ap, r, w):
        return p.do(q, lambda e, o=out_ap, i=in_ap: e.dma_start(o, i), r=r, w=w, key=key)

    dma("sync", "k_id", id_bf[:, :], did, [], [(m_id, *WHOLE)])
    dma("sync", "k_mask", mask_t[:, :], dmask, [], [(m_mask, *WHOLE)])
    dma("sync", "k_sel", sel16[:, :], dsel, [], [(m_sel, *WHOLE)])
    if SEQS < 2:
        vec(lambda e: e.memset(c2row[:, :], 0.0), [], [(m_c2, *WHOLE)])
    dma("sync", "k_c2", c2row[0:SEQS, :], dc[:, :], [],
        [(m_c2, 0, D if SEQS == 2 else 1)])
    dma("gpsimd", "k_bo", borow[0:1, :], dbout.rearrange("(o d) -> o d", o=1), [], [(m_bo, *WHOLE)])
    dma("gpsimd", "k_bir", b_in_row[0:1, :], dbin.rearrange("(o d) -> o d", o=1)[0:1, 0:2 * D],
        [], [(m_bir, *WHOLE)])
    dma("gpsimd", "k_bv", bvrow[0:1, :], dbin.rearrange("(o d) -> o d", o=1)[0:1, 2 * D:3 * D],
        [], [(m_bv, *WHOLE)])
    dma("gpsimd", "k_gw", gw_bf[:, :], dgate, [], [(m_gw, *WHOLE)])

    vec(lambda e: e.memset(ones_r[:, :], 1.0), [], [(m_onr, *WHOLE)])
    vec(lambda e: e.memset(smalls[:, 6:7], cfg.LN_EPS), [], [(m_sm, 6, 7)])
    vec(lambda e: e.memset(ones_c[:, :], 1.0), [], [(m_onc, *WHOLE)])



    for n in range(NOT):
        mm(PS[6][:, 0:OW], ones_r[0:1, 0:128], bvrow[0:1, n * OW:(n + 1) * OW],
           [(m_onr, *WHOLE), (m_bv, n * OW, (n + 1) * OW)], [(m_ps[6], 0, OW)])
        vec(lambda e, n=n: e.tensor_copy(bcast[:, n * OW:(n + 1) * OW], PS[6][:, 0:OW]),
            [(m_ps[6], 0, OW)], [(m_bc, n * OW, (n + 1) * OW)])

    sca(lambda e: e.activation(c2row[0:2, :], c2row[0:2, :], Act.Silu),
        [(m_c2, *WHOLE)], [(m_c2, *WHOLE)])
    vec(lambda e: e.tensor_copy(c2bf[0:2, :], c2row[0:2, :]),
        [(m_c2, *WHOLE)], [(m_c2b, *WHOLE)])
    for ch in range(KC):
        mm(PS[6][0:128, 0:2], c2bf[0:2, ch * 128:(ch + 1) * 128], id_bf[0:2, 0:2],
           [(m_c2b, ch * 128, (ch + 1) * 128), (m_id, 0, 2)], [(m_ps[6], 0, 2)])
        vec(lambda e, ch=ch: e.tensor_copy(csilu[:, ch * 2:ch * 2 + 2], PS[6][0:128, 0:2]),
            [(m_ps[6], 0, 2)], [(m_cs, ch * 2, ch * 2 + 2)])

    for ch in range(KC):
        mm(PS[6][0:128, 0:NE], gw_bf[0:NE, ch * 128:(ch + 1) * 128], id_bf[0:NE, 0:NE],
           [(m_gw, ch * 128, (ch + 1) * 128), (m_id, 0, NE)], [(m_ps[6], 0, NE)])
        vec(lambda e, ch=ch: e.tensor_copy(gateT[:, ch * NE:(ch + 1) * NE], PS[6][0:128, 0:NE]),
            [(m_ps[6], 0, NE)], [(m_gT, ch * NE, (ch + 1) * NE)])

    WM = 6 * D
    for kc in range(KC):
        sl = kc % 2
        dma("gpsimd", f"wm{sl}", arena_a[:, sl * WM:(sl + 1) * WM], dwmod[kc * 128:(kc + 1) * 128, :],
            [], [(m_a, sl * WM, (sl + 1) * WM)])
        for mc in range(MODC):
            mm(PS[7][:, mc * 2:mc * 2 + 2],
               arena_a[:, sl * WM + mc * 128: sl * WM + (mc + 1) * 128],
               csilu[:, kc * 2:kc * 2 + 2],
               [(m_a, sl * WM + mc * 128, sl * WM + (mc + 1) * 128), (m_cs, kc * 2, kc * 2 + 2)],
               [(m_ps[7], mc * 2, mc * 2 + 2)],
               start=(kc == 0 and mc == 0), stop=(kc == KC - 1), skip=True)
    CA0, SA0, GA0, CB0, SB0, GB0 = (NK, 0, 2 * NK, 4 * NK, 3 * NK, 5 * NK)
    vec(lambda e: e.tensor_scalar_add(mods_fm[:, 0:NK], PS[7][:, CA0:CA0 + NK], 1.0),
        [(m_ps[7], CA0, CA0 + NK)], [(m_mods, 0, NK)])
    vec(lambda e: e.tensor_copy(mods_fm[:, NK:2 * NK], PS[7][:, SA0:SA0 + NK]),
        [(m_ps[7], SA0, SA0 + NK)], [(m_mods, NK, 2 * NK)])
    vec(lambda e: e.tensor_scalar_add(mods_fm[:, 2 * NK:3 * NK], PS[7][:, CB0:CB0 + NK], 1.0),
        [(m_ps[7], CB0, CB0 + NK)], [(m_mods, 2 * NK, 3 * NK)])
    vec(lambda e: e.tensor_copy(mods_fm[:, 3 * NK:4 * NK], PS[7][:, SB0:SB0 + NK]),
        [(m_ps[7], SB0, SB0 + NK)], [(m_mods, 3 * NK, 4 * NK)])
    vec(lambda e: e.reciprocal(rout[:, NK:2 * NK], mods_fm[:, 0:NK]),
        [(m_mods, 0, NK)], [(m_rt, NK, 2 * NK)])
    vec(lambda e: e.tensor_tensor(rout[:, 0:NK], mods_fm[:, NK:2 * NK],
                                  rout[:, NK:2 * NK], Alu.mult),
        [(m_mods, NK, 2 * NK), (m_rt, NK, 2 * NK)], [(m_rt, 0, NK)])
    vec(lambda e: e.tensor_copy(sap_bf[:, 0:NK], rout[:, 0:NK]),
        [(m_rt, 0, NK)], [(m_sap, 0, NK)])
    vec(lambda e: e.reciprocal(rout[:, NK:2 * NK], mods_fm[:, 2 * NK:3 * NK]),
        [(m_mods, 2 * NK, 3 * NK)], [(m_rt, NK, 2 * NK)])
    vec(lambda e: e.tensor_tensor(rout[:, 0:NK], mods_fm[:, 3 * NK:4 * NK],
                                  rout[:, NK:2 * NK], Alu.mult),
        [(m_mods, 3 * NK, 4 * NK), (m_rt, NK, 2 * NK)], [(m_rt, 0, NK)])
    vec(lambda e: e.tensor_copy(sap_bf[:, NK:2 * NK], rout[:, 0:NK]),
        [(m_rt, 0, NK)], [(m_sap, NK, 2 * NK)])
    for i, off in enumerate((GA0, GB0)):
        vec(lambda e, off=off: e.tensor_scalar(rout[:, 0:NK], PS[7][:, off:off + NK],
                                               0.0, None, Alu.is_ge),
            [(m_ps[7], off, off + NK)], [(m_rt, 0, NK)])
        vec(lambda e: e.tensor_scalar(rout[:, 0:NK], rout[:, 0:NK], 2.0, -1.0,
                                      Alu.mult, Alu.add),
            [(m_rt, 0, NK)], [(m_rt, 0, NK)])
        vec(lambda e, off=off: e.tensor_scalar_mul(rout[:, NK:2 * NK], PS[7][:, off:off + NK], -1.0),
            [(m_ps[7], off, off + NK)], [(m_rt, NK, 2 * NK)])
        vec(lambda e, off=off: e.tensor_tensor(rout[:, NK:2 * NK], PS[7][:, off:off + NK],
                                               rout[:, NK:2 * NK], Alu.max),
            [(m_ps[7], off, off + NK), (m_rt, NK, 2 * NK)], [(m_rt, NK, 2 * NK)])
        vec(lambda e: e.tensor_scalar_max(rout[:, NK:2 * NK], rout[:, NK:2 * NK], 0.25),
            [(m_rt, NK, 2 * NK)], [(m_rt, NK, 2 * NK)])
        vec(lambda e, i=i: e.tensor_tensor(ga_fm[:, i * NK:(i + 1) * NK], rout[:, 0:NK],
                                           rout[:, NK:2 * NK], Alu.mult),
            [(m_rt, 0, 2 * NK)], [(m_gafm, i * NK, (i + 1) * NK)])
    for s in range(SEQS):
        for ch in range(KC):
            mm(PS[6][0:1, 0:128], ga_fm[:, ch * 2 + s:ch * 2 + s + 1], id_bf[:, :],
               [(m_gafm, ch * 2 + s, ch * 2 + s + 1), (m_id, *WHOLE)], [(m_ps[6], 0, 128)])
            vec(lambda e, s=s, ch=ch: e.tensor_copy(
                    rows[0:1, R_GA + s * D + ch * 128:R_GA + s * D + (ch + 1) * 128],
                    PS[6][0:1, 0:128]),
                [(m_ps[6], 0, 128)], [(m_rows, R_GA + s * D + ch * 128, R_GA + s * D + (ch + 1) * 128)])

    if dbg:
        dump_bf(dbg_mods, mods_fm, m_mods, 4 * NK)

    BNW = min(512, D)
    NBN = D // BNW

    def ln_tile(x_ap, mx, xlo, hn_ap, mhn_lo):
        for i in range(NBN):
            vec(lambda e, i=i: e.bn_stats(rout[:, i * 6:(i + 1) * 6],
                                          x_ap[:, i * BNW:(i + 1) * BNW]),
                [(mx, xlo + i * BNW, xlo + (i + 1) * BNW)], [(m_rt, i * 6, (i + 1) * 6)],
                drain=True)
        mcol = 6 * NBN
        vec(lambda e: e.bn_aggr(rout[:, mcol:mcol + 2], rout[:, 0:6 * NBN]),
            [(m_rt, 0, 6 * NBN)], [(m_rt, mcol, mcol + 2)], drain=True)
        sca(lambda e: e.activation(smalls[:, 2:3], rout[:, mcol + 1:mcol + 2], Act.Sqrt,
                                   bias=smalls[:, 6:7]),
            [(m_rt, mcol + 1, mcol + 2), (m_sm, 6, 7)], [(m_sm, 2, 3)])
        vec(lambda e: e.reciprocal(smalls[:, 3:4], smalls[:, 2:3]),
            [(m_sm, 2, 3)], [(m_sm, 3, 4)])
        vec(lambda e: e.tensor_scalar_add(smalls[:, 4:5], rout[:, mcol + 1:mcol + 2], cfg.LN_EPS),
            [(m_rt, mcol + 1, mcol + 2)], [(m_sm, 4, 5)])
        vec(lambda e: e.tensor_tensor(smalls[:, 5:6], smalls[:, 3:4], smalls[:, 3:4], Alu.mult),
            [(m_sm, 3, 4)], [(m_sm, 5, 6)])
        vec(lambda e: e.tensor_tensor(smalls[:, 5:6], smalls[:, 5:6], smalls[:, 4:5], Alu.mult),
            [(m_sm, 4, 6)], [(m_sm, 5, 6)])
        vec(lambda e: e.tensor_scalar(smalls[:, 5:6], smalls[:, 5:6], -0.5, 1.5,
                                      Alu.mult, Alu.add),
            [(m_sm, 5, 6)], [(m_sm, 5, 6)])
        vec(lambda e: e.tensor_tensor(smalls[:, 3:4], smalls[:, 3:4], smalls[:, 5:6], Alu.mult),
            [(m_sm, 3, 6)], [(m_sm, 3, 4)])
        vec(lambda e: e.tensor_scalar(hn_ap, x_ap, rout[:, mcol:mcol + 1], smalls[:, 3:4],
                                      Alu.subtract, Alu.mult),
            [(mx, xlo, xlo + D), (m_rt, mcol, mcol + 1), (m_sm, 3, 4)],
            [(m_hn, mhn_lo, mhn_lo + D)])

    stopped = [False]
    do_gb = True
    pending_final = None
    for s in range(SEQS):
        if stop_after == "setup":
            stopped[0] = True
            break
        for half, rbase in ((0, R_SAP), (1, R_SBP)):
            for ch in range(KC):
                col = half * NK + ch * 2 + s
                mm(PS[6][0:1, 0:128], sap_bf[:, col:col + 1], id_bf[:, :],
                   [(m_sap, col, col + 1), (m_id, *WHOLE)], [(m_ps[6], 0, 128)])
                vec(lambda e, rbase=rbase, ch=ch: e.tensor_copy(
                        rows[0:1, rbase + ch * 128:rbase + (ch + 1) * 128], PS[6][0:1, 0:128]),
                    [(m_ps[6], 0, 128)], [(m_rows, rbase + ch * 128, rbase + (ch + 1) * 128)])
        for t in range(T128):
            if pending_final is not None:
                pending_final(t)
            sl = t % 2
            xg = arena_g[:, sl * D:(sl + 1) * D]
            dma("sync", f"x{sl}", xg, dx[s, t * 128:(t + 1) * 128, :], [],
                [(m_g, sl * D, (sl + 1) * D)])
            hn = hn_ring[:, sl * D:(sl + 1) * D]
            ln_tile(xg, m_g, sl * D, hn, sl * D)
            for ch in range(KC):
                pb = 2 + (ch % 2)
                mm(PS[pb][:, 0:128], hn[:, ch * 128:(ch + 1) * 128], id_bf[:, :],
                   [(m_hn, sl * D + ch * 128, sl * D + (ch + 1) * 128), (m_id, *WHOLE)],
                   [(m_ps[pb], 0, 128)], start=True, stop=False)
                mm(PS[pb][:, 0:128], rows[0:1, R_SAP + ch * 128:R_SAP + (ch + 1) * 128],
                   ones_r[0:1, 0:128],
                   [(m_rows, R_SAP + ch * 128, R_SAP + (ch + 1) * 128), (m_onr, *WHOLE)],
                   [(m_ps[pb], 0, 128)], start=False, stop=True)
                sca(lambda e, ch=ch, t=t, pb=pb, s=s: e.activation(
                        arena_b[:, ch * S + t * 128:ch * S + (t + 1) * 128],
                        PS[pb][:, 0:128], Act.Copy,
                        scale=mods_fm[:, ch * 2 + s:ch * 2 + s + 1]),
                    [(m_ps[pb], 0, 128), (m_mods, 0, 2 * NK)],
                    [(m_b, ch * S + t * 128, ch * S + (t + 1) * 128)])

        if dbg and s == 0:
            dump_bf(dbg_h, arena_b, m_b, KC * S)
        if stop_after == "ln1":
            stopped[0] = True
            break
        for tb in range(T128):
            vec(lambda e, tb=tb: e.memset(arena_d[:, D_VA + tb * 65 + 64:D_VA + tb * 65 + 65], 1.0),
                [], [(m_d, D_VA + tb * 65 + 64, D_VA + tb * 65 + 65)])
            vec(lambda e, tb=tb: e.memset(arena_d[:, D_VB + tb * 65 + 64:D_VB + tb * 65 + 65], 1.0),
                [], [(m_d, D_VB + tb * 65 + 64, D_VB + tb * 65 + 65)])
        WIN = KC * 384
        for pr in range(NPAIR):
            wsl = pr % 2
            win_lo = wsl * WIN
            dma("gpsimd", f"wi{wsl}",
                arena_a[:, win_lo:win_lo + WIN].rearrange("p (a c) -> p a c", a=KC),
                dwin[:, pr, :].rearrange("(a p) c -> p a c", p=128),
                [], [(m_a, win_lo, win_lo + WIN)])
            win = arena_a[:, win_lo:win_lo + WIN]

            for part in range(2):
                off = part * 128
                dst0 = part * S
                for nt in range(T512):
                    pb = part * 2 + nt % 2
                    for kc in range(KC):
                        mm(PS[pb][:, 0:512],
                           win[:, kc * 384 + off:kc * 384 + off + 128],
                           arena_b[:, kc * S + nt * 512:kc * S + (nt + 1) * 512],
                           [(m_a, win_lo + kc * 384 + off, win_lo + kc * 384 + off + 128),
                            (m_b, kc * S + nt * 512, kc * S + (nt + 1) * 512)],
                           [(m_ps[pb], 0, 512)],
                           start=(kc == 0), stop=False)
                    bcol = part * D + pr * 128
                    mm(PS[pb][:, 0:512], b_in_row[0:1, bcol:bcol + 128], ones_r[0:1, 0:512],
                       [(m_bir, bcol, bcol + 128), (m_onr, *WHOLE)],
                       [(m_ps[pb], 0, 512)], start=False, stop=True)
                    if part == 0:
                        vec(lambda e, nt=nt, pb=pb, dst0=dst0: e.tensor_scalar_mul(
                                arena_e[:, dst0 + nt * 512:dst0 + (nt + 1) * 512],
                                PS[pb][:, 0:512], 0.125),
                            [(m_ps[pb], 0, 512)],
                            [(m_e, dst0 + nt * 512, dst0 + (nt + 1) * 512)])
                    else:
                        vec(lambda e, nt=nt, pb=pb, dst0=dst0: e.tensor_copy(
                                arena_e[:, dst0 + nt * 512:dst0 + (nt + 1) * 512],
                                PS[pb][:, 0:512]),
                            [(m_ps[pb], 0, 512)],
                            [(m_e, dst0 + nt * 512, dst0 + (nt + 1) * 512)])
            for tb in range(T128):
                pb = 2 + (tb % 2)
                for kc in range(KC):
                    mm(PS[pb][:, 0:128],
                       arena_b[:, kc * S + tb * 128:kc * S + (tb + 1) * 128],
                       win[:, kc * 384 + 256:kc * 384 + 384],
                       [(m_b, kc * S + tb * 128, kc * S + (tb + 1) * 128),
                        (m_a, win_lo + kc * 384 + 256, win_lo + kc * 384 + 384)],
                       [(m_ps[pb], 0, 128)],
                       start=(kc == 0), stop=(kc == KC - 1))
                vec(lambda e, tb=tb, pb=pb, pr=pr: e.tensor_tensor(
                        arena_d[:, D_VA + tb * 65:D_VA + tb * 65 + 64],
                        PS[pb][:, 0:64], bcast[:, pr * 128:pr * 128 + 64], Alu.add),
                    [(m_ps[pb], 0, 64), (m_bc, pr * 128, pr * 128 + 64)],
                    [(m_d, D_VA + tb * 65, D_VA + tb * 65 + 64)])
                vec(lambda e, tb=tb, pb=pb, pr=pr: e.tensor_tensor(
                        arena_d[:, D_VB + tb * 65:D_VB + tb * 65 + 64],
                        PS[pb][:, 64:128], bcast[:, pr * 128 + 64:pr * 128 + 128], Alu.add),
                    [(m_ps[pb], 64, 128), (m_bc, pr * 128 + 64, pr * 128 + 128)],
                    [(m_d, D_VB + tb * 65, D_VB + tb * 65 + 64)])

            for side in range(2):
                src0 = side * S
                for nt in range(T512):
                    qsl = nt % 2
                    sca(lambda e, nt=nt, qsl=qsl, src0=src0: e.activation(
                            arena_d[:, D_SQ + qsl * 512:D_SQ + (qsl + 1) * 512],
                            arena_e[:, src0 + nt * 512:src0 + (nt + 1) * 512], Act.Square),
                        [(m_e, src0 + nt * 512, src0 + (nt + 1) * 512)],
                        [(m_d, D_SQ + qsl * 512, D_SQ + (qsl + 1) * 512)])
                    for hh in range(2):
                        pb = 4 + hh
                        mm(PS[pb][0:1, 0:512],
                           ones_c[hh * 64:(hh + 1) * 64, 0:1],
                           arena_d[hh * 64:(hh + 1) * 64, D_SQ + qsl * 512:D_SQ + (qsl + 1) * 512],
                           [(m_onc, *WHOLE), (m_d, D_SQ + qsl * 512, D_SQ + (qsl + 1) * 512)],
                           [(m_ps[pb], 0, 512)],
                           start=(nt == 0), stop=(nt == T512 - 1), skip=True)
                for hh in range(2):
                    pb = 4 + hh
                    vec(lambda e, side=side, hh=hh, pb=pb: e.reduce_max(
                            brow[0:1, side * 2 + hh:side * 2 + hh + 1], PS[pb][0:1, 0:512], Ax.X),
                        [(m_ps[pb], 0, 512)], [(m_br, side * 2 + hh, side * 2 + hh + 1)],
                        drain=True)
            for hh in range(2):
                vec(lambda e, hh=hh: e.tensor_tensor(
                        brow[0:1, 4 + hh:5 + hh], brow[0:1, hh:hh + 1],
                        brow[0:1, 2 + hh:3 + hh], Alu.add),
                    [(m_br, 0, 4)], [(m_br, 4 + hh, 5 + hh)])
                vec(lambda e, hh=hh: e.tensor_scalar(
                        brow[0:1, 4 + hh:5 + hh], brow[0:1, 4 + hh:5 + hh],
                        -1.0 / 16.0, cfg.SHIFT_CLAMP, Alu.mult, Alu.max),
                    [(m_br, 4 + hh, 5 + hh)], [(m_br, 4 + hh, 5 + hh)])
            vec(lambda e: e.tensor_copy(brow_bf[0:1, 0:2], brow[0:1, 4:6]),
                [(m_br, 4, 6)], [(m_brb, 0, 2)])

            for hh in range(2):
                qrow = arena_e[hh * 64:(hh + 1) * 64, 0:S]
                krow = arena_e[hh * 64:(hh + 1) * 64, S:2 * S]
                voff = D_VA if hh == 0 else D_VB
                mm(PS[6][:, 0:1], ones_r[0:1, 0:128], brow_bf[0:1, hh:hh + 1],
                   [(m_onr, *WHOLE), (m_brb, hh, hh + 1)], [(m_ps[6], 0, 1)])
                vec(lambda e: e.tensor_copy(smalls[:, 0:1], PS[6][:, 0:1]),
                    [(m_ps[6], 0, 1)], [(m_sm, 0, 1)])

                STB = (0, 1, 6)

                for qt in range(T512):
                    nk = 4 * qt + 4

                    def emit_st(kblk):
                        sl = STB[kblk % 3]
                        ptslot = kblk % 3
                        off = max(0, kblk * 128 - qt * 512)
                        mm(PS[sl][:, off:512],
                           krow[:, kblk * 128:(kblk + 1) * 128],
                           qrow[:, qt * 512 + off:(qt + 1) * 512],
                           [(m_e, S + kblk * 128, S + (kblk + 1) * 128),
                            (m_e, qt * 512 + off, (qt + 1) * 512)],
                           [(m_ps[sl], 0, 512)])
                        if kblk >= 4 * qt:
                            vec(lambda e, sl=sl, off=off: e.tensor_tensor(
                                    PS[sl][:, off:off + 128], PS[sl][:, off:off + 128],
                                    mask_t[:, :], Alu.add),
                                [(m_mask, *WHOLE), (m_ps[sl], off, off + 128)],
                                [(m_ps[sl], off, off + 128)])
                        sca(lambda e, sl=sl, ptslot=ptslot, off=off: e.activation(
                                arena_d[:, D_PT + ptslot * 512 + off:D_PT + (ptslot + 1) * 512],
                                PS[sl][:, off:512], Act.Exp, bias=smalls[:, 0:1]),
                            [(m_ps[sl], off, 512), (m_sm, 0, 1)],
                            [(m_d, D_PT + ptslot * 512 + off, D_PT + (ptslot + 1) * 512)])

                    def emit_pv(kblk):
                        ptslot = kblk % 3
                        for qb in range(4):
                            if 4 * qt + qb < kblk:
                                continue
                            pb = 2 + qb
                            mm(PS[pb][0:128, 0:65],
                               arena_d[:, D_PT + ptslot * 512 + qb * 128:D_PT + ptslot * 512 + (qb + 1) * 128],
                               arena_d[:, voff + kblk * 65:voff + (kblk + 1) * 65],
                               [(m_d, D_PT + ptslot * 512 + qb * 128, D_PT + ptslot * 512 + (qb + 1) * 128),
                                (m_d, voff + kblk * 65, voff + (kblk + 1) * 65)],
                               [(m_ps[pb], 0, 65)],
                               start=(kblk == 0), stop=(kblk == 4 * qt + qb), skip=True)

                    emit_st(0)
                    if nk > 1:
                        emit_st(1)
                    for kblk in range(2, nk):
                        emit_st(kblk)
                        emit_pv(kblk - 2)
                    emit_pv(nk - 2)
                    emit_pv(nk - 1)
                    for qb in range(4):
                        pb = 2 + qb
                        vec(lambda e, pb=pb: e.reciprocal(smalls[:, 1:2], PS[pb][:, 64:65]),
                            [(m_ps[pb], 64, 65)], [(m_sm, 1, 2)])
                        vec(lambda e, pb=pb, qt=qt, qb=qb, hh=hh: e.tensor_scalar_mul(
                                arena_d[:, D_AT + (qt * 4 + qb) * 128 + hh * 64:
                                        D_AT + (qt * 4 + qb) * 128 + hh * 64 + 64],
                                PS[pb][:, 0:64], smalls[:, 1:2]),
                            [(m_ps[pb], 0, 64), (m_sm, 1, 2)],
                            [(m_d, D_AT + (qt * 4 + qb) * 128 + hh * 64,
                              D_AT + (qt * 4 + qb) * 128 + hh * 64 + 64)])
            if dbg and s == 0 and pr == 0:
                dma("gpsimd", "dbgk", dbg_bir[0:1, 0:2 * D], b_in_row[0:1, :],
                    [(m_bir, 0, 2 * D)], [(m_dbg, 0, 1)])
                dma("gpsimd", "dbgk", dbg_bir[0:1, 2 * D:2 * D + 512], ones_r[0:1, :],
                    [(m_onr, 0, 512)], [(m_dbg, 0, 1)])
                dma("gpsimd", "dbgk", dbg_bir[0:1, 2 * D + 512:2 * D + 512 + D], bvrow[0:1, :],
                    [(m_bv, 0, D)], [(m_dbg, 0, 1)])
                dump_bf(dbg_qk, arena_e, m_e, 2 * S)
                dump_bf(dbg_v, arena_d, m_d, 4224)
                for col in range(0, T128 * 128, 2048):
                    dma("gpsimd", "dbgk", dbg_at[:, col:col + 2048],
                        arena_d[:, D_AT + col:D_AT + col + 2048],
                        [(m_d, D_AT + col, D_AT + col + 2048)], [(m_dbg, 0, 1)])
                dma("gpsimd", "dbgk", dbg_nc[:, 0:4], smalls[:, 0:4],
                    [(m_sm, 0, 4)], [(m_dbg, 0, 1)])
            for qb in range(T128):
                pb = 6 + (qb % 2)
                mm(PS[pb][:, 0:128], arena_d[:, D_AT + qb * 128:D_AT + (qb + 1) * 128],
                   id_bf[:, :],
                   [(m_d, D_AT + qb * 128, D_AT + (qb + 1) * 128), (m_id, *WHOLE)],
                   [(m_ps[pb], 0, 128)])
                vec(lambda e, qb=qb, pb=pb, pr=pr: e.tensor_copy(
                        arena_c[:, pr * S + qb * 128:pr * S + (qb + 1) * 128], PS[pb][:, 0:128]),
                    [(m_ps[pb], 0, 128)],
                    [(m_c, pr * S + qb * 128, pr * S + (qb + 1) * 128)])

        if dbg and s == 0:
            dump_bf(dbg_attn, arena_c, m_c, KC * S)
        if stop_after == "attn":
            stopped[0] = True
            break
        dma("gpsimd", "wo", arena_a[:, 0:KC * D].rearrange("p (a c) -> p a c", a=KC),
            dwout.rearrange("(a p) c -> p a c", p=128),
            [], [(m_a, 0, KC * D)])
        vec(lambda e, s=s: e.tensor_tensor(rows[0:1, R_GAB:R_GAB + D],
                                           rows[0:1, R_GA + s * D:R_GA + (s + 1) * D],
                                           borow[0:1, :], Alu.mult),
            [(m_rows, R_GA + s * D, R_GA + (s + 1) * D), (m_bo, *WHOLE)],
            [(m_rows, R_GAB, R_GAB + D)])
        for n in range(NOT):
            mm(PS[6][:, 0:OW], ones_r[0:1, 0:128],
               rows[0:1, R_GA + s * D + n * OW:R_GA + s * D + (n + 1) * OW],
               [(m_onr, *WHOLE), (m_rows, R_GA + s * D + n * OW, R_GA + s * D + (n + 1) * OW)],
               [(m_ps[6], 0, OW)])
            vec(lambda e, n=n: e.tensor_copy(bcast[:, D + n * OW:D + (n + 1) * OW],
                                             PS[6][:, 0:OW]),
                [(m_ps[6], 0, OW)], [(m_bc, D + n * OW, D + (n + 1) * OW)])
        for kc in range(KC):
            vec(lambda e, kc=kc: e.tensor_tensor(
                    arena_d[:, kc * D:(kc + 1) * D],
                    arena_a[:, kc * D:(kc + 1) * D],
                    bcast[:, D:2 * D], Alu.mult),
                [(m_a, kc * D, (kc + 1) * D), (m_bc, D, 2 * D)],
                [(m_d, kc * D, (kc + 1) * D)])

        def emit_h2_transposes(t, s=s):
            sl = t % 2
            hn = hn_ring[:, sl * D:(sl + 1) * D]
            for ch in range(KC):
                pb = 2 + (ch % 2)
                mm(PS[pb][:, 0:128], hn[:, ch * 128:(ch + 1) * 128], id_bf[:, :],
                   [(m_hn, sl * D + ch * 128, sl * D + (ch + 1) * 128), (m_id, *WHOLE)],
                   [(m_ps[pb], 0, 128)], start=True, stop=False)
                mm(PS[pb][:, 0:128], rows[0:1, R_SBP + ch * 128:R_SBP + (ch + 1) * 128],
                   ones_r[0:1, 0:128],
                   [(m_rows, R_SBP + ch * 128, R_SBP + (ch + 1) * 128), (m_onr, *WHOLE)],
                   [(m_ps[pb], 0, 128)], start=False, stop=True)
                sca(lambda e, ch=ch, t=t, pb=pb, s=s: e.activation(
                        arena_b[:, ch * S + t * 128:ch * S + (t + 1) * 128],
                        PS[pb][:, 0:128], Act.Copy,
                        scale=mods_fm[:, 2 * NK + ch * 2 + s:2 * NK + ch * 2 + s + 1]),
                    [(m_ps[pb], 0, 128), (m_mods, 2 * NK, 4 * NK)],
                    [(m_b, ch * S + t * 128, ch * S + (t + 1) * 128)])

        for t in range(T128):
            sl = t % 2
            xg = arena_g[:, sl * D:(sl + 1) * D]
            dma("sync", f"x{sl}", xg, dx[s, t * 128:(t + 1) * 128, :], [],
                [(m_g, sl * D, (sl + 1) * D)])
            for n in range(NOT):
                pb = n % 2
                for kc in range(KC):
                    mm(PS[pb][:, 0:OW],
                       arena_c[:, kc * S + t * 128:kc * S + (t + 1) * 128],
                       arena_d[:, kc * D + n * OW:kc * D + (n + 1) * OW],
                       [(m_c, kc * S + t * 128, kc * S + (t + 1) * 128),
                        (m_d, kc * D + n * OW, kc * D + (n + 1) * OW)],
                       [(m_ps[pb], 0, OW)],
                       start=(kc == 0), stop=False)
                mm(PS[pb][:, 0:OW], ones_r[:, 0:128],
                   rows[0:1, R_GAB + n * OW:R_GAB + (n + 1) * OW],
                   [(m_onr, *WHOLE), (m_rows, R_GAB + n * OW, R_GAB + (n + 1) * OW)],
                   [(m_ps[pb], 0, OW)], start=False, stop=True)
            xb = arena_g[:, (2 + sl) * D:(3 + sl) * D]
            for n in range(NOT):
                vec(lambda e, n=n, xb=xb, xg=xg: e.tensor_tensor(
                        xb[:, n * OW:(n + 1) * OW], PS[n % 2][:, 0:OW],
                        xg[:, n * OW:(n + 1) * OW], Alu.add),
                    [(m_ps[n % 2], 0, OW), (m_g, sl * D + n * OW, sl * D + (n + 1) * OW)],
                    [(m_g, (2 + sl) * D + n * OW, (2 + sl) * D + (n + 1) * OW)])
            dma("sync", f"sp{sl}", dxb[s, t * 128:(t + 1) * 128, :], xb,
                [(m_g, (2 + sl) * D, (3 + sl) * D)],
                [(m_xbs, s * S + t * 128, s * S + (t + 1) * 128)])
            hn = hn_ring[:, sl * D:(sl + 1) * D]
            ln_tile(xb, m_g, (2 + sl) * D, hn, sl * D)
            if t > 0:
                emit_h2_transposes(t - 1)
            if t == T128 - 1:
                emit_h2_transposes(t)

        if stop_after == "proj":
            stopped[0] = True
            break
        def _emit_routing_all():
         for t in range(T128):
            for kc in range(KC):
                mm(PS[4][:, 0:NE],
                   arena_b[:, kc * S + t * 128:kc * S + (t + 1) * 128],
                   gateT[:, kc * NE:(kc + 1) * NE],
                   [(m_b, kc * S + t * 128, kc * S + (t + 1) * 128), (m_gT, kc * NE, (kc + 1) * NE)],
                   [(m_ps[4], 0, NE)],
                   start=(kc == 0), stop=(kc == KC - 1))
            routing_tile(p, cfg, PS[4], m_ps[4], rout, m_rt, sc_bf, m_scb, vec, sca)
            mm(PS[5][0:NE, 0:128], sc_bf[:, 0:NE], id_bf[:, :],
               [(m_scb, 0, NE), (m_id, *WHOLE)], [(m_ps[5], 0, 128)])
            vec(lambda e, t=t: e.tensor_copy(c_fm[0:NE, t * 128:(t + 1) * 128],
                                             PS[5][0:NE, 0:128]),
                [(m_ps[5], 0, 128)], [(m_cfm, t * 128, (t + 1) * 128)])

        if dbg and s == 0:
            dump_bf(dbg_cfm, c_fm, m_cfm, S)
        if stop_after == "gate":
            stopped[0] = True
            break
        do_gb = stop_after != "moe_nogb"
        for ch in (range(KC) if do_gb else ()):
            mm(PS[6][0:1, 0:128], ga_fm[:, NK + ch * 2 + s:NK + ch * 2 + s + 1], id_bf[:, :],
               [(m_gafm, NK + ch * 2 + s, NK + ch * 2 + s + 1), (m_id, *WHOLE)],
               [(m_ps[6], 0, 128)])
            vec(lambda e, ch=ch: e.tensor_copy(rows[0:1, R_GB + ch * 128:R_GB + (ch + 1) * 128],
                                               PS[6][0:1, 0:128]),
                [(m_ps[6], 0, 128)], [(m_rows, R_GB + ch * 128, R_GB + (ch + 1) * 128)])
        for n in (range(NOT) if do_gb else ()):
            mm(PS[6][:, 0:OW], ones_r[0:1, 0:128], rows[0:1, R_GB + n * OW:R_GB + (n + 1) * OW],
               [(m_onr, *WHOLE), (m_rows, R_GB + n * OW, R_GB + (n + 1) * OW)],
               [(m_ps[6], 0, OW)])
            vec(lambda e, n=n: e.tensor_copy(bcast[:, D + n * OW:D + (n + 1) * OW],
                                             PS[6][:, 0:OW]),
                [(m_ps[6], 0, OW)], [(m_bc, D + n * OW, D + (n + 1) * OW)])

        SW = INTER
        ICL = SW // 128
        SLAB = KC * SW + KC * SW + ICL * D
        W1O, W3O, W2O = 0, KC * SW, 2 * KC * SW
        slabs = [("sh", 0), ("sh", 1)] + [("e", e) for e in range(NE)]
        if stop_after in ("moe_dma", "moe_g0t0", "moe_g0", "moe_nogb"):
            slabs = slabs[:1]
        if stop_after is not None:
            _emit_routing_all()
        for si, (kind, idx) in enumerate(slabs):
            sl = si % 2
            base = sl * SLAB
            key = f"sl{sl}"
            if kind == "e":
                srcs = (dw1[idx], dw3[idx], dw2[idx])
            else:
                h0 = idx * SW
                srcs = (dws1[:, h0:h0 + SW], dws3[:, h0:h0 + SW], dws2[h0:h0 + SW, :])
            for (wo, wa_dim, wid, src) in ((W1O, KC, KC * SW, srcs[0]),
                                           (W3O, KC, KC * SW, srcs[1]),
                                           (W2O, ICL, ICL * D, srcs[2])):
                dma("gpsimd", key,
                    arena_a[:, base + wo:base + wo + wid].rearrange(
                        "p (a c) -> p a c", a=wa_dim),
                    src.rearrange("(a p) c -> p a c", p=128),
                    [], [(m_a, base + wo, base + wo + wid)])
            for icl in range(ICL):
                vec(lambda e, base=base, icl=icl: e.tensor_tensor(
                        arena_a[:, base + W2O + icl * D:base + W2O + (icl + 1) * D],
                        arena_a[:, base + W2O + icl * D:base + W2O + (icl + 1) * D],
                        bcast[:, D:2 * D], Alu.mult),
                    [(m_a, base + W2O + icl * D, base + W2O + (icl + 1) * D), (m_bc, D, 2 * D)],
                    [(m_a, base + W2O + icl * D, base + W2O + (icl + 1) * D)])
            if si == 1 and stop_after is None:
                _emit_routing_all()
            for t in range(T512):
                asl = t % 2
                A0 = asl * ICL * 512
                if kind == "e":
                    mm(PS[6][:, 0:512],
                       sel16[:, idx * 128:(idx + 1) * 128],
                       c_fm[0:NE, t * 512:(t + 1) * 512],
                       [(m_sel, idx * 128, (idx + 1) * 128), (m_cfm, t * 512, (t + 1) * 512)],
                       [(m_ps[6], 0, 512)])
                    vec(lambda e, asl=asl: e.tensor_copy(
                            cb_ring[:, asl * 512:(asl + 1) * 512], PS[6][:, 0:512]),
                        [(m_ps[6], 0, 512)], [(m_cb, asl * 512, (asl + 1) * 512)])
                for icl in range(ICL):
                    p1b, p3b = icl % 2, 2 + icl % 2
                    for kc in range(KC):
                        mm(PS[p1b][:, 0:512],
                           arena_a[:, base + W1O + kc * SW + icl * 128:base + W1O + kc * SW + (icl + 1) * 128],
                           arena_b[:, kc * S + t * 512:kc * S + (t + 1) * 512],
                           [(m_a, base + W1O + kc * SW + icl * 128, base + W1O + kc * SW + (icl + 1) * 128),
                            (m_b, kc * S + t * 512, kc * S + (t + 1) * 512)],
                           [(m_ps[p1b], 0, 512)],
                           start=(kc == 0), stop=(kc == KC - 1))
                    for kc in range(KC):
                        mm(PS[p3b][:, 0:512],
                           arena_a[:, base + W3O + kc * SW + icl * 128:base + W3O + kc * SW + (icl + 1) * 128],
                           arena_b[:, kc * S + t * 512:kc * S + (t + 1) * 512],
                           [(m_a, base + W3O + kc * SW + icl * 128, base + W3O + kc * SW + (icl + 1) * 128),
                            (m_b, kc * S + t * 512, kc * S + (t + 1) * 512)],
                           [(m_ps[p3b], 0, 512)],
                           start=(kc == 0), stop=(kc == KC - 1))
                    tsl = icl % 2
                    t1 = moe_tmp[:, tsl * 512:(tsl + 1) * 512]
                    t3 = moe_tmp[:, 1024 + tsl * 512:1024 + (tsl + 1) * 512]
                    if kind == "sh":
                        vec(lambda e, t3=t3, p3b=p3b: e.tensor_copy(t3, PS[p3b][:, 0:512]),
                            [(m_ps[p3b], 0, 512)], [(m_mt, 1024 + tsl * 512, 1024 + (tsl + 1) * 512)])
                    else:
                        vec(lambda e, t3=t3, p3b=p3b, asl=asl: e.tensor_tensor(
                                t3, PS[p3b][:, 0:512],
                                cb_ring[:, asl * 512:(asl + 1) * 512], Alu.mult),
                            [(m_ps[p3b], 0, 512), (m_cb, asl * 512, (asl + 1) * 512)],
                            [(m_mt, 1024 + tsl * 512, 1024 + (tsl + 1) * 512)])
                    sca(lambda e, t1=t1, p1b=p1b: e.activation(t1, PS[p1b][:, 0:512], Act.Silu),
                        [(m_ps[p1b], 0, 512)], [(m_mt, tsl * 512, (tsl + 1) * 512)])
                    vec(lambda e, t1=t1, t3=t3, icl=icl, A0=A0: e.tensor_tensor(
                            arena_d[:, A0 + icl * 512:A0 + (icl + 1) * 512], t1, t3, Alu.mult),
                        [(m_mt, tsl * 512, (tsl + 1) * 512),
                         (m_mt, 1024 + tsl * 512, 1024 + (tsl + 1) * 512)],
                        [(m_d, A0 + icl * 512, A0 + (icl + 1) * 512)])
                for mtile in range(4):
                    for n in range(NOT):
                        pb = 4 + ((mtile * NOT + n) % 2)
                        for icl in range(ICL):
                            mm(PS[pb][:, 0:OW],
                               arena_d[:, A0 + icl * 512 + mtile * 128:A0 + icl * 512 + (mtile + 1) * 128],
                               arena_a[:, base + W2O + icl * D + n * OW:base + W2O + icl * D + (n + 1) * OW],
                               [(m_d, A0 + icl * 512 + mtile * 128, A0 + icl * 512 + (mtile + 1) * 128),
                                (m_a, base + W2O + icl * D + n * OW, base + W2O + icl * D + (n + 1) * OW)],
                               [(m_ps[pb], 0, OW)],
                               start=(icl == 0), stop=(icl == ICL - 1))
                        ycol = (t * 4 + mtile) * D + n * OW
                        if si == 0:
                            vec(lambda e, pb=pb, ycol=ycol: e.tensor_copy(
                                    arena_c[:, ycol:ycol + OW], PS[pb][:, 0:OW]),
                                [(m_ps[pb], 0, OW)], [(m_c, ycol, ycol + OW)])
                        else:
                            vec(lambda e, pb=pb, ycol=ycol: e.tensor_tensor(
                                    arena_c[:, ycol:ycol + OW], PS[pb][:, 0:OW],
                                    arena_c[:, ycol:ycol + OW], Alu.add),
                                [(m_ps[pb], 0, OW), (m_c, ycol, ycol + OW)],
                                [(m_c, ycol, ycol + OW)])

        def emit_final_tile(t, s=s):
            rsl = 2 + t % 2
            osl = 2 + (t + 1) % 2
            xg = arena_g[:, rsl * D:(rsl + 1) * D]
            dma("sync", f"fx{t % 2}", xg, dxb[s, t * 128:(t + 1) * 128, :],
                [(m_xbs, s * S + t * 128, s * S + (t + 1) * 128)],
                [(m_g, rsl * D, (rsl + 1) * D)])
            vec(lambda e, t=t, xg=xg, osl=osl: e.tensor_tensor(
                    arena_g[:, osl * D:(osl + 1) * D], xg,
                    arena_c[:, t * D:(t + 1) * D], Alu.add),
                [(m_g, rsl * D, (rsl + 1) * D), (m_c, t * D, (t + 1) * D)],
                [(m_g, osl * D, (osl + 1) * D)])
            dma("sync", "st", dout[s, t * 128:(t + 1) * 128, :],
                arena_g[:, osl * D:(osl + 1) * D],
                [(m_g, osl * D, (osl + 1) * D)],
                [(m_out, s * S + t * 128, s * S + (t + 1) * 128)])

        pending_final = emit_final_tile
        if s == SEQS - 1 or stop_after is not None:
            for t in range(T128):
                emit_final_tile(t)
            pending_final = None

    if not stopped[0]:
        p.do("sync", None, r=[(m_out, 0, SEQS * S)])

    plans, sem_names = p.finalize()

    sems = {}
    for sn in sem_names:
        g = nc.semaphore(f"sem_{sn}")
        sems[sn] = g.__enter__()
        ctx.append(g)

    def mk(e):
        def run(eng):
            for waits, emit, inc, inc_sem, drain in plans[e]:
                for (sn, v) in waits:
                    eng.wait_ge(sems[sn], v)
                if emit is not None:
                    inst = emit(eng)
                    if inc:
                        inst.then_inc(sems[inc_sem], inc)
                    if drain:
                        eng.drain()
        return run

    blk = nc.Block()
    block = blk.__enter__()
    block.tensor(mk("tensor"))
    block.vector(mk("vector"))
    block.scalar(mk("scalar"))
    block.sync(mk("sync"))
    block.gpsimd(mk("gpsimd"))
    blk.__exit__(None, None, None)

    for g in reversed(ctx):
        g.__exit__(None, None, None)
    return nc


def routing_tile(p, cfg, psl, m_psl, rout, m_rt, sc_bf, m_scb, vec, sca):
    NE = cfg.NE
    G0, W0, E0, S0, M0, T0 = 0, 4, 20, 36, 52, 68
    vec(lambda e: e.reduce_max(rout[:, T0:T0 + 1], psl[:, 0:NE], Ax.X, negate=True),
        [(m_psl, 0, NE)], [(m_rt, T0, T0 + 1)], drain=True)
    sca(lambda e: e.activation(rout[:, E0:E0 + NE], psl[:, 0:NE], Act.Exp,
                               bias=rout[:, T0:T0 + 1]),
        [(m_psl, 0, NE), (m_rt, T0, T0 + 1)], [(m_rt, E0, E0 + NE)])
    vec(lambda e: e.reduce_sum(rout[:, T0 + 1:T0 + 2], rout[:, E0:E0 + NE], Ax.X),
        [(m_rt, E0, E0 + NE)], [(m_rt, T0 + 1, T0 + 2)], drain=True)
    vec(lambda e: e.reciprocal(rout[:, T0 + 2:T0 + 3], rout[:, T0 + 1:T0 + 2]),
        [(m_rt, T0 + 1, T0 + 2)], [(m_rt, T0 + 2, T0 + 3)])
    vec(lambda e: e.tensor_scalar_mul(rout[:, S0:S0 + NE], rout[:, E0:E0 + NE],
                                      rout[:, T0 + 2:T0 + 3]),
        [(m_rt, E0, E0 + NE), (m_rt, T0 + 2, T0 + 3)], [(m_rt, S0, S0 + NE)])
    for r in range(4):
        vec(lambda e, r=r: e.reduce_max(rout[:, G0 + r:G0 + r + 1],
                                        rout[:, S0 + 4 * r:S0 + 4 * r + 4], Ax.X),
            [(m_rt, S0 + 4 * r, S0 + 4 * r + 4)], [(m_rt, G0 + r, G0 + r + 1)], drain=True)
    vec(lambda e: e.reduce_max(rout[:, T0 + 3:T0 + 4], rout[:, G0:G0 + 4], Ax.X),
        [(m_rt, G0, G0 + 4)], [(m_rt, T0 + 3, T0 + 4)], drain=True)
    vec(lambda e: e.tensor_scalar(rout[:, W0:W0 + 4], rout[:, G0:G0 + 4],
                                  rout[:, T0 + 3:T0 + 4], None, Alu.is_ge),
        [(m_rt, G0, G0 + 4), (m_rt, T0 + 3, T0 + 4)], [(m_rt, W0, W0 + 4)])
    vec(lambda e: e.scalar_tensor_tensor(rout[:, W0:W0 + 4], rout[:, W0:W0 + 4], KNOCK_NEG,
                                         rout[:, G0:G0 + 4], Alu.mult, Alu.add),
        [(m_rt, W0, W0 + 4), (m_rt, G0, G0 + 4)], [(m_rt, W0, W0 + 4)])
    vec(lambda e: e.reduce_max(rout[:, T0 + 4:T0 + 5], rout[:, W0:W0 + 4], Ax.X),
        [(m_rt, W0, W0 + 4)], [(m_rt, T0 + 4, T0 + 5)], drain=True)
    vec(lambda e: e.tensor_scalar(rout[:, G0:G0 + 4], rout[:, G0:G0 + 4],
                                  rout[:, T0 + 4:T0 + 5], None, Alu.is_ge),
        [(m_rt, G0, G0 + 4), (m_rt, T0 + 4, T0 + 5)], [(m_rt, G0, G0 + 4)])
    for r in range(4):
        vec(lambda e, r=r: e.tensor_scalar_mul(rout[:, M0 + 4 * r:M0 + 4 * r + 4],
                                               rout[:, S0 + 4 * r:S0 + 4 * r + 4],
                                               rout[:, G0 + r:G0 + r + 1]),
            [(m_rt, S0 + 4 * r, S0 + 4 * r + 4), (m_rt, G0 + r, G0 + r + 1)],
            [(m_rt, M0 + 4 * r, M0 + 4 * r + 4)])
    vec(lambda e: e.tensor_copy(rout[:, W0:W0 + NE], rout[:, M0:M0 + NE]),
        [(m_rt, M0, M0 + NE)], [(m_rt, W0, W0 + NE)])
    for k in range(3):
        vec(lambda e: e.reduce_max(rout[:, T0 + 5:T0 + 6], rout[:, W0:W0 + NE], Ax.X),
            [(m_rt, W0, W0 + NE)], [(m_rt, T0 + 5, T0 + 6)], drain=True)
        vec(lambda e: e.tensor_scalar(rout[:, E0:E0 + NE], rout[:, W0:W0 + NE],
                                      rout[:, T0 + 5:T0 + 6], None, Alu.is_ge),
            [(m_rt, W0, W0 + NE), (m_rt, T0 + 5, T0 + 6)], [(m_rt, E0, E0 + NE)])
        vec(lambda e: e.scalar_tensor_tensor(rout[:, W0:W0 + NE], rout[:, E0:E0 + NE], KNOCK_NEG,
                                             rout[:, W0:W0 + NE], Alu.mult, Alu.add),
            [(m_rt, E0, E0 + NE), (m_rt, W0, W0 + NE)], [(m_rt, W0, W0 + NE)])
    vec(lambda e: e.reduce_max(rout[:, T0 + 5:T0 + 6], rout[:, W0:W0 + NE], Ax.X),
        [(m_rt, W0, W0 + NE)], [(m_rt, T0 + 5, T0 + 6)], drain=True)
    vec(lambda e: e.tensor_scalar(rout[:, E0:E0 + NE], rout[:, M0:M0 + NE],
                                  rout[:, T0 + 5:T0 + 6], None, Alu.is_ge),
        [(m_rt, M0, M0 + NE), (m_rt, T0 + 5, T0 + 6)], [(m_rt, E0, E0 + NE)])
    vec(lambda e: e.tensor_tensor(sc_bf[:, 0:NE], rout[:, S0:S0 + NE],
                                  rout[:, E0:E0 + NE], Alu.mult),
        [(m_rt, S0, S0 + NE), (m_rt, E0, E0 + NE)], [(m_scb, 0, NE)])



def mk_consts(cfg):
    NE = cfg.NE
    id128 = np.eye(128, dtype=ml_dtypes.bfloat16)
    i = np.arange(128)
    mask = np.where(i[None, :] >= i[:, None], 0.0, MASK_NEG).astype(np.float32)
    sel = np.zeros((NE, NE * 128), dtype=ml_dtypes.bfloat16)
    for e in range(NE):
        sel[e, e * 128:(e + 1) * 128] = 1.0
    return id128, mask, sel


def make_inputs(cfg, **a):
    D = cfg.D
    x = np.ascontiguousarray(np.asarray(a["x"], dtype=np.float32))
    w_in = np.asarray(a["w_in"], dtype=np.float32)
    wq = w_in[:, 0:D].reshape(D, cfg.NPAIR, 128)
    wk = w_in[:, D:2 * D].reshape(D, cfg.NPAIR, 128)
    wv = w_in[:, 2 * D:3 * D].reshape(D, cfg.NPAIR, 128)
    w_in_re = np.ascontiguousarray(np.concatenate([wq, wk, wv], axis=2))
    id128, mask, sel = mk_consts(cfg)
    shared = dict(
        w_in_re=w_in_re,
        b_in=np.ascontiguousarray(np.asarray(a["b_in"], dtype=np.float32)),
        w_out=np.ascontiguousarray(np.asarray(a["w_out"], dtype=np.float32)),
        b_out=np.ascontiguousarray(np.asarray(a["b_out"], dtype=np.float32)),
        gate_w=np.ascontiguousarray(np.asarray(a["gate_w"], dtype=np.float32)),
        w1=np.ascontiguousarray(np.asarray(a["w1"], dtype=np.float32)),
        w2=np.ascontiguousarray(np.asarray(a["w2"], dtype=np.float32)),
        w3=np.ascontiguousarray(np.asarray(a["w3"], dtype=np.float32)),
        ws1=np.ascontiguousarray(np.asarray(a["ws1"], dtype=np.float32)),
        ws2=np.ascontiguousarray(np.asarray(a["ws2"], dtype=np.float32)),
        ws3=np.ascontiguousarray(np.asarray(a["ws3"], dtype=np.float32)),
        wmod=np.ascontiguousarray(np.asarray(a["wmod"], dtype=np.float32)),
        id128d=id128, mask_td=mask, sel16d=sel,
    )
    cvec = np.asarray(a["cvec"], dtype=np.float32)
    return x, cvec, shared


def kernel(L0_attn_x, L0_attn_w_in, L0_attn_b_in, L0_attn_w_out, L0_attn_b_out,
           L0_ffn_gate_w, L0_ffn_w1, L0_ffn_w2, L0_ffn_w3, L0_ffn_ws1, L0_ffn_ws2,
           L0_ffn_ws3, L0_c, L0_wmod):
    cfg = CFG()
    x, cvec, shared = make_inputs(
        cfg, x=L0_attn_x, w_in=L0_attn_w_in, b_in=L0_attn_b_in, w_out=L0_attn_w_out,
        b_out=L0_attn_b_out, gate_w=L0_ffn_gate_w, w1=L0_ffn_w1, w2=L0_ffn_w2,
        w3=L0_ffn_w3, ws1=L0_ffn_ws1, ws2=L0_ffn_ws2, ws3=L0_ffn_ws3, cvec=L0_c,
        wmod=L0_wmod)
    nc = build_program(cfg, paranoid={"vector"})
    in_maps = []
    for c in range(NCORES):
        m = dict(shared)
        m["x"] = np.ascontiguousarray(x[c * cfg.SEQS:(c + 1) * cfg.SEQS])
        m["cvec"] = np.ascontiguousarray(cvec[c * cfg.SEQS:(c + 1) * cfg.SEQS])
        in_maps.append(m)
    res = run_bass_kernel_spmd(nc, in_maps, core_ids=list(range(NCORES)))
    out = np.concatenate([res.results[c]["out"] for c in range(NCORES)], axis=0)
    return np.ascontiguousarray(out.astype(np.float32))
